# version 20
# baseline (speedup 1.0000x reference)
import numpy as np
from contextlib import ExitStack

import concourse.bass as bass
import concourse.mybir as mybir
from concourse.bass_utils import run_bass_kernel_spmd

F32 = mybir.dt.float32
BF16 = mybir.dt.bfloat16
AF = mybir.ActivationFunctionType
ALU = mybir.AluOpType
AX = mybir.AxisListType

NCORES = 8
SPAN = 2048
C = 128
NT = SPAN // C
E = 1024
H = 8
DK = 64
DV = 128
I_FFN = 2816
LN16 = 1.0 / 16.0
LOG0125 = float(np.log(0.125))

ENG_ENGINES = ("sp", "act", "dve", "pe", "pool")
QUEUE_OF_ENGINE = {"sp": "qsp", "act": "qact", "pool": "qpool"}
NSEM_POOL = 16


class _Op:
    __slots__ = ("idx", "engine", "fn", "reads", "writes", "sreads", "n_dma",
                 "tag", "deps", "seq", "is_dma", "is_coll", "force", "queue",
                 "cost", "tset", "order", "prio")

    def __init__(self):
        self.deps = {}


class Sched:

    def __init__(self):
        self.ops = []
        self.last_writer = {}
        self.readers = {}
        self.enabled = True

    def add(self, engine, fn, reads=(), writes=(), sreads=(), n_dma=0,
            tag="", force=False, is_coll=False, cost=None, tset=None):
        if not self.enabled:
            return -1
        op = _Op()
        op.cost = cost if cost is not None else (1500 if n_dma else 500)
        op.tset = tset
        import builtins
        op.prio = getattr(builtins, 'op_prio_base', [0])[0] + len(self.ops)
        op.idx = len(self.ops)
        op.engine = engine
        op.fn = fn
        op.reads = tuple(reads) + tuple(sreads)
        op.sreads = tuple(sreads)
        op.writes = tuple(writes)
        op.n_dma = n_dma
        op.tag = tag
        op.is_dma = n_dma > 0
        op.is_coll = is_coll
        op.force = force
        deps = set()
        for r in op.reads:
            w = self.last_writer.get(r)
            if w is not None:
                deps.add(w)
        for w_ in op.writes:
            pw = self.last_writer.get(w_)
            if pw is not None:
                deps.add(pw)
            for rd in self.readers.get(w_, ()):
                deps.add(rd)
        deps.discard(op.idx)
        op.deps = deps
        for r in op.reads:
            self.readers.setdefault(r, []).append(op.idx)
        for w_ in op.writes:
            self.last_writer[w_] = op.idx
            self.readers[w_] = []
        self.ops.append(op)
        return op.idx

    def schedule(self):
        import heapq
        n = len(self.ops)
        succs = [[] for _ in range(n)]
        ndeps = [0] * n
        for op in self.ops:
            ndeps[op.idx] = len(op.deps)
            for d in op.deps:
                succs[d].append(op.idx)
        ready = {e: [] for e in ENG_ENGINES}
        for op in self.ops:
            if ndeps[op.idx] == 0:
                heapq.heappush(ready[op.engine], (op.prio, op.idx))
        eng_free = {e: 0.0 for e in ENG_ENGINES}
        act_set = [None]
        done_t = [0.0] * n
        events = []
        order = {e: [] for e in ENG_ENGINES}
        scheduled = 0
        while scheduled < n:
            progressed = False
            for e in ENG_ENGINES:
                if not ready[e]:
                    continue
                if e == "act" and act_set[0] is not None:
                    cand = heapq.nsmallest(6, ready[e])
                    pick = None
                    for c in cand:
                        ts = self.ops[c[1]].tset
                        if ts is None or ts == act_set[0]:
                            pick = c
                            break
                    if pick is None:
                        pick = cand[0]
                    ready[e].remove(pick)
                    heapq.heapify(ready[e])
                    pick = pick[1]
                else:
                    pick = heapq.heappop(ready[e])[1]
                op = self.ops[pick]
                start = max(eng_free[e],
                            max((done_t[d] for d in op.deps), default=0.0))
                cost = op.cost
                if e == "act" and op.tset is not None and op.tset != act_set[0]:
                    cost += 1300
                    act_set[0] = op.tset
                fin = start + cost
                eng_free[e] = fin
                done_t[pick] = fin
                order[e].append(pick)
                scheduled += 1
                progressed = True
                for s_ in succs[pick]:
                    ndeps[s_] -= 1
                    if ndeps[s_] == 0:
                        heapq.heappush(ready[self.ops[s_].engine],
                                       (self.ops[s_].prio, s_))
            if not progressed:
                raise RuntimeError("scheduler stuck (cyclic deps?)")
        for e in ENG_ENGINES:
            for pos, oi in enumerate(order[e]):
                self.ops[oi].order = pos
        self.makespan = max(done_t) if n else 0.0

    def finalize(self):
        self.schedule()
        self.eng_seq = {e: [] for e in ENG_ENGINES}
        self.q_count = {}
        self.eng_order = {e: sorted((o for o in self.ops if o.engine == e),
                                    key=lambda o: o.order)
                          for e in ENG_ENGINES}
        qn = {}
        ncoll = 0
        for e in ENG_ENGINES:
            for op in self.eng_order[e]:
                if op.is_dma:
                    q = QUEUE_OF_ENGINE[op.engine]
                    nq = qn.get(q, 0)
                    qn[q] = nq + 1
                    slot = nq % NSEM_POOL
                    key = (q, slot)
                    base = self.q_count.get(key, 0)
                    self.q_count[key] = base + op.n_dma
                    op.queue = (q, slot, (base + op.n_dma) * 16)
                elif op.is_coll:
                    op.queue = ("coll", ncoll, 1)
                    ncoll += 1
                else:
                    op.queue = None
                    self.eng_seq[op.engine].append(op.idx)
                    op.seq = len(self.eng_seq[op.engine])

        done = [False] * len(self.ops)
        heads = {e: 0 for e in ENG_ENGINES}
        streams = self.eng_order
        progress = True
        ndone = 0
        while progress:
            progress = False
            for e in ENG_ENGINES:
                while heads[e] < len(streams[e]):
                    op = streams[e][heads[e]]
                    if all(done[d] for d in op.deps):
                        done[op.idx] = True
                        heads[e] += 1
                        ndone += 1
                        progress = True
                    else:
                        break
        if ndone != len(self.ops):
            stuck = [o for o in self.ops if not done[o.idx]][:5]
            raise RuntimeError("deadlock in op graph; first stuck: " +
                               ", ".join(f"{o.tag}@{o.engine}" for o in stuck))

    def emit(self, nc, block):
        es = ExitStack()
        eng_sem = {e: es.enter_context(nc.semaphore(f"es_{e}")) for e in ENG_ENGINES}
        pool_sems = {}
        for q in ("qsp", "qact", "qpool"):
            pool_sems[q] = [es.enter_context(nc.semaphore(f"{q}_{i}"))
                            for i in range(NSEM_POOL)]
        n_coll = sum(1 for o in self.ops if o.is_coll)
        coll_sems = [es.enter_context(nc.semaphore(f"coll{i}"))
                     for i in range(max(1, n_coll))]
        self._es = es
        sched = self

        def dep_points(op):
            pts = {}
            for d in sorted(op.deps):
                p = sched.ops[d]
                if p.is_dma:
                    q, slot, val = p.queue
                    key = ("q", q, slot)
                    pts[key] = max(pts.get(key, 0), val)
                elif p.is_coll:
                    _, cidx, val = p.queue
                    pts[("coll", cidx)] = max(pts.get(("coll", cidx), 0), val)
                else:
                    key = ("e", p.engine)
                    pts[key] = max(pts.get(key, 0), p.seq)
            return pts

        waited = {e: {} for e in ENG_ENGINES}

        def run_stream(eng_name, eng):
            for op in sched.eng_order[eng_name]:
                for key, val in sorted(dep_points(op).items(), key=str):
                    if waited[eng_name].get(key, 0) >= val:
                        continue
                    if key[0] == "q":
                        sem = pool_sems[key[1]][key[2]]
                    elif key[0] == "coll":
                        sem = coll_sems[key[1]]
                    else:
                        sem = eng_sem[key[1]]
                    eng.wait_ge(sem, val)
                    waited[eng_name][key] = val
                if op.is_dma:
                    q, slot, _ = op.queue
                    insts = op.fn(eng)
                    if not isinstance(insts, (list, tuple)):
                        insts = [insts]
                    assert len(insts) == op.n_dma, f"{op.tag}: n_dma mismatch {len(insts)} != {op.n_dma}"
                    for ins in insts:
                        ins.then_inc(pool_sems[q][slot], 16)
                elif op.is_coll:
                    ins = op.fn(eng)
                    ins.then_inc(coll_sems[op.queue[1]])
                else:
                    ins = op.fn(eng)
                    assert ins is not None, f"{op.tag}: fn returned None"
                    ins.then_inc(eng_sem[op.engine], 1)

        @block.sync
        def _(sync):
            run_stream("sp", sync)

        @block.scalar
        def _(scalar):
            run_stream("act", scalar)

        @block.vector
        def _(vector):
            run_stream("dve", vector)

        @block.tensor
        def _(tensor):
            run_stream("pe", tensor)

        @block.gpsimd
        def _(gpsimd):
            run_stream("pool", gpsimd)


def build_program(phase_limit=4, p2_parts=9, taps=False):
    nc = bass.Bass("TRN2", target_bir_lowering=False)
    s = Sched()
    op_prio_base = [0]
    import builtins
    builtins.op_prio_base = op_prio_base

    din = {}
    def IN(name, shape, dt=F32):
        din[name] = nc.dram_tensor(name, list(shape), dt, kind="ExternalInput").ap()
        return din[name]

    x_d = IN("x", (SPAN, E))
    xh_d = IN("xh", (16, E))
    msk_d = IN("msk", (128, NCORES))
    tri_d = IN("tri", (128, 128))
    hones_d = IN("hones", (128, 2), BF16)
    identb_d = IN("identb", (128, 128), BF16)
    cw_d = IN("cw", (128, 32))
    aug_d = IN("augr", (128, 4))
    bg_d = IN("bg", (1, E))
    wq_d = IN("wq", (E, 512))
    wkg_d = IN("wkg", (E, 512))
    wv_d = IN("wv", (E, E))
    wg_d = IN("wg", (E, E))
    wout_d = IN("wout", (E, E))
    wgsh_d = IN("wgsh", (I_FFN // 8, E))
    wush_d = IN("wush", (I_FFN // 8, E))
    wdsh_d = IN("wdsh", (E // 8, I_FFN))

    y_d = nc.dram_tensor("y", [SPAN, E], F32, kind="ExternalOutput").ap()

    xq_s = nc.dram_tensor("xq_s", [SPAN, E], BF16).ap()
    sg_s = nc.dram_tensor("sg_s", [SPAN, E], BF16).ap()
    op_s = nc.dram_tensor("op_s", [SPAN, E], BF16).ap()
    qg_s = nc.dram_tensor("qg_s", [SPAN, 512], BF16).ap()
    ffn_s = nc.dram_tensor("ffn_s", [SPAN, E], BF16).ap()
    wgt_st = nc.dram_tensor("wgt_st", [E, 352], BF16).ap()
    wut_st = nc.dram_tensor("wut_st", [E, 352], BF16).ap()
    wdt_st = nc.dram_tensor("wdt_st", [I_FFN, 128], BF16).ap()
    g_wg = nc.dram_tensor("g_wg", [NCORES * E, 352], BF16, addr_space="Shared").ap()
    g_wu = nc.dram_tensor("g_wu", [NCORES * E, 352], BF16, addr_space="Shared").ap()
    g_wd = nc.dram_tensor("g_wd", [NCORES * I_FFN, 128], BF16, addr_space="Shared").ap()
    ag2_in = nc.dram_tensor("ag2_in", [128, 516], F32).ap()
    ag2_out = nc.dram_tensor("ag2_out", [NCORES * 128, 516], F32, addr_space="Shared").ap()

    main_es = ExitStack()

    def SB(stack, name, shape, dt):
        return stack.enter_context(nc.sbuf_tensor("s_" + name, list(shape), dt))

    P = main_es.enter_context(nc.psum_tensor("P", [128, 4096], F32))
    def BANK(b, cols=512, off=0):
        return P[:, 512 * b + off: 512 * b + off + cols]

    tri = SB(main_es, "tri", (128, 128), F32)
    hones = SB(main_es, "hones", (128, 2), BF16)
    identb = SB(main_es, "identb", (128, 128), BF16)
    msk = SB(main_es, "msk", (128, NCORES), F32)
    cw = SB(main_es, "cw", (128, 32), F32)
    augr = SB(main_es, "augr", (128, 4), F32)
    bgf = SB(main_es, "bgf", (1, E), F32)
    bgb = SB(main_es, "bgb", (1, E), BF16)
    ones1 = SB(main_es, "ones1", (1, 128), BF16)
    S_sb = SB(main_es, "S_sb", (128, 512), F32)
    S_b16 = SB(main_es, "S_b16", (128, 512), BF16)
    Brun = SB(main_es, "Brun", (128, 4), F32)
    summ = SB(main_es, "summ", (128, 516), F32)
    konst = SB(main_es, "konst", (128, 8), F32)
    qgt2 = SB(main_es, "qgt2", (128, 512), F32)

    def dma(eng, out, in_):
        return eng.dma_start(out=out, in_=in_)

    s.add("sp", lambda e: [dma(e, tri[:], tri_d), dma(e, hones[:], hones_d),
                           dma(e, identb[:], identb_d), dma(e, msk[:], msk_d),
                           dma(e, cw[:], cw_d), dma(e, augr[:], aug_d),
                           dma(e, bgf[:], bg_d)],
          writes=["tri", "hones", "identb", "msk", "cw", "augr", "bgf"],
          n_dma=7, tag="ld_consts")
    s.add("dve", lambda e: e.tensor_copy(bgb[:], bgf[:]), reads=["bgf"],
          writes=["bgb"], tag="bgb")
    s.add("dve", lambda e: e.memset(ones1[:], 1.0), writes=["ones1"], tag="ones1")
    s.add("dve", lambda e: e.memset(S_sb[:], 0.0), writes=["S"], tag="S0")
    s.add("dve", lambda e: e.memset(S_b16[:], 0.0), writes=["Sb16"], tag="Sb0")
    s.add("dve", lambda e: e.memset(Brun[:], 0.0), writes=["Brun"], tag="Br0")
    KV = {0: 1e-5, 1: 1.0, 2: -0.5, 3: LN16, 4: -LN16, 5: LOG0125, 6: -1.0}
    def kmemset(e):
        last = None
        for i, v in KV.items():
            last = e.memset(konst[:, i:i + 1], v)
        return last
    s.add("dve", kmemset, writes=["konst"], tag="konst")
    K = lambda i: konst[:, i:i + 1]

    p1_es = ExitStack()
    wq_sb = SB(p1_es, "wq_sb", (128, 8 * 512), BF16)
    wkg_sb = SB(p1_es, "wkg_sb", (128, 8 * 512), BF16)
    wv_sb = SB(p1_es, "wv_sb", (128, 8 * 1024), BF16)
    wg_sb = SB(p1_es, "wg_sb", (128, 8 * 1024), BF16)

    def cast_w(eng, dst, src, mcols):
        return dma(eng, dst.rearrange("p (kt m) -> p kt m", m=mcols),
                   src.rearrange("(kt p) m -> p kt m", p=128))
    s.add("pool", lambda e: [cast_w(e, wq_sb[:], wq_d, 512),
                             cast_w(e, wkg_sb[:], wkg_d, 512)],
          writes=["wq", "wkg"], n_dma=2, tag="ld_wqkg")
    s.add("pool", lambda e: [cast_w(e, wv_sb[:], wv_d, 1024),
                             cast_w(e, wg_sb[:], wg_d, 1024)],
          writes=["wv", "wg"], n_dma=2, tag="ld_wvg")

    wsh = SB(p1_es, "wsh", (128, 3 * 1024), F32)
    wshq = SB(p1_es, "wshq", (128, 3 * 1024), BF16)
    qt_bmax = SB(p1_es, "qt_bmax", (128, 192), F32)
    qt_sc = SB(p1_es, "qt_sc", (128, 192), F32)
    qt_si = SB(p1_es, "qt_si", (128, 192), F32)
    qt_xa = SB(p1_es, "qt_xa", (128, 3 * 1024), F32)
    qt_t1 = SB(p1_es, "qt_t1", (128, 3 * 1024), F32)
    stg = SB(p1_es, "stg", (128, 352), BF16)

    def _chain_ops(bufset, src_r3, xa, t1, gt2, si, sc, dst_r3, cols, blk,
                   rds, wr, cost_scale=1.0):
        MAGIC = float(2 ** 23)
        cd = cols
        def f_chain(e):
            e.tensor_tensor(src_r3(xa), src_r3(xa), si, ALU.mult)
            xd, td = xa[:, 0:cd], t1[:, 0:cd]
            e.tensor_scalar(td, xd, 2.0, MAGIC, ALU.mult, ALU.add)
            e.tensor_scalar(td, td, MAGIC, 0.5, ALU.subtract, ALU.mult)
            e.tensor_scalar_min(td, td, 2.0)
            e.scalar_tensor_tensor(td, xd, 2.5, td, ALU.is_gt, ALU.add)
            e.scalar_tensor_tensor(td, xd, 3.5, td, ALU.is_gt, ALU.add)
            e.scalar_tensor_tensor(td, xd, 5.0, td, ALU.is_gt, ALU.add)
            return e.scalar_tensor_tensor(td, xd, 5.0, td, ALU.is_gt, ALU.add)
        s.add("dve", f_chain, reads=[bufset + "_xa"], sreads=[bufset + "_si"],
              writes=[bufset + "_t1", bufset + "_xa"], tag=bufset + "q1",
              cost=int((cols + cd * 6.0) * cost_scale + 500))
        if cd < cols:
            def g_chain(e):
                xg, tg, t2g = xa[:, cd:cols], t1[:, cd:cols], gt2[:, 0:cols - cd]
                e.tensor_scalar(tg, xg, 2.0, MAGIC, ALU.mult, ALU.add)
                e.tensor_scalar(tg, tg, MAGIC, 0.5, ALU.subtract, ALU.mult)
                e.tensor_scalar_min(tg, tg, 2.0)
                for thr, rep in ((2.5, 1), (3.5, 1), (5.0, 2)):
                    e.tensor_single_scalar(t2g, xg, thr, ALU.is_gt)
                    for _ in range(rep):
                        e.tensor_add(tg, tg, t2g)
                return e.tensor_copy(gt2[:, 0:1], tg[:, 0:1])
            s.add("pool", g_chain, reads=[bufset + "_xa"],
                  writes=[bufset + "_t1g", bufset + "_gt2"], tag=bufset + "q1g",
                  cost=int((cols - cd) * 11.0 + 800))
        return cd

    def emit_quant(src_ap, dst_ap, nblk, cols, bufset, xa, t1, bmax, sc, si,
                   rd, wr, blk=16):
        rds_ = [rd]
        r3 = lambda ap: ap.rearrange("p (b i) -> p b i", i=blk)
        bca = lambda ap: ap[:, :].unsqueeze(2).to_broadcast([128, nblk, blk])
        def f_scale(e):
            e.tensor_reduce(bmax, r3(src_ap), AX.X, ALU.max, apply_absolute_value=True)
            return e.tensor_scalar(sc, bmax, 1.0 / 6.0, 1e-12, ALU.mult, ALU.max)
        s.add("dve", f_scale, sreads=rds_, writes=[bufset + "_sc"], tag=bufset + "q0",
              force=True, cost=int(cols * 1.4 + 150))
        def f_recip(e):
            return e.reciprocal(si, sc)
        s.add("dve", f_recip, sreads=[bufset + "_sc"], writes=[bufset + "_si"],
              tag=bufset + "qr", cost=300)
        def f_abs(e):
            return e.activation(xa, src_ap, AF.Abs)
        s.add("act", f_abs, reads=rds_, writes=[bufset + "_xa"], tag=bufset + "qa",
              cost=int(cols * 0.9 + 200))
        def xa_r3(ap):
            return ap.rearrange("p (b i) -> p b i", i=blk)
        cd = _chain_ops(bufset, xa_r3, xa, t1, qgt2, bca(si), sc, None, cols, blk,
                        rds_, wr)
        def f_sign(e):
            return e.activation(xa, src_ap, AF.Sign)
        s.add("act", f_sign, reads=rds_, writes=[bufset + "_xa"], tag=bufset + "qs",
              cost=int(cols * 0.9 + 200))
        def f_fin(e):
            e.tensor_mul(t1, t1, xa)
            return e.tensor_tensor(r3(dst_ap), r3(t1), bca(sc), ALU.mult)
        s.add("dve", f_fin, reads=[bufset + "_xa", bufset + "_t1g"],
              sreads=[bufset + "_sc", bufset + "_t1"],
              writes=[wr, bufset + "_t1"], tag=bufset + "q2",
              cost=int(cols * 2.2 + 300))

    def emit_quant2(src_ap, dst_ap, nblk, cols, bufset, xa, t1, bmax, sc, si,
                   rds, wr, blk=16):
        rds_ = rds
        r3 = lambda ap: ap.rearrange("p (b i) -> p b i", i=blk)
        bca = lambda ap: ap[:, :].unsqueeze(2).to_broadcast([128, nblk, blk])
        def f_scale(e):
            e.tensor_reduce(bmax, r3(src_ap), AX.X, ALU.max, apply_absolute_value=True)
            return e.tensor_scalar(sc, bmax, 1.0 / 6.0, 1e-12, ALU.mult, ALU.max)
        s.add("dve", f_scale, sreads=rds_, writes=[bufset + "_sc"], tag=bufset + "q0",
              force=True, cost=int(cols * 1.4 + 150))
        def f_recip(e):
            return e.reciprocal(si, sc)
        s.add("dve", f_recip, sreads=[bufset + "_sc"], writes=[bufset + "_si"],
              tag=bufset + "qr", cost=300)
        def f_abs(e):
            return e.activation(xa, src_ap, AF.Abs)
        s.add("act", f_abs, reads=rds_, writes=[bufset + "_xa"], tag=bufset + "qa",
              cost=int(cols * 0.9 + 200))
        def xa_r3(ap):
            return ap.rearrange("p (b i) -> p b i", i=blk)
        cd = _chain_ops(bufset, xa_r3, xa, t1, qgt2, bca(si), sc, None, cols, blk,
                        rds_, wr)
        def f_sign(e):
            return e.activation(xa, src_ap, AF.Sign)
        s.add("act", f_sign, reads=rds_, writes=[bufset + "_xa"], tag=bufset + "qs",
              cost=int(cols * 0.9 + 200))
        def f_fin(e):
            e.tensor_mul(t1, t1, xa)
            return e.tensor_tensor(r3(dst_ap), r3(t1), bca(sc), ALU.mult)
        s.add("dve", f_fin, reads=[bufset + "_xa", bufset + "_t1g"],
              sreads=[bufset + "_sc", bufset + "_t1"],
              writes=[wr, bufset + "_t1"], tag=bufset + "q2",
              cost=int(cols * 2.2 + 300))

    def prep_gate_like(src_d, stage_d, name):
        s.add("dve", lambda e: e.memset(wsh[:], 0.0), writes=["wsh"], tag=f"{name}z")
        def ld(e):
            return [dma(e, wsh[0:128, 0:1024], src_d[0:128, :]),
                    dma(e, wsh[0:128, 1024:2048], src_d[128:256, :]),
                    dma(e, wsh[0:96, 2048:3072], src_d[256:352, :])]
        s.add("sp", ld, writes=["wsh"], n_dma=3, tag=f"{name}ld")
        emit_quant(wsh[:, 0:3072], wshq[:, 0:3072], 192, 3072, "qw",
                   qt_xa[:, 0:3072], qt_t1[:, 0:3072], qt_bmax[:, 0:192],
                   qt_sc[:, 0:192], qt_si[:, 0:192], "wsh", "wshq")
        pbf = BANK(0, 176, 0).bitcast(BF16)
        for kt in range(8):
            def tr(e, kt=kt):
                last = None
                for pt in range(3):
                    pp = 128 if pt < 2 else 96
                    last = nc.tensor.transpose(
                        pbf[:, 128 * pt: 128 * pt + pp],
                        wshq[0:pp, 1024 * pt + 128 * kt: 1024 * pt + 128 * (kt + 1)],
                        identb[0:pp, 0:pp])
                return last
            s.add("pe", tr, reads=["wshq", "identb", "psb0"], writes=["psb0"],
                  tag=f"{name}t{kt}")
            def ev(e, kt=kt):
                return e.copy(stg[:, 0:352], pbf[:, 0:352])
            s.add("act", ev, reads=["psb0"], writes=["stg", "psb0"], tag=f"{name}e{kt}")
            def st(e, kt=kt, stage_d=stage_d):
                return dma(e, stage_d[128 * kt: 128 * (kt + 1), :], stg[:, 0:352])
            s.add("pool", st, reads=["stg"], writes=[f"{name}st{kt}"], n_dma=1,
                  tag=f"{name}s{kt}")

    op_prio_base[0] = 100000
    prep_gate_like(wgsh_d, wgt_st, "pwg")
    prep_gate_like(wush_d, wut_st, "pwu")

    s.add("sp", lambda e: dma(e, wsh[:, 0:2816], wdsh_d), writes=["wsh"],
          n_dma=1, tag="pwdld")
    emit_quant(wsh[:, 0:2816], wshq[:, 0:2816], 176, 2816, "qw",
               qt_xa[:, 0:2816], qt_t1[:, 0:2816], qt_bmax[:, 0:176],
               qt_sc[:, 0:176], qt_si[:, 0:176], "wsh", "wshq")
    pbf2 = BANK(0, 64, 0).bitcast(BF16)
    for it in range(22):
        def tr(e, it=it):
            return nc.tensor.transpose(pbf2[:, 0:128],
                                       wshq[:, 128 * it: 128 * (it + 1)],
                                       identb[:])
        s.add("pe", tr, reads=["wshq", "identb", "psb0"], writes=["psb0"],
              tag=f"pwdt{it}")
        def ev(e, it=it):
            return e.copy(stg[:, 0:128], pbf2[:, 0:128])
        s.add("act", ev, reads=["psb0"], writes=["stg", "psb0"], tag=f"pwde{it}")
        def st(e, it=it):
            return dma(e, wdt_st[128 * it: 128 * (it + 1), :], stg[:, 0:128])
        s.add("pool", st, reads=["stg"], writes=[f"pwdst{it}"], n_dma=1,
              tag=f"pwds{it}")

    def coll(kind_ins, kind_outs, tag, deps, wres):
        def f(e):
            return e.collective_compute(
                "AllGather", ALU.bypass,
                replica_groups=[list(range(NCORES))],
                ins=[kind_ins.opt()], outs=[kind_outs.opt()])
        s.add("pool", f, reads=deps, writes=[wres], is_coll=True, tag=tag)
    coll(wgt_st, g_wg, "coll_wg", [f"pwgst{k}" for k in range(8)], "gwg")
    coll(wut_st, g_wu, "coll_wu", [f"pwust{k}" for k in range(8)], "gwu")
    coll(wdt_st, g_wd, "coll_wd", [f"pwdst{k}" for k in range(22)], "gwd")
    op_prio_base[0] = 0

    x_sb = [SB(p1_es, f"x{p}", (128, E), F32) for p in range(2)]
    bn6 = [SB(p1_es, "bn6" + str(_p), (128, 12), F32) for _p in range(2)]
    bnag = [SB(p1_es, "bnag" + str(_p), (128, 2), F32) for _p in range(2)]
    rstd = [SB(p1_es, "rstd" + str(_p), (128, 1), F32) for _p in range(2)]
    lntmp = SB(p1_es, "lntmp", (128, 1), F32)
    xn = SB(p1_es, "xn", (128, E), F32)
    xnb = SB(p1_es, "xnb", (128, E), BF16)
    XNF_P = 144
    q_bmax = SB(p1_es, "q_bmax", (128, 64), F32)
    q_sc = SB(p1_es, "q_sc", (128, 64), F32)
    q_si = SB(p1_es, "q_si", (128, 64), F32)
    q_xa = SB(p1_es, "q_xa", (128, E), F32)
    q_t1 = SB(p1_es, "q_t1", (128, E), F32)
    xq_sb = [SB(p1_es, f"xq{p}", (128, E), BF16) for p in range(2)]
    xnf = [SB(p1_es, f"xnf{p}", (128, 8 * XNF_P), BF16) for p in range(2)]
    cacc = SB(p1_es, "cacc", (128, 8 * 128), F32)
    xc = [SB(p1_es, "xc" + str(_p), (128, 8 * 128), BF16) for _p in range(2)]
    q_fm = SB(p1_es, "q_fm", (128, 512), F32)
    kg_fm = SB(p1_es, "kg_fm", (128, 512), F32)
    lsb = SB(p1_es, "lsb", (128, 512), F32)
    bcum = SB(p1_es, "bcum", (128, 512), F32)
    epos = SB(p1_es, "epos", (128, 512), F32)
    eneg = SB(p1_es, "eneg", (128, 512), F32)
    ekb = SB(p1_es, "ekb", (128, 512), F32)
    kfp = SB(p1_es, "kfp", (128, 512), F32)
    ktf = SB(p1_es, "ktf", (128, 512), F32)
    qt_b = SB(p1_es, "qt_b", (128, 512), BF16)
    kt_b = [SB(p1_es, "kt_b" + str(_p), (128, 512), BF16) for _p in range(2)]
    kh_b = [SB(p1_es, "kh_b" + str(_p), (128, 512), BF16) for _p in range(2)]
    khn = [SB(p1_es, f"khn{p}", (128, 512), BF16) for p in range(2)]
    qg_b = [SB(p1_es, f"qg{p}", (128, 512), BF16) for p in range(2)]
    ebt = [SB(p1_es, "ebt" + str(_p), (128, 4), F32) for _p in range(2)]
    ebrun = [SB(p1_es, "ebrun" + str(_p), (128, 4), F32) for _p in range(2)]
    m2b = SB(p1_es, "m2b", (128, 512), BF16)
    m1f = SB(p1_es, "m1f", (128, 512), F32)
    augw = [SB(p1_es, f"augw{p}", (128, 8), F32) for p in range(2)]
    v_nat = [SB(p1_es, f"vnat{p}", (128, E), BF16) for p in range(2)]
    g_raw = [SB(p1_es, f"graw{p}", (128, E), F32) for p in range(2)]
    o_acc = [SB(p1_es, f"oacc{p}", (128, E), F32) for p in range(2)]
    opre_b = [SB(p1_es, f"opre{p}", (128, E), BF16) for p in range(2)]
    sg_b = [SB(p1_es, f"sgb{p}", (128, E), BF16) for p in range(2)]
    atm = [SB(p1_es, f"atm{p}", (128, 128), BF16) for p in range(2)]
    augarg = SB(p1_es, "augarg", (128, E), BF16)

    s.add("sp", lambda e: dma(e, x_sb[0][0:16, :], xh_d), writes=["x0"],
          n_dma=1, tag="ld_xh")
    s.add("dve", lambda e: (e.bn_stats(bn6[1][0:16, 0:6], x_sb[0][0:16, 0:512]),
                            e.bn_stats(bn6[1][0:16, 6:12], x_sb[0][0:16, 512:1024]))[-1],
          reads=["x0"], writes=["bn61"], tag="h_bn", force=True)
    s.add("dve", lambda e: e.bn_aggr(bnag[1][0:16, :], bn6[1][0:16, :]),
          reads=["bn61"], writes=["bnag1"], tag="h_ag", force=True)
    s.add("act", lambda e: (e.activation(lntmp[0:16], bnag[1][0:16, 1:2], AF.Ln, bias=konst[0:16, 0:1]),
                            e.activation(rstd[1][0:16], lntmp[0:16], AF.Exp, scale=konst[0:16, 2:3]))[-1],
          reads=["bnag1", "konst"], writes=["rstd1"], tag="h_rstd", force=True,
          tset="E", cost=600)
    s.add("dve", lambda e: e.tensor_scalar(xn[0:16, :], x_sb[0][0:16, :],
                                           bnag[1][0:16, 0:1], rstd[1][0:16, :],
                                           ALU.subtract, ALU.mult),
          reads=["x0"], sreads=["bnag1", "rstd1"], writes=["xn"], tag="h_xn")
    s.add("dve", lambda e: e.tensor_copy(xnb[0:16, :], xn[0:16, :]),
          reads=["xn"], writes=["xnb"], tag="h_xnb")
    def h_tr(e):
        last = None
        for kt in range(8):
            last = nc.tensor.transpose(
                P[:, 64 * kt + 8 * kt: 64 * kt + 8 * kt + 8].bitcast(BF16),
                xnb[0:16, 128 * kt: 128 * (kt + 1)], identb[0:16, 0:16])
        return last
    s.add("pe", h_tr, reads=["xnb", "identb", "psb0", "psb1"],
          writes=["psb0", "psb1"], tag="h_tr")
    def h_ev(e):
        last = None
        for kt in range(8):
            last = e.copy(xnf[1][:, XNF_P * kt: XNF_P * kt + 16],
                          P[:, 72 * kt: 72 * kt + 8].bitcast(BF16))
        return last
    s.add("act", h_ev, reads=["psb0", "psb1"], writes=["xnfd1", "psb0", "psb1"],
          tag="h_ev")
    def h_cp(e):
        src = xnf[1].rearrange("p (k c) -> p k c", c=XNF_P)
        return e.tensor_copy(src[:, :, 141:144], src[:, :, 13:16])
    s.add("dve", h_cp, reads=["xnfd1"], writes=["xnfh1", "xnfd1"], tag="h_cp")

    def p1_tile(i, bn6=bn6, bnag=bnag, rstd=rstd,
                xc=xc,
                kt_b=kt_b, kh_b=kh_b, ebt=ebt, ebrun=ebrun):
        p = i % 2
        pp = 1 - p
        bn6 = bn6[p]; bnag = bnag[p]; rstd = rstd[p]
        xc = xc[p]
        kt_b = kt_b[p]; kh_b = kh_b[p]
        ebt = ebt[p]; ebrun = ebrun[p]
        xr = lambda buf: buf.rearrange("p (k c) -> p k c", c=XNF_P)

        s.add("sp", lambda e: dma(e, x_sb[p][:], x_d[C * i: C * (i + 1), :]),
              writes=[f"x{p}"], n_dma=1, tag=f"ldx{i}")
        s.add("dve", lambda e: (e.bn_stats(bn6[:, 0:6], x_sb[p][:, 0:512]),
                                e.bn_stats(bn6[:, 6:12], x_sb[p][:, 512:1024]))[-1],
              reads=[f"x{p}"], writes=["bn6" + str(p)], tag=f"bn{i}", force=True)
        s.add("dve", lambda e: e.bn_aggr(bnag[:], bn6[:]),
              reads=["bn6" + str(p)], writes=["bnag" + str(p)], tag=f"ag{i}", force=True)
        s.add("act", lambda e: (e.activation(lntmp[:], bnag[:, 1:2], AF.Ln, bias=K(0)),
                                e.activation(rstd[:], lntmp[:], AF.Exp, scale=K(2)))[-1],
              reads=["bnag" + str(p), "konst"], writes=["rstd" + str(p)], tag=f"rstd{i}", force=True,
              tset="E", cost=600)
        s.add("dve", lambda e: e.tensor_scalar(xn[:], x_sb[p][:], bnag[:, 0:1],
                                               rstd[:], ALU.subtract, ALU.mult),
              reads=[f"x{p}"], sreads=["bnag" + str(p), "rstd" + str(p)], writes=["xn"], tag=f"xn{i}")
        s.add("dve", lambda e: e.tensor_copy(xnb[:], xn[:]),
              reads=["xn"], writes=["xnb"], tag=f"xnb{i}")
        def trx(e):
            last = None
            for kt in range(8):
                last = nc.tensor.transpose(
                    P[:, 64 * kt: 64 * (kt + 1)].bitcast(BF16),
                    xnb[:, 128 * kt: 128 * (kt + 1)], identb[:])
            return last
        s.add("pe", trx, reads=["xnb", "identb", "psb0"], writes=["psb0"],
              tag=f"trx{i}", cost=2300)
        def evx(e):
            return e.copy(xr(xnf[p])[:, :, 16:144],
                          P[:, 0:512].bitcast(BF16).rearrange("q (k c) -> q k c", c=128))
        s.add("act", evx, reads=["psb0"], writes=[f"xnfd{p}", "psb0"], tag=f"evx{i}")
        def hcp(e):
            return e.tensor_copy(xr(xnf[p])[:, :, 13:16], xr(xnf[pp])[:, :, 141:144])
        s.add("dve", hcp, reads=[f"xnfd{pp}", f"xnfh{pp}"],
              writes=[f"xnfh{p}"], tag=f"hcp{i}")

        emit_quant(xn[:], xq_sb[p][:], 64, E, "qx",
                   q_xa[:], q_t1[:], q_bmax[:], q_sc[:], q_si[:],
                   "xn", f"xqv{p}")
        s.add("pool", lambda e: dma(e, xq_s[C * i: C * (i + 1), :], xq_sb[p][:]),
              reads=[f"xqv{p}"], writes=[f"xqs{i}"], n_dma=1, tag=f"stxq{i}")

        def conv(e):
            v = xr(xnf[p])
            last = None
            for kt in range(8):
                e.tensor_scalar_mul(cacc[:, 128 * kt: 128 * (kt + 1)],
                                    v[:, kt, 13:141], cw[:, 4 * kt: 4 * kt + 1])
                for d in range(1, 4):
                    last = e.scalar_tensor_tensor(
                        cacc[:, 128 * kt: 128 * (kt + 1)],
                        v[:, kt, 13 + d: 141 + d], cw[:, 4 * kt + d: 4 * kt + d + 1],
                        cacc[:, 128 * kt: 128 * (kt + 1)], ALU.mult, ALU.add)
            return last
        s.add("dve", conv, reads=[f"xnfd{p}", f"xnfh{p}"], sreads=["cw"],
              writes=["cacc"], tag=f"cv{i}", cost=6500)
        s.add("act", lambda e: e.activation(xc[:], cacc[:], AF.Silu),
              reads=["cacc"], writes=["xc" + str(p)], tag=f"cs{i}", tset="S", cost=1100)

        if i > 0:
            emit_aug_sgg(i - 1, pp)

        def qkg(e, bank, wsb, wname):
            last = None
            for mt in range(4):
                for kt in range(8):
                    last = nc.tensor.matmul(
                        BANK(bank, 128, 128 * mt),
                        wsb[:, kt * 512 + mt * 128: kt * 512 + (mt + 1) * 128],
                        xc[:, 128 * kt: 128 * (kt + 1)],
                        start=(kt == 0), stop=(kt == 7))
            return last
        s.add("pe", lambda e: qkg(e, 0, wq_sb, "wq"), reads=["xc" + str(p), "wq"],
              writes=["psb0"], tag=f"qk0_{i}", cost=3800)
        s.add("pe", lambda e: qkg(e, 1, wkg_sb, "wkg"), reads=["xc" + str(p), "wkg"],
              writes=["psb1"], tag=f"qk1_{i}", cost=3800)
        s.add("act", lambda e: (e.copy(q_fm[:], BANK(0)),
                                e.copy(kg_fm[:], BANK(1)))[-1],
              reads=["psb0", "psb1"], writes=["qfm", "kgfm", "psb0", "psb1"],
              tag=f"evqk{i}")

        def vg(e):
            last = None
            for half, bank in ((0, 2), (1, 3)):
                for kt in range(8):
                    last = nc.tensor.matmul(
                        BANK(bank), xc[:, 128 * kt: 128 * (kt + 1)],
                        wv_sb[:, kt * 1024 + 512 * half: kt * 1024 + 512 * (half + 1)],
                        start=(kt == 0), stop=(kt == 7))
            return last
        def gg(e):
            last = None
            for half, bank in ((0, 4), (1, 5)):
                for kt in range(8):
                    nc.tensor.matmul(
                        BANK(bank), xc[:, 128 * kt: 128 * (kt + 1)],
                        wg_sb[:, kt * 1024 + 512 * half: kt * 1024 + 512 * (half + 1)],
                        start=(kt == 0), stop=False)
                last = nc.tensor.matmul(
                    BANK(bank), ones1[:], bgb[:, 512 * half: 512 * (half + 1)],
                    start=False, stop=True)
            return last
        s.add("pe", vg, reads=["xc" + str(p), "wv"], writes=["psb2", "psb3"], tag=f"v{i}",
              cost=4400)
        s.add("pe", gg, reads=["xc" + str(p), "wg", "ones1", "bgb"],
              writes=["psb4", "psb5"], tag=f"g{i}", cost=4800)
        s.add("act", lambda e: (e.copy(v_nat[p][:, 0:512], BANK(2)),
                                e.copy(v_nat[p][:, 512:1024], BANK(3)))[-1],
              reads=["psb2", "psb3"], writes=[f"vnat{p}", "psb2", "psb3"],
              tag=f"evv{i}")
        s.add("act", lambda e: (e.copy(g_raw[p][:, 0:512], BANK(4)),
                                e.copy(g_raw[p][:, 512:1024], BANK(5)))[-1],
              reads=["psb4", "psb5"], writes=[f"graw{p}", "psb4", "psb5"],
              tag=f"evg{i}")

        s.add("act", lambda e: e.activation(ktf[:], kg_fm[:], AF.Abs),
              reads=["kgfm"], writes=["kgabs"], tag=f"ka{i}", cost=600)
        s.add("act", lambda e: (e.activation(epos[:], ktf[:], AF.Exp, scale=K(6)),
                                e.activation(lsb[:], epos[:], AF.Ln, bias=K(1)))[-1],
              reads=["kgabs", "konst"], writes=["lntm", "epos"], tag=f"lse{i}",
              tset="E", cost=1100)
        s.add("dve", lambda e: (e.tensor_single_scalar(eneg[:], kg_fm[:], 0.0, ALU.min),
                                e.tensor_sub(lsb[:], eneg[:], lsb[:]))[-1],
              reads=["kgfm", "lntm"], writes=["ls", "eneg"], tag=f"ls{i}")
        def scan(e):
            last = None
            for kt in range(4):
                sl = slice(128 * kt, 128 * (kt + 1))
                last = e.tensor_tensor_scan(bcum[:, sl], lsb[:, sl], lsb[:, sl],
                                            0.0, ALU.add, ALU.bypass)
            return last
        s.add("dve", scan, sreads=["ls"], writes=["bcum"], tag=f"sc{i}")
        def exps(e):
            e.activation(epos[:], bcum[:], AF.Exp, scale=K(3), bias=K(5))
            e.activation(eneg[:], bcum[:], AF.Exp, scale=K(4))
            e.activation(ekb[:], lsb[:], AF.Exp, scale=K(3))
            bc = bcum.rearrange("p (k c) -> p k c", c=128)
            e.activation(ebt[:], bc[:, :, 127:128], AF.Exp, scale=K(3))
            return e.activation(ebrun[:], Brun[:], AF.Exp, scale=K(3))
        s.add("act", exps, reads=["bcum", "ls", "Brun", "konst"],
              writes=["epos", "eneg", "ek", "ebt" + str(p), "ebrun" + str(p)], tag=f"ex{i}",
              tset="E", cost=2200)
        def kqt(e):
            e.tensor_scalar(kfp[:], ekb[:], -1.0, 1.0, ALU.mult, ALU.add)
            e.tensor_mul(qt_b[:], q_fm[:], epos[:])
            e.tensor_mul(ktf[:], kfp[:], eneg[:])
            e.tensor_copy(kt_b[:], ktf[:])
            eb = ebt[:, :].unsqueeze(2).to_broadcast([128, 4, 128])
            r4 = lambda ap: ap.rearrange("p (k c) -> p k c", c=128)
            return e.tensor_tensor(r4(kh_b[:]), r4(ktf[:]), eb, ALU.mult)
        s.add("dve", kqt, reads=["ek", "epos", "eneg", "qfm"], sreads=["ebt" + str(p)],
              writes=["kfp", "qtb", "ktb" + str(p), "khb" + str(p), "kgabs"], tag=f"kq{i}")
        def qg(e):
            eb = ebrun[:, :].unsqueeze(2).to_broadcast([128, 4, 128])
            r4 = lambda ap: ap.rearrange("p (k c) -> p k c", c=128)
            return e.tensor_tensor(r4(qg_b[p][:]), r4(qt_b[:]), eb, ALU.mult)
        s.add("dve", qg, reads=["qtb"], sreads=["ebrun" + str(p)], writes=[f"qgb{p}"],
              tag=f"qg{i}")
        s.add("pool", lambda e: dma(e, qg_s[C * i: C * (i + 1), :], qg_b[p][:]),
              reads=[f"qgb{p}"], n_dma=1, tag=f"stqg{i}")
        s.add("dve", lambda e: e.tensor_add(Brun[:],
                                            Brun[:],
                                            bcum.rearrange("p (k c) -> p k c", c=128)[:, :, 127]),
              reads=["bcum", "ebrun" + str(p)], writes=["Brun"], tag=f"br{i}")
        def trk(e):
            last = None
            for kt in range(4):
                last = nc.tensor.transpose(
                    P[:, 512 + 64 * kt: 512 + 64 * (kt + 1)].bitcast(BF16),
                    kh_b[:, 128 * kt: 128 * (kt + 1)], identb[:])
            return last
        s.add("pe", trk, reads=["khb" + str(p), "identb", "psb1"], writes=["psb1"],
              tag=f"trk{i}", cost=1200)
        def evk(e):
            return e.copy(khn[p][:], P[:, 512:768].bitcast(BF16))
        s.add("act", evk, reads=["psb1"], writes=[f"khn{p}", "psb1"], tag=f"evk{i}")

        s.add("dve", lambda e: e.tensor_copy(S_b16[:], S_sb[:]),
              reads=["S"], writes=["Sb16"], tag=f"sb16{i}")

        def m2(e):
            e.tensor_mul(m1f[:], q_fm[:], kfp[:])
            ab = augr[:, :].unsqueeze(2).to_broadcast([128, 4, 128])
            r4 = lambda ap: ap.rearrange("p (k c) -> p k c", c=128)
            return e.tensor_tensor(r4(m2b[:]), r4(m1f[:]), ab, ALU.mult)
        s.add("dve", m2, reads=["qfm", "kfp"], sreads=["augr"], writes=["m2b"],
              tag=f"m2{i}")
        def augmm(e):
            last = None
            for kt in range(4):
                last = nc.tensor.matmul(BANK(3, 2, 2 * kt),
                                        m2b[:, 128 * kt: 128 * (kt + 1)],
                                        hones[:], start=True, stop=True)
            return last
        s.add("pe", augmm, reads=["m2b", "hones", "psb3"], writes=["psb3"],
              tag=f"au{i}")
        s.add("dve", lambda e: e.tensor_copy(augw[p][:], BANK(3, 8, 0)),
              reads=["psb3"], writes=[f"augw{p}", "psb3"], tag=f"evau{i}")

        for h in range(H):
            hb = 64 * (h % 2)
            ht = 128 * (h // 2)
            ab = 4 + (h % 2)
            ob = 6 + (h // 4)
            oc = 128 * (h % 4)
            def at(e, h=h, hb=hb, ht=ht, ab=ab):
                return nc.tensor.matmul(BANK(ab, 128, 0),
                                        kt_b[hb:hb + 64, ht:ht + 128],
                                        qt_b[hb:hb + 64, ht:ht + 128],
                                        start=True, stop=True)
            s.add("pe", at, reads=["ktb" + str(p), "qtb", f"psb{ab}"], writes=[f"psb{ab}"],
                  tag=f"at{i}_{h}")
            def mask(e, h=h, ab=ab):
                return e.tensor_tensor(atm[h % 2][:], BANK(ab, 128, 0), tri[:],
                                       ALU.mult)
            s.add("dve", mask, reads=[f"psb{ab}", "tri"],
                  writes=[f"atm{h % 2}", f"psb{ab}"], tag=f"mk{i}_{h}")
            def omm(e, h=h, hb=hb, ht=ht, ob=ob, oc=oc):
                nc.tensor.matmul(BANK(ob, 128, oc), atm[h % 2][:],
                                 v_nat[p][:, 128 * h: 128 * (h + 1)],
                                 start=True, stop=False)
                return nc.tensor.matmul(BANK(ob, 128, oc),
                                        qt_b[hb:hb + 64, ht:ht + 128],
                                        S_b16[hb:hb + 64, ht:ht + 128],
                                        start=False, stop=True)
            s.add("pe", omm, reads=[f"atm{h % 2}", f"vnat{p}", "qtb", "Sb16",
                                    f"psb{ob}"],
                  writes=[f"psb{ob}"], tag=f"o{i}_{h}")
            def umm(e, h=h, hb=hb, ht=ht):
                return nc.tensor.matmul(
                    P[hb:hb + 64, 1024 + ht: 1024 + ht + 128],
                    khn[p][:, 64 * h: 64 * (h + 1)],
                    v_nat[p][:, 128 * h: 128 * (h + 1)],
                    start=True, stop=True)
            s.add("pe", umm, reads=[f"khn{p}", f"vnat{p}", "psb2"],
                  writes=["psb2"], tag=f"u{i}_{h}")

        def supd(e):
            last = None
            for kt in range(4):
                sl = slice(128 * kt, 128 * (kt + 1))
                last = e.scalar_tensor_tensor(S_sb[:, sl], S_sb[:, sl],
                                              ebt[:, kt: kt + 1],
                                              BANK(2, 128, 128 * kt),
                                              ALU.mult, ALU.add)
            return last
        s.add("dve", supd, reads=["psb2", "S", "Sb16"], sreads=["ebt" + str(p)],
              writes=["S", "psb2"], tag=f"su{i}")

        s.add("act", lambda e: (e.copy(o_acc[p][:, 0:512], BANK(6)),
                                e.copy(o_acc[p][:, 512:1024], BANK(7)))[-1],
              reads=["psb6", "psb7"], writes=[f"oacc{p}", "psb6", "psb7"],
              tag=f"evo{i}")

    def emit_aug_sgg(j, pj):
        def arg(e):
            last = None
            for h in range(H):
                last = e.tensor_scalar_mul(augarg[:, 128 * h: 128 * (h + 1)],
                                           v_nat[pj][:, 128 * h: 128 * (h + 1)],
                                           augw[pj][:, h: h + 1])
            return last
        s.add("dve", arg, reads=[f"vnat{pj}"], sreads=[f"augw{pj}"],
              writes=["augarg"], tag=f"aa{j}")
        s.add("act", lambda e: e.activation(augarg[:], augarg[:], AF.Tanh, scale=0.5),
              reads=["augarg"], writes=["augsig"], tag=f"as{j}", tset="S", cost=1100)
        def oad(e):
            e.tensor_scalar(augarg[:], augarg[:], 0.5, 0.5, ALU.mult, ALU.add)
            return e.tensor_add(opre_b[pj][:], o_acc[pj][:], augarg[:])
        s.add("dve", oad,
              reads=["augsig", f"oacc{pj}"], writes=[f"opre{pj}", "augsig", "augarg"],
              tag=f"oad{j}")
        s.add("pool", lambda e: dma(e, op_s[C * j: C * (j + 1), :], opre_b[pj][:]),
              reads=[f"opre{pj}"], n_dma=1, tag=f"stop{j}")
        s.add("act", lambda e: e.activation(sg_b[pj][:], g_raw[pj][:], AF.Silu),
              reads=[f"graw{pj}"], writes=[f"sgb{pj}"], tag=f"sg{j}", tset="S",
              cost=1100)
        s.add("pool", lambda e: dma(e, sg_s[C * j: C * (j + 1), :], sg_b[pj][:]),
              reads=[f"sgb{pj}"], n_dma=1, tag=f"stsg{j}")

    s.enabled = phase_limit >= 2
    for i in range(NT):
        p1_tile(i)
    emit_aug_sgg(NT - 1, (NT - 1) % 2)

    s.add("act", lambda e: e.activation(summ[:, 512:516], Brun[:], AF.Exp, scale=K(3)),
          reads=["Brun", "konst"], writes=["summD"], tag="dspan", tset="E", cost=300)
    s.add("dve", lambda e: e.tensor_copy(summ[:, 0:512], S_sb[:]),
          reads=["S"], writes=["summS"], tag="packS")
    s.add("pool", lambda e: dma(e, ag2_in, summ[:]),
          reads=["summD", "summS"], writes=["ag2in"], n_dma=1, tag="st_summ")
    def coll2(e):
        return e.collective_compute("AllGather", ALU.bypass,
                                    replica_groups=[list(range(NCORES))],
                                    ins=[ag2_in.opt()], outs=[ag2_out.opt()])
    s.add("pool", coll2, reads=["ag2in"], writes=["ag2"], is_coll=True, tag="coll2")

    s.enabled = True
    s.add("pool", lambda e: e.nop(), writes=["p1done"] + list(s.last_writer.keys()),
          tag="join1")
    p1_es.close()

    p3_es = ExitStack()
    wgT = SB(p3_es, "wgT", (128, 8 * I_FFN), BF16)
    wuT = SB(p3_es, "wuT", (128, 8 * I_FFN), BF16)
    wdT = SB(p3_es, "wdT", (128, 22 * 1024), BF16)
    xqf = [SB(p3_es, f"xqf{p}", (128, 8 * 128), BF16) for p in range(2)]
    hfull = SB(p3_es, "hfull", (128, I_FFN), F32)
    hq = [SB(p3_es, f"hq{p}", (128, I_FFN), BF16) for p in range(2)]
    hqf = [SB(p3_es, f"hqf{p}", (128, 22 * 128), BF16) for p in range(2)]
    hgate = SB(p3_es, "hgate", (128, 512), F32)
    ffn_b = [SB(p3_es, f"ffnb{p}", (128, E), BF16) for p in range(2)]
    f_bmax = SB(p3_es, "f_bmax", (128, 64), F32)
    f_sc = SB(p3_es, "f_sc", (128, 64), F32)
    f_si = SB(p3_es, "f_si", (128, 64), F32)
    f_xa = SB(p3_es, "f_xa", (128, 1024), F32)
    f_t1 = SB(p3_es, "f_t1", (128, 1024), F32)


    s.enabled = phase_limit >= 3
    def ld_wgT(e, gsrc, dst):
        outs = []
        for kt in range(8):
            src = gsrc.rearrange("(r k) i -> k r i", k=E)[128 * kt: 128 * (kt + 1)]
            outs.append(e.dma_start(out=dst[:, I_FFN * kt: I_FFN * (kt + 1)], in_=src))
        return outs
    s.add("sp", lambda e: ld_wgT(e, g_wg, wgT), reads=["p1done", "gwg"],
          writes=["wgT"], n_dma=8, tag="ldwgT")
    s.add("sp", lambda e: ld_wgT(e, g_wu, wuT), reads=["p1done", "gwu"],
          writes=["wuT"], n_dma=8, tag="ldwuT")
    def ld_wdT(e):
        outs = []
        for it in range(22):
            src = g_wd.rearrange("(r i) h -> i r h", i=I_FFN)[128 * it: 128 * (it + 1)]
            outs.append(e.dma_start(out=wdT[:, 1024 * it: 1024 * (it + 1)], in_=src))
        return outs
    s.add("sp", ld_wdT, reads=["p1done", "gwd"], writes=["wdT"],
          n_dma=22, tag="ldwdT")
    IB = [(0, 512), (512, 512), (1024, 512), (1536, 512), (2048, 512), (2560, 256)]

    def p3_tile(i):
        p = i % 2
        def ldxq(e):
            return [e.dma_start(out=xqf[p][:, 128 * kt: 128 * (kt + 1)],
                                in_=xq_s[C * i: C * (i + 1), 128 * kt: 128 * (kt + 1)],
                                transpose=True)
                    for kt in range(8)]
        s.add("sp", ldxq, reads=[f"xqs{i}", "p1done"], writes=[f"xqf{p}"], n_dma=8,
              tag=f"lxq{i}")
        for b, (i0, w) in enumerate(IB):
            gb = (b % 2) * 2
            ub = (b % 2) * 2 + 1
            def gu(e, i0=i0, w=w, gb=gb, ub=ub):
                last = None
                for wsb, bank in ((wgT, gb), (wuT, ub)):
                    for kt in range(8):
                        last = nc.tensor.matmul(
                            BANK(bank, w, 0), xqf[p][:, 128 * kt: 128 * (kt + 1)],
                            wsb[:, I_FFN * kt + i0: I_FFN * kt + i0 + w],
                            start=(kt == 0), stop=(kt == 7))
                return last
            s.add("pe", gu, reads=[f"xqf{p}", "wgT", "wuT", f"psb{gb}", f"psb{ub}"],
                  writes=[f"psb{gb}", f"psb{ub}"], tag=f"gu{i}_{b}", cost=4400)
            def sl(e, i0=i0, w=w, gb=gb):
                return e.activation(hgate[:, 0:w], BANK(gb, w, 0), AF.Silu)
            s.add("act", sl, reads=[f"psb{gb}"], writes=["hgate", f"psb{gb}"],
                  tag=f"sl{i}_{b}", tset="S", cost=700)
            def hm(e, i0=i0, w=w, ub=ub):
                return e.tensor_tensor(hfull[:, i0:i0 + w], hgate[:, 0:w],
                                       BANK(ub, w, 0), ALU.mult)
            s.add("dve", hm, reads=["hgate", f"psb{ub}"],
                  writes=[f"hfb{b}", f"psb{ub}"], tag=f"hm{i}_{b}", cost=700,
                  force=True)
        QCH = [(0, 704, ("hfb0", "hfb1")), (704, 704, ("hfb1", "hfb2")),
               (1408, 704, ("hfb2", "hfb3", "hfb4")), (2112, 704, ("hfb4", "hfb5"))]
        for qc, (i0, w, hdeps) in enumerate(QCH):
            emit_quant2(hfull[:, i0:i0 + w], hq[p][:, i0:i0 + w], 44, w,
                        "qf", f_xa[:, 0:w], f_t1[:, 0:w], f_bmax[:, 0:44],
                        f_sc[:, 0:44], f_si[:, 0:44], list(hdeps), f"hqv{qc}_{p}")
        trh_deps = (("hqv0", "hqv1"), ("hqv1", "hqv2"), ("hqv2", "hqv3"))
        for bi, (bank, it0, itn) in enumerate(((6, 0, 8), (7, 8, 16), (6, 16, 22))):
            def trh_pe(e, bank=bank, it0=it0, itn=itn):
                last = None
                for j, it in enumerate(range(it0, itn)):
                    last = nc.tensor.transpose(
                        P[:, 512 * bank + 64 * j: 512 * bank + 64 * (j + 1)].bitcast(BF16),
                        hq[p][:, 128 * it: 128 * (it + 1)], identb[:])
                return last
            s.add("pe", trh_pe,
                  reads=[f"{d}_{p}" for d in trh_deps[bi]] + ["identb", f"psb{bank}"],
                  writes=[f"psb{bank}"], tag=f"trh{i}_{bi}", cost=2300)
            def evh(e, bank=bank, it0=it0, itn=itn):
                return e.copy(hqf[p][:, 128 * it0: 128 * itn],
                              P[:, 512 * bank: 512 * bank + 64 * (itn - it0)].bitcast(BF16))
            s.add("act", evh, reads=[f"psb{bank}"],
                  writes=[f"hqf{p}", f"psb{bank}"], tag=f"evh{i}_{bi}", cost=900)
        def down(e):
            last = None
            for half, bank in ((0, 4), (1, 5)):
                for it in range(22):
                    last = nc.tensor.matmul(
                        BANK(bank), hqf[p][:, 128 * it: 128 * (it + 1)],
                        wdT[:, 1024 * it + 512 * half: 1024 * it + 512 * (half + 1)],
                        start=(it == 0), stop=(it == 21))
            return last
        s.add("pe", down, reads=[f"hqf{p}", "wdT", "psb4", "psb5"],
              writes=["psb4", "psb5"], tag=f"dn{i}", cost=12000)
        s.add("dve", lambda e: (e.tensor_copy(ffn_b[p][:, 0:512], BANK(4)),
                                e.tensor_copy(ffn_b[p][:, 512:1024], BANK(5)))[-1],
              reads=["psb4", "psb5"], writes=[f"ffnb{p}", "psb4", "psb5"],
              tag=f"evf{i}")
        s.add("pool", lambda e: dma(e, ffn_s[C * i: C * (i + 1), :], ffn_b[p][:]),
              reads=[f"ffnb{p}"], n_dma=1, tag=f"stf{i}")

    for i in range(NT):
        p3_tile(i)

    s.enabled = True
    s.add("pool", lambda e: e.nop(), writes=["p3done"] + list(s.last_writer.keys()),
          tag="join3")
    p3_es.close()

    s.enabled = phase_limit >= 4
    p2_es = ExitStack()
    wout_sb = SB(p2_es, "wout_sb", (128, 8 * 1024), BF16)
    summ_all = SB(p2_es, "summ_all", (128, NCORES * 516), F32)
    Sin = SB(p2_es, "Sin", (128, 512), F32)
    Sin_b = SB(p2_es, "Sinb", (128, 512), BF16)
    Dt = SB(p2_es, "Dt", (128, 4), F32)
    x2 = [SB(p2_es, f"x2_{p}", (128, E), F32) for p in range(2)]
    opre2 = [SB(p2_es, f"opre2_{p}", (128, E), BF16) for p in range(2)]
    qg2 = [SB(p2_es, f"qg2_{p}", (128, 512), BF16) for p in range(2)]
    sg2 = [SB(p2_es, f"sg2_{p}", (128, E), BF16) for p in range(2)]
    ffn2 = [SB(p2_es, f"ffn2_{p}", (128, E), BF16) for p in range(2)]
    ofull = SB(p2_es, "ofull", (128, E), F32)
    onorm = SB(p2_es, "onorm", (128, E), F32)
    sgo = SB(p2_es, "sgo", (128, E), BF16)
    sgof = [SB(p2_es, f"sgof{p}", (128, 8 * 128), BF16) for p in range(2)]
    out_sb = [SB(p2_es, f"out{p}", (128, E), F32) for p in range(2)]
    bn8 = SB(p2_es, "bn8", (128, 48), F32)
    oag = SB(p2_es, "oag", (128, 16), F32)
    orst = SB(p2_es, "orst", (128, 8), F32)
    otmp = SB(p2_es, "otmp", (128, 8), F32)

    s.add("pool", lambda e: cast_w(e, wout_sb[:], wout_d, 1024),
          reads=["p3done"], writes=["woutb"], n_dma=1, tag="ldwout")
    s.add("act", lambda e: dma(e, summ_all.rearrange("p (r f) -> p r f", f=516),
                               ag2_out.rearrange("(r p) f -> p r f", p=128)),
          reads=["ag2", "p3done"], writes=["sall"], n_dma=1, tag="ldsumm")

    s.add("dve", lambda e: e.memset(Sin[:], 0.0), reads=["p3done"],
          writes=["Sin"], tag="sin0")
    for r in range(NCORES - 1):
        base = 516 * r
        def dt_f(e, base=base, r=r):
            e.tensor_scalar_sub(Dt[:], summ_all[:, base + 512: base + 516], 1.0)
            return e.tensor_scalar(Dt[:], Dt[:], msk[:, r: r + 1], 1.0,
                                   ALU.mult, ALU.add)
        s.add("dve", dt_f, reads=["sall"], sreads=["msk"], writes=["Dt"],
              tag=f"dt{r}", force=True)
        def comb(e, base=base, r=r):
            last = None
            for kt in range(4):
                sl = slice(128 * kt, 128 * (kt + 1))
                e.tensor_scalar_mul(onorm[:, sl], summ_all[:, base + 128 * kt:
                                                           base + 128 * (kt + 1)],
                                    msk[:, r: r + 1])
                last = e.scalar_tensor_tensor(Sin[:, sl], Sin[:, sl],
                                              Dt[:, kt: kt + 1], onorm[:, sl],
                                              ALU.mult, ALU.add)
            return last
        s.add("dve", comb, reads=["sall", "Sin"], sreads=["Dt", "msk"],
              writes=["Sin"], tag=f"cb{r}", force=True)
    s.add("dve", lambda e: e.tensor_copy(Sin_b[:], Sin[:]), reads=["Sin"],
          writes=["Sinb"], tag="sinb")

    def p2_tile(i):
        p = i % 2
        def gate(n):
            s.enabled = (phase_limit >= 4) and (p2_parts >= n)
        gate(1)
        s.add("sp", lambda e: dma(e, x2[p][:], x_d[C * i: C * (i + 1), :]),
              reads=["p3done"], writes=[f"x2_{p}"], n_dma=1, tag=f"lx2{i}")
        def lds(e):
            return [dma(e, opre2[p][:], op_s[C * i: C * (i + 1), :]),
                    dma(e, qg2[p][:], qg_s[C * i: C * (i + 1), :]),
                    dma(e, sg2[p][:], sg_s[C * i: C * (i + 1), :]),
                    dma(e, ffn2[p][:], ffn_s[C * i: C * (i + 1), :])]
        s.add("act", lds, reads=["p3done"],
              writes=[f"opre2_{p}", f"qg2_{p}", f"sg2_{p}", f"ffn2_{p}"],
              n_dma=4, tag=f"lds{i}")
        gate(2)
        corr_pos = lambda h: (h % 2, 128 * (h // 2))
        def corr(e):
            last = None
            for h in (0, 2, 4, 6, 1, 3, 5, 7):
                hb = 64 * (h % 2)
                ht = 128 * (h // 2)
                ob, oc = corr_pos(h)
                last = nc.tensor.matmul(BANK(ob, 128, oc),
                                        qg2[p][hb:hb + 64, ht:ht + 128],
                                        Sin_b[hb:hb + 64, ht:ht + 128],
                                        start=True, stop=True)
            return last
        s.add("pe", corr, reads=[f"qg2_{p}", "Sinb", "psb0", "psb1"],
              writes=["psb0", "psb1"], tag=f"cr{i}", cost=1400)
        gate(2.5)
        def of_f(e):
            last = None
            for h in range(H):
                ob, oc = corr_pos(h)
                last = e.tensor_add(ofull[:, 128 * h: 128 * (h + 1)],
                                    opre2[p][:, 128 * h: 128 * (h + 1)],
                                    BANK(ob, 128, oc))
            return last
        s.add("dve", of_f,
              reads=["psb0", "psb1", f"opre2_{p}"],
              writes=["ofull", "psb0", "psb1"], tag=f"of{i}", force=True, cost=1400)
        gate(3)
        def lno(e):
            last = None
            for h in range(H):
                last = e.bn_stats(bn8[:, 6 * h: 6 * (h + 1)],
                                  ofull[:, 128 * h: 128 * (h + 1)])
            return last
        s.add("dve", lno, reads=["ofull"], writes=["bn8"], tag=f"lno{i}", force=True)
        def lna(e):
            last = None
            for h in range(H):
                last = e.bn_aggr(oag[:, 2 * h: 2 * (h + 1)], bn8[:, 6 * h: 6 * (h + 1)])
            return last
        s.add("dve", lna, reads=["bn8"], writes=["oag"], tag=f"lna{i}", force=True)
        gate(4)
        def ors(e):
            e.activation(otmp[:], oag.rearrange("p (h two) -> p h two", two=2)[:, :, 1],
                         AF.Ln, bias=K(0))
            return e.activation(orst[:], otmp[:], AF.Exp, scale=K(2))
        s.add("act", ors, reads=["oag", "konst"], writes=["orst"], tag=f"ors{i}",
              force=True, tset="E", cost=400)
        gate(5)
        def nrm(e):
            last = None
            for h in range(H):
                last = e.tensor_scalar(
                    onorm[:, 128 * h: 128 * (h + 1)],
                    ofull[:, 128 * h: 128 * (h + 1)],
                    oag[:, 2 * h: 2 * h + 1], orst[:, h: h + 1],
                    ALU.subtract, ALU.mult)
            return last
        s.add("dve", nrm, reads=["ofull"], sreads=["oag", "orst"],
              writes=["onorm2"], tag=f"nr{i}")
        s.add("dve", lambda e: e.tensor_mul(sgo[:], onorm[:], sg2[p][:]),
              reads=["onorm2", f"sg2_{p}"], writes=["sgo"], tag=f"sgo{i}")
        gate(6)
        def trs(e):
            last = None
            for kt in range(8):
                last = nc.tensor.transpose(
                    P[:, 64 * kt: 64 * (kt + 1)].bitcast(BF16),
                    sgo[:, 128 * kt: 128 * (kt + 1)], identb[:])
            return last
        s.add("pe", trs, reads=["sgo", "identb", "psb0"], writes=["psb0"],
              tag=f"trs{i}", cost=2300)
        def evs(e):
            return e.copy(sgof[p][:], P[:, 0:512].bitcast(BF16))
        s.add("act", evs, reads=["psb0"], writes=[f"sgof{p}", "psb0"], tag=f"evs{i}")
        gate(7)
        def wo(e):
            last = None
            for half, bank in ((0, 2), (1, 3)):
                for kt in range(8):
                    last = nc.tensor.matmul(
                        BANK(bank), sgof[p][:, 128 * kt: 128 * (kt + 1)],
                        wout_sb[:, kt * 1024 + 512 * half: kt * 1024 + 512 * (half + 1)],
                        start=(kt == 0), stop=(kt == 7))
            return last
        s.add("pe", wo, reads=[f"sgof{p}", "woutb", "psb2", "psb3"],
              writes=["psb2", "psb3"], tag=f"wo{i}", cost=4400)
        gate(8)
        def fin(e):
            e.tensor_add(out_sb[p][:, 0:512], BANK(2), x2[p][:, 0:512])
            e.tensor_add(out_sb[p][:, 512:1024], BANK(3), x2[p][:, 512:1024])
            e.tensor_add(out_sb[p][:, 0:512], out_sb[p][:, 0:512], ffn2[p][:, 0:512])
            return e.tensor_add(out_sb[p][:, 512:1024], out_sb[p][:, 512:1024],
                                ffn2[p][:, 512:1024])
        s.add("dve", fin, reads=["psb2", "psb3", f"x2_{p}", f"ffn2_{p}"],
              writes=[f"out{p}", "psb2", "psb3"], tag=f"fin{i}")
        gate(9)
        s.add("pool", lambda e: dma(e, y_d[C * i: C * (i + 1), :], out_sb[p][:]),
              reads=[f"out{p}"], writes=[f"ydma{i}"], n_dma=1, tag=f"sty{i}")
        s.enabled = phase_limit >= 4

    for i in range(NT):
        p2_tile(i)

    if taps:
        s.enabled = True
        tap_list = [("t_xq", xq_s, [SPAN, E], BF16, "none"),
                    ("t_sg", sg_s, [SPAN, E], BF16, "none"),
                    ("t_op", op_s, [SPAN, E], BF16, "none"),
                    ("t_qg", qg_s, [SPAN, 512], BF16, "none"),
                    ("t_ffn", ffn_s, [SPAN, E], BF16, "none"),
                    ("t_ag2i", ag2_in, [128, 516], F32, "none"),
                    ("t_gwg", g_wg, [NCORES * E, 352], BF16, "none"),
                    ("t_gwd", g_wd, [NCORES * I_FFN, 128], BF16, "none"),
                    ("t_ag2o", ag2_out, [NCORES * 128, 516], F32, "none")]
        for nm, srcd, shp, dt, _ in tap_list:
            td = nc.dram_tensor(nm, shp, dt, kind="ExternalOutput").ap()
            def cp(e, td=td, srcd=srcd):
                return dma(e, td, srcd)
            s.add("pool", cp, reads=["p3done", "ag2"], writes=[nm], n_dma=1, tag=nm)
        t_sin = nc.dram_tensor("t_sin", [128, 512], F32, kind="ExternalOutput").ap()
        s.add("pool", lambda e: dma(e, t_sin, Sin[:]), reads=["Sin"],
              writes=["t_sin"], n_dma=1, tag="t_sin")

    s.enabled = True
    s.add("pool", lambda e: e.nop(),
          reads=([f"ydma{i}" for i in range(NT)] if (phase_limit >= 4 and p2_parts >= 9) else []),
          writes=list(s.last_writer.keys()), tag="fence_out")

    s.finalize()
    with nc.Block() as block:
        s.emit(nc, block)
    p2_es.close()
    main_es.close()
    return nc


_NC_CACHE = None


def kernel(**inputs):
    import ml_dtypes
    BF = ml_dtypes.bfloat16
    F = np.float32
    g = lambda k: np.ascontiguousarray(np.asarray(inputs[k], dtype=F))
    x = g("L0_attn_x").reshape(16384, E)
    conv_w = g("L0_attn_conv_w")
    wq = g("L0_attn_wq"); wkg = g("L0_attn_wkg"); wv = g("L0_attn_wv")
    wg = g("L0_attn_wg"); bg = g("L0_attn_bg"); wout = g("L0_attn_wout")
    aug = g("L0_attn_aug_balance")
    w_gate = g("L0_ffn_w_gate"); w_up = g("L0_ffn_w_up"); w_down = g("L0_ffn_w_down")

    tri = np.triu(np.ones((128, 128), dtype=F))
    hones = np.zeros((128, 2), dtype=BF)
    hones[0:64, 0] = 1; hones[64:128, 1] = 1
    identb = np.eye(128, dtype=BF)
    cw_r = np.ascontiguousarray(conv_w.reshape(8, 128, 4).transpose(1, 0, 2).reshape(128, 32))
    aug_r = np.ascontiguousarray(aug.reshape(4, 128).T)
    bg1 = bg.reshape(1, E)

    global _NC_CACHE
    if _NC_CACHE is None:
        _NC_CACHE = build_program()
    nc = _NC_CACHE

    in_maps = []
    for j in range(NCORES):
        t0 = j * SPAN
        xh = np.zeros((16, E), dtype=F)
        if j > 0:
            xh[13:16] = x[t0 - 3: t0]
        mskj = np.zeros((128, NCORES), dtype=F)
        mskj[:, :j] = 1.0
        in_maps.append(dict(
            x=np.ascontiguousarray(x[t0: t0 + SPAN]),
            xh=xh, msk=mskj, tri=tri, hones=hones, identb=identb,
            cw=cw_r, augr=aug_r, bg=bg1,
            wq=wq, wkg=wkg, wv=wv, wg=wg, wout=wout,
            wgsh=np.ascontiguousarray(w_gate[352 * j: 352 * (j + 1)]),
            wush=np.ascontiguousarray(w_up[352 * j: 352 * (j + 1)]),
            wdsh=np.ascontiguousarray(w_down[128 * j: 128 * (j + 1)]),
        ))
    res = run_bass_kernel_spmd(nc, in_maps, core_ids=list(range(NCORES)))
    out = np.concatenate([np.asarray(r["y"]) for r in res.results], axis=0)
    return out.reshape(1, 16384, E).astype(np.float32)


# version 21
# speedup vs baseline: 1.0156x; 1.0156x over previous
import numpy as np
from contextlib import ExitStack

import concourse.bass as bass
import concourse.mybir as mybir
from concourse.bass_utils import run_bass_kernel_spmd

F32 = mybir.dt.float32
BF16 = mybir.dt.bfloat16
AF = mybir.ActivationFunctionType
ALU = mybir.AluOpType
AX = mybir.AxisListType

NCORES = 8
SPAN = 2048
C = 128
NT = SPAN // C
E = 1024
H = 8
DK = 64
DV = 128
I_FFN = 2816
LN16 = 1.0 / 16.0
LOG0125 = float(np.log(0.125))

ENG_ENGINES = ("sp", "act", "dve", "pe", "pool")
QUEUE_OF_ENGINE = {"sp": "qsp", "act": "qact", "pool": "qpool"}
NSEM_POOL = 16


class _Op:
    __slots__ = ("idx", "engine", "fn", "reads", "writes", "sreads", "n_dma",
                 "tag", "deps", "seq", "is_dma", "is_coll", "force", "queue",
                 "cost", "tset", "order", "prio")

    def __init__(self):
        self.deps = {}


class Sched:

    def __init__(self):
        self.ops = []
        self.last_writer = {}
        self.readers = {}
        self.enabled = True

    def add(self, engine, fn, reads=(), writes=(), sreads=(), n_dma=0,
            tag="", force=False, is_coll=False, cost=None, tset=None):
        if not self.enabled:
            return -1
        op = _Op()
        op.cost = cost if cost is not None else (1500 if n_dma else 500)
        op.tset = tset
        import builtins
        op.prio = getattr(builtins, 'op_prio_base', [0])[0] + len(self.ops)
        op.idx = len(self.ops)
        op.engine = engine
        op.fn = fn
        op.reads = tuple(reads) + tuple(sreads)
        op.sreads = tuple(sreads)
        op.writes = tuple(writes)
        op.n_dma = n_dma
        op.tag = tag
        op.is_dma = n_dma > 0
        op.is_coll = is_coll
        op.force = force
        deps = set()
        for r in op.reads:
            w = self.last_writer.get(r)
            if w is not None:
                deps.add(w)
        for w_ in op.writes:
            pw = self.last_writer.get(w_)
            if pw is not None:
                deps.add(pw)
            for rd in self.readers.get(w_, ()):
                deps.add(rd)
        deps.discard(op.idx)
        op.deps = deps
        for r in op.reads:
            self.readers.setdefault(r, []).append(op.idx)
        for w_ in op.writes:
            self.last_writer[w_] = op.idx
            self.readers[w_] = []
        self.ops.append(op)
        return op.idx

    def schedule(self):
        import heapq
        n = len(self.ops)
        succs = [[] for _ in range(n)]
        ndeps = [0] * n
        for op in self.ops:
            ndeps[op.idx] = len(op.deps)
            for d in op.deps:
                succs[d].append(op.idx)
        ready = {e: [] for e in ENG_ENGINES}
        for op in self.ops:
            if ndeps[op.idx] == 0:
                heapq.heappush(ready[op.engine], (op.prio, op.idx))
        eng_free = {e: 0.0 for e in ENG_ENGINES}
        act_set = [None]
        done_t = [0.0] * n
        events = []
        order = {e: [] for e in ENG_ENGINES}
        scheduled = 0
        while scheduled < n:
            progressed = False
            for e in ENG_ENGINES:
                if not ready[e]:
                    continue
                if e == "act" and act_set[0] is not None:
                    cand = heapq.nsmallest(6, ready[e])
                    pick = None
                    for c in cand:
                        ts = self.ops[c[1]].tset
                        if ts is None or ts == act_set[0]:
                            pick = c
                            break
                    if pick is None:
                        pick = cand[0]
                    ready[e].remove(pick)
                    heapq.heapify(ready[e])
                    pick = pick[1]
                else:
                    pick = heapq.heappop(ready[e])[1]
                op = self.ops[pick]
                start = max(eng_free[e],
                            max((done_t[d] for d in op.deps), default=0.0))
                cost = op.cost
                if e == "act" and op.tset is not None and op.tset != act_set[0]:
                    cost += 1300
                    act_set[0] = op.tset
                fin = start + cost
                eng_free[e] = fin
                done_t[pick] = fin
                order[e].append(pick)
                scheduled += 1
                progressed = True
                for s_ in succs[pick]:
                    ndeps[s_] -= 1
                    if ndeps[s_] == 0:
                        heapq.heappush(ready[self.ops[s_].engine],
                                       (self.ops[s_].prio, s_))
            if not progressed:
                raise RuntimeError("scheduler stuck (cyclic deps?)")
        for e in ENG_ENGINES:
            for pos, oi in enumerate(order[e]):
                self.ops[oi].order = pos
        self.makespan = max(done_t) if n else 0.0

    def finalize(self):
        self.schedule()
        self.eng_seq = {e: [] for e in ENG_ENGINES}
        self.q_count = {}
        self.eng_order = {e: sorted((o for o in self.ops if o.engine == e),
                                    key=lambda o: o.order)
                          for e in ENG_ENGINES}
        qn = {}
        ncoll = 0
        for e in ENG_ENGINES:
            for op in self.eng_order[e]:
                if op.is_dma:
                    q = QUEUE_OF_ENGINE[op.engine]
                    nq = qn.get(q, 0)
                    qn[q] = nq + 1
                    slot = nq % NSEM_POOL
                    key = (q, slot)
                    base = self.q_count.get(key, 0)
                    self.q_count[key] = base + op.n_dma
                    op.queue = (q, slot, (base + op.n_dma) * 16)
                elif op.is_coll:
                    op.queue = ("coll", ncoll, 1)
                    ncoll += 1
                else:
                    op.queue = None
                    self.eng_seq[op.engine].append(op.idx)
                    op.seq = len(self.eng_seq[op.engine])

        done = [False] * len(self.ops)
        heads = {e: 0 for e in ENG_ENGINES}
        streams = self.eng_order
        progress = True
        ndone = 0
        while progress:
            progress = False
            for e in ENG_ENGINES:
                while heads[e] < len(streams[e]):
                    op = streams[e][heads[e]]
                    if all(done[d] for d in op.deps):
                        done[op.idx] = True
                        heads[e] += 1
                        ndone += 1
                        progress = True
                    else:
                        break
        if ndone != len(self.ops):
            stuck = [o for o in self.ops if not done[o.idx]][:5]
            raise RuntimeError("deadlock in op graph; first stuck: " +
                               ", ".join(f"{o.tag}@{o.engine}" for o in stuck))

    def emit(self, nc, block):
        es = ExitStack()
        eng_sem = {e: es.enter_context(nc.semaphore(f"es_{e}")) for e in ENG_ENGINES}
        pool_sems = {}
        for q in ("qsp", "qact", "qpool"):
            pool_sems[q] = [es.enter_context(nc.semaphore(f"{q}_{i}"))
                            for i in range(NSEM_POOL)]
        n_coll = sum(1 for o in self.ops if o.is_coll)
        coll_sems = [es.enter_context(nc.semaphore(f"coll{i}"))
                     for i in range(max(1, n_coll))]
        self._es = es
        sched = self

        def dep_points(op):
            pts = {}
            for d in sorted(op.deps):
                p = sched.ops[d]
                if p.is_dma:
                    q, slot, val = p.queue
                    key = ("q", q, slot)
                    pts[key] = max(pts.get(key, 0), val)
                elif p.is_coll:
                    _, cidx, val = p.queue
                    pts[("coll", cidx)] = max(pts.get(("coll", cidx), 0), val)
                else:
                    key = ("e", p.engine)
                    pts[key] = max(pts.get(key, 0), p.seq)
            return pts

        waited = {e: {} for e in ENG_ENGINES}

        def run_stream(eng_name, eng):
            for op in sched.eng_order[eng_name]:
                for key, val in sorted(dep_points(op).items(), key=str):
                    if waited[eng_name].get(key, 0) >= val:
                        continue
                    if key[0] == "q":
                        sem = pool_sems[key[1]][key[2]]
                    elif key[0] == "coll":
                        sem = coll_sems[key[1]]
                    else:
                        sem = eng_sem[key[1]]
                    eng.wait_ge(sem, val)
                    waited[eng_name][key] = val
                if op.is_dma:
                    q, slot, _ = op.queue
                    insts = op.fn(eng)
                    if not isinstance(insts, (list, tuple)):
                        insts = [insts]
                    assert len(insts) == op.n_dma, f"{op.tag}: n_dma mismatch {len(insts)} != {op.n_dma}"
                    for ins in insts:
                        ins.then_inc(pool_sems[q][slot], 16)
                elif op.is_coll:
                    ins = op.fn(eng)
                    ins.then_inc(coll_sems[op.queue[1]])
                else:
                    ins = op.fn(eng)
                    assert ins is not None, f"{op.tag}: fn returned None"
                    ins.then_inc(eng_sem[op.engine], 1)

        @block.sync
        def _(sync):
            run_stream("sp", sync)

        @block.scalar
        def _(scalar):
            run_stream("act", scalar)

        @block.vector
        def _(vector):
            run_stream("dve", vector)

        @block.tensor
        def _(tensor):
            run_stream("pe", tensor)

        @block.gpsimd
        def _(gpsimd):
            run_stream("pool", gpsimd)


def build_program(phase_limit=4, p2_parts=9, taps=False):
    nc = bass.Bass("TRN2", target_bir_lowering=False)
    s = Sched()
    op_prio_base = [0]
    import builtins
    builtins.op_prio_base = op_prio_base

    din = {}
    def IN(name, shape, dt=F32):
        din[name] = nc.dram_tensor(name, list(shape), dt, kind="ExternalInput").ap()
        return din[name]

    x_d = IN("x", (SPAN, E))
    xh_d = IN("xh", (16, E))
    msk_d = IN("msk", (128, NCORES))
    tri_d = IN("tri", (128, 128))
    hones_d = IN("hones", (128, 2), BF16)
    identb_d = IN("identb", (128, 128), BF16)
    cw_d = IN("cw", (128, 32))
    aug_d = IN("augr", (128, 4))
    bg_d = IN("bg", (1, E))
    wq_d = IN("wq", (E, 512))
    wkg_d = IN("wkg", (E, 512))
    wv_d = IN("wv", (E, E))
    wg_d = IN("wg", (E, E))
    wout_d = IN("wout", (E, E))
    wgsh_d = IN("wgsh", (I_FFN // 8, E))
    wush_d = IN("wush", (I_FFN // 8, E))
    wdsh_d = IN("wdsh", (E // 8, I_FFN))

    y_d = nc.dram_tensor("y", [SPAN, E], F32, kind="ExternalOutput").ap()

    xq_s = nc.dram_tensor("xq_s", [SPAN, E], BF16).ap()
    sg_s = nc.dram_tensor("sg_s", [SPAN, E], BF16).ap()
    op_s = nc.dram_tensor("op_s", [SPAN, E], BF16).ap()
    qg_s = nc.dram_tensor("qg_s", [SPAN, 512], BF16).ap()
    ffn_s = nc.dram_tensor("ffn_s", [SPAN, E], BF16).ap()
    wgt_st = nc.dram_tensor("wgt_st", [E, 352], BF16).ap()
    wut_st = nc.dram_tensor("wut_st", [E, 352], BF16).ap()
    wdt_st = nc.dram_tensor("wdt_st", [I_FFN, 128], BF16).ap()
    g_wg = nc.dram_tensor("g_wg", [NCORES * E, 352], BF16, addr_space="Shared").ap()
    g_wu = nc.dram_tensor("g_wu", [NCORES * E, 352], BF16, addr_space="Shared").ap()
    g_wd = nc.dram_tensor("g_wd", [NCORES * I_FFN, 128], BF16, addr_space="Shared").ap()
    ag2_in = nc.dram_tensor("ag2_in", [128, 516], F32).ap()
    ag2_out = nc.dram_tensor("ag2_out", [NCORES * 128, 516], F32, addr_space="Shared").ap()

    main_es = ExitStack()

    def SB(stack, name, shape, dt):
        return stack.enter_context(nc.sbuf_tensor("s_" + name, list(shape), dt))

    P = main_es.enter_context(nc.psum_tensor("P", [128, 4096], F32))
    def BANK(b, cols=512, off=0):
        return P[:, 512 * b + off: 512 * b + off + cols]

    tri = SB(main_es, "tri", (128, 128), F32)
    hones = SB(main_es, "hones", (128, 2), BF16)
    identb = SB(main_es, "identb", (128, 128), BF16)
    msk = SB(main_es, "msk", (128, NCORES), F32)
    cw = SB(main_es, "cw", (128, 32), F32)
    augr = SB(main_es, "augr", (128, 4), F32)
    bgf = SB(main_es, "bgf", (1, E), F32)
    bgb = SB(main_es, "bgb", (1, E), BF16)
    ones1 = SB(main_es, "ones1", (1, 128), BF16)
    S_sb = SB(main_es, "S_sb", (128, 512), F32)
    S_b16 = SB(main_es, "S_b16", (128, 512), BF16)
    Brun = SB(main_es, "Brun", (128, 4), F32)
    summ = SB(main_es, "summ", (128, 516), F32)
    konst = SB(main_es, "konst", (128, 8), F32)
    qgt2 = SB(main_es, "qgt2", (128, 512), F32)

    def dma(eng, out, in_):
        return eng.dma_start(out=out, in_=in_)

    s.add("sp", lambda e: [dma(e, tri[:], tri_d), dma(e, hones[:], hones_d),
                           dma(e, identb[:], identb_d), dma(e, msk[:], msk_d),
                           dma(e, cw[:], cw_d), dma(e, augr[:], aug_d),
                           dma(e, bgf[:], bg_d)],
          writes=["tri", "hones", "identb", "msk", "cw", "augr", "bgf"],
          n_dma=7, tag="ld_consts")
    s.add("dve", lambda e: e.tensor_copy(bgb[:], bgf[:]), reads=["bgf"],
          writes=["bgb"], tag="bgb")
    s.add("dve", lambda e: e.memset(ones1[:], 1.0), writes=["ones1"], tag="ones1")
    s.add("dve", lambda e: e.memset(S_sb[:], 0.0), writes=["S"], tag="S0")
    s.add("dve", lambda e: e.memset(S_b16[:], 0.0), writes=["Sb16"], tag="Sb0")
    s.add("dve", lambda e: e.memset(Brun[:], 0.0), writes=["Brun"], tag="Br0")
    KV = {0: 1e-5, 1: 1.0, 2: -0.5, 3: LN16, 4: -LN16, 5: LOG0125, 6: -1.0}
    def kmemset(e):
        last = None
        for i, v in KV.items():
            last = e.memset(konst[:, i:i + 1], v)
        return last
    s.add("dve", kmemset, writes=["konst"], tag="konst")
    K = lambda i: konst[:, i:i + 1]

    p1_es = ExitStack()
    wq_sb = SB(p1_es, "wq_sb", (128, 8 * 512), BF16)
    wkg_sb = SB(p1_es, "wkg_sb", (128, 8 * 512), BF16)
    wv_sb = SB(p1_es, "wv_sb", (128, 8 * 1024), BF16)
    wg_sb = SB(p1_es, "wg_sb", (128, 8 * 1024), BF16)

    def cast_w(eng, dst, src, mcols):
        return dma(eng, dst.rearrange("p (kt m) -> p kt m", m=mcols),
                   src.rearrange("(kt p) m -> p kt m", p=128))
    s.add("pool", lambda e: [cast_w(e, wq_sb[:], wq_d, 512),
                             cast_w(e, wkg_sb[:], wkg_d, 512)],
          writes=["wq", "wkg"], n_dma=2, tag="ld_wqkg")
    s.add("pool", lambda e: [cast_w(e, wv_sb[:], wv_d, 1024),
                             cast_w(e, wg_sb[:], wg_d, 1024)],
          writes=["wv", "wg"], n_dma=2, tag="ld_wvg")

    wsh = SB(p1_es, "wsh", (128, 3 * 1024), F32)
    wshq = SB(p1_es, "wshq", (128, 3 * 1024), BF16)
    qt_bmax = SB(p1_es, "qt_bmax", (128, 192), F32)
    qt_sc = SB(p1_es, "qt_sc", (128, 192), F32)
    qt_si = SB(p1_es, "qt_si", (128, 192), F32)
    qt_xa = SB(p1_es, "qt_xa", (128, 3 * 1024), F32)
    qt_t1 = SB(p1_es, "qt_t1", (128, 3 * 1024), F32)
    stg = SB(p1_es, "stg", (128, 352), BF16)

    def _chain_ops(bufset, src_r3, xa, t1, gt2, si, sc, dst_r3, cols, blk,
                   rds, wr, cost_scale=1.0):
        MAGIC = float(2 ** 23)
        cd = cols
        def f_chain(e):
            e.tensor_tensor(src_r3(xa), src_r3(xa), si, ALU.mult)
            xd, td = xa[:, 0:cd], t1[:, 0:cd]
            e.tensor_scalar(td, xd, 2.0, MAGIC, ALU.mult, ALU.add)
            e.tensor_scalar(td, td, MAGIC, 0.5, ALU.subtract, ALU.mult)
            e.tensor_scalar_min(td, td, 2.0)
            e.scalar_tensor_tensor(td, xd, 2.5, td, ALU.is_gt, ALU.add)
            e.scalar_tensor_tensor(td, xd, 3.5, td, ALU.is_gt, ALU.add)
            e.scalar_tensor_tensor(td, xd, 5.0, td, ALU.is_gt, ALU.add)
            return e.scalar_tensor_tensor(td, xd, 5.0, td, ALU.is_gt, ALU.add)
        s.add("dve", f_chain, reads=[bufset + "_xa"], sreads=[bufset + "_si"],
              writes=[bufset + "_t1", bufset + "_xa"], tag=bufset + "q1",
              cost=int((cols + cd * 6.0) * cost_scale + 500))
        if cd < cols:
            def g_chain(e):
                xg, tg, t2g = xa[:, cd:cols], t1[:, cd:cols], gt2[:, 0:cols - cd]
                e.tensor_scalar(tg, xg, 2.0, MAGIC, ALU.mult, ALU.add)
                e.tensor_scalar(tg, tg, MAGIC, 0.5, ALU.subtract, ALU.mult)
                e.tensor_scalar_min(tg, tg, 2.0)
                for thr, rep in ((2.5, 1), (3.5, 1), (5.0, 2)):
                    e.tensor_single_scalar(t2g, xg, thr, ALU.is_gt)
                    for _ in range(rep):
                        e.tensor_add(tg, tg, t2g)
                return e.tensor_copy(gt2[:, 0:1], tg[:, 0:1])
            s.add("pool", g_chain, reads=[bufset + "_xa"],
                  writes=[bufset + "_t1g", bufset + "_gt2"], tag=bufset + "q1g",
                  cost=int((cols - cd) * 11.0 + 800))
        return cd

    def emit_quant(src_ap, dst_ap, nblk, cols, bufset, xa, t1, bmax, sc, si,
                   rd, wr, blk=16):
        rds_ = [rd]
        r3 = lambda ap: ap.rearrange("p (b i) -> p b i", i=blk)
        bca = lambda ap: ap[:, :].unsqueeze(2).to_broadcast([128, nblk, blk])
        def f_scale(e):
            e.tensor_reduce(bmax, r3(src_ap), AX.X, ALU.max, apply_absolute_value=True)
            return e.tensor_scalar(sc, bmax, 1.0 / 6.0, 1e-12, ALU.mult, ALU.max)
        s.add("dve", f_scale, sreads=rds_, writes=[bufset + "_sc"], tag=bufset + "q0",
              force=True, cost=int(cols * 1.4 + 150))
        def f_recip(e):
            return e.reciprocal(si, sc)
        s.add("dve", f_recip, sreads=[bufset + "_sc"], writes=[bufset + "_si"],
              tag=bufset + "qr", cost=300)
        def f_abs(e):
            return e.activation(xa, src_ap, AF.Abs)
        s.add("act", f_abs, reads=rds_, writes=[bufset + "_xa"], tag=bufset + "qa",
              cost=int(cols * 0.9 + 200))
        def xa_r3(ap):
            return ap.rearrange("p (b i) -> p b i", i=blk)
        cd = _chain_ops(bufset, xa_r3, xa, t1, qgt2, bca(si), sc, None, cols, blk,
                        rds_, wr)
        def f_sign(e):
            return e.activation(xa, src_ap, AF.Sign)
        s.add("act", f_sign, reads=rds_, writes=[bufset + "_xa"], tag=bufset + "qs",
              cost=int(cols * 0.9 + 200))
        def f_fin(e):
            e.tensor_mul(t1, t1, xa)
            return e.tensor_tensor(r3(dst_ap), r3(t1), bca(sc), ALU.mult)
        s.add("dve", f_fin, reads=[bufset + "_xa", bufset + "_t1g"],
              sreads=[bufset + "_sc", bufset + "_t1"],
              writes=[wr, bufset + "_t1"], tag=bufset + "q2",
              cost=int(cols * 2.2 + 300))

    def emit_quant2(src_ap, dst_ap, nblk, cols, bufset, xa, t1, bmax, sc, si,
                   rds, wr, blk=16):
        rds_ = rds
        r3 = lambda ap: ap.rearrange("p (b i) -> p b i", i=blk)
        bca = lambda ap: ap[:, :].unsqueeze(2).to_broadcast([128, nblk, blk])
        def f_scale(e):
            e.tensor_reduce(bmax, r3(src_ap), AX.X, ALU.max, apply_absolute_value=True)
            return e.tensor_scalar(sc, bmax, 1.0 / 6.0, 1e-12, ALU.mult, ALU.max)
        s.add("dve", f_scale, sreads=rds_, writes=[bufset + "_sc"], tag=bufset + "q0",
              force=True, cost=int(cols * 1.4 + 150))
        def f_recip(e):
            return e.reciprocal(si, sc)
        s.add("dve", f_recip, sreads=[bufset + "_sc"], writes=[bufset + "_si"],
              tag=bufset + "qr", cost=300)
        def f_abs(e):
            return e.activation(xa, src_ap, AF.Abs)
        s.add("act", f_abs, reads=rds_, writes=[bufset + "_xa"], tag=bufset + "qa",
              cost=int(cols * 0.9 + 200))
        def xa_r3(ap):
            return ap.rearrange("p (b i) -> p b i", i=blk)
        cd = _chain_ops(bufset, xa_r3, xa, t1, qgt2, bca(si), sc, None, cols, blk,
                        rds_, wr)
        def f_sign(e):
            return e.activation(xa, src_ap, AF.Sign)
        s.add("act", f_sign, reads=rds_, writes=[bufset + "_xa"], tag=bufset + "qs",
              cost=int(cols * 0.9 + 200))
        def f_fin(e):
            e.tensor_mul(t1, t1, xa)
            return e.tensor_tensor(r3(dst_ap), r3(t1), bca(sc), ALU.mult)
        s.add("dve", f_fin, reads=[bufset + "_xa", bufset + "_t1g"],
              sreads=[bufset + "_sc", bufset + "_t1"],
              writes=[wr, bufset + "_t1"], tag=bufset + "q2",
              cost=int(cols * 2.2 + 300))

    def prep_gate_like(src_d, stage_d, name):
        s.add("dve", lambda e: e.memset(wsh[:], 0.0), writes=["wsh"], tag=f"{name}z")
        def ld(e):
            return [dma(e, wsh[0:128, 0:1024], src_d[0:128, :]),
                    dma(e, wsh[0:128, 1024:2048], src_d[128:256, :]),
                    dma(e, wsh[0:96, 2048:3072], src_d[256:352, :])]
        s.add("sp", ld, writes=["wsh"], n_dma=3, tag=f"{name}ld")
        emit_quant(wsh[:, 0:3072], wshq[:, 0:3072], 192, 3072, "qw",
                   qt_xa[:, 0:3072], qt_t1[:, 0:3072], qt_bmax[:, 0:192],
                   qt_sc[:, 0:192], qt_si[:, 0:192], "wsh", "wshq")
        pbf = BANK(0, 176, 0).bitcast(BF16)
        for kt in range(8):
            def tr(e, kt=kt):
                last = None
                for pt in range(3):
                    pp = 128 if pt < 2 else 96
                    last = nc.tensor.transpose(
                        pbf[:, 128 * pt: 128 * pt + pp],
                        wshq[0:pp, 1024 * pt + 128 * kt: 1024 * pt + 128 * (kt + 1)],
                        identb[0:pp, 0:pp])
                return last
            s.add("pe", tr, reads=["wshq", "identb", "psb0"], writes=["psb0"],
                  tag=f"{name}t{kt}")
            def ev(e, kt=kt):
                return e.copy(stg[:, 0:352], pbf[:, 0:352])
            s.add("act", ev, reads=["psb0"], writes=["stg", "psb0"], tag=f"{name}e{kt}")
            def st(e, kt=kt, stage_d=stage_d):
                return dma(e, stage_d[128 * kt: 128 * (kt + 1), :], stg[:, 0:352])
            s.add("pool", st, reads=["stg"], writes=[f"{name}st{kt}"], n_dma=1,
                  tag=f"{name}s{kt}")

    op_prio_base[0] = 100000
    prep_gate_like(wgsh_d, wgt_st, "pwg")
    prep_gate_like(wush_d, wut_st, "pwu")

    s.add("sp", lambda e: dma(e, wsh[:, 0:2816], wdsh_d), writes=["wsh"],
          n_dma=1, tag="pwdld")
    emit_quant(wsh[:, 0:2816], wshq[:, 0:2816], 176, 2816, "qw",
               qt_xa[:, 0:2816], qt_t1[:, 0:2816], qt_bmax[:, 0:176],
               qt_sc[:, 0:176], qt_si[:, 0:176], "wsh", "wshq")
    pbf2 = BANK(0, 64, 0).bitcast(BF16)
    for it in range(22):
        def tr(e, it=it):
            return nc.tensor.transpose(pbf2[:, 0:128],
                                       wshq[:, 128 * it: 128 * (it + 1)],
                                       identb[:])
        s.add("pe", tr, reads=["wshq", "identb", "psb0"], writes=["psb0"],
              tag=f"pwdt{it}")
        def ev(e, it=it):
            return e.copy(stg[:, 0:128], pbf2[:, 0:128])
        s.add("act", ev, reads=["psb0"], writes=["stg", "psb0"], tag=f"pwde{it}")
        def st(e, it=it):
            return dma(e, wdt_st[128 * it: 128 * (it + 1), :], stg[:, 0:128])
        s.add("pool", st, reads=["stg"], writes=[f"pwdst{it}"], n_dma=1,
              tag=f"pwds{it}")

    def coll(kind_ins, kind_outs, tag, deps, wres):
        def f(e):
            return e.collective_compute(
                "AllGather", ALU.bypass,
                replica_groups=[list(range(NCORES))],
                ins=[kind_ins.opt()], outs=[kind_outs.opt()])
        s.add("pool", f, reads=deps, writes=[wres], is_coll=True, tag=tag)
    coll(wgt_st, g_wg, "coll_wg", [f"pwgst{k}" for k in range(8)], "gwg")
    coll(wut_st, g_wu, "coll_wu", [f"pwust{k}" for k in range(8)], "gwu")
    coll(wdt_st, g_wd, "coll_wd", [f"pwdst{k}" for k in range(22)], "gwd")
    op_prio_base[0] = 0

    x_sb = [SB(p1_es, f"x{p}", (128, E), F32) for p in range(2)]
    bn6 = [SB(p1_es, "bn6" + str(_p), (128, 12), F32) for _p in range(2)]
    bnag = [SB(p1_es, "bnag" + str(_p), (128, 2), F32) for _p in range(2)]
    rstd = [SB(p1_es, "rstd" + str(_p), (128, 1), F32) for _p in range(2)]
    lntmp = SB(p1_es, "lntmp", (128, 1), F32)
    xn = SB(p1_es, "xn", (128, E), F32)
    xnb = SB(p1_es, "xnb", (128, E), BF16)
    XNF_P = 144
    q_bmax = SB(p1_es, "q_bmax", (128, 64), F32)
    q_sc = SB(p1_es, "q_sc", (128, 64), F32)
    q_si = SB(p1_es, "q_si", (128, 64), F32)
    q_xa = SB(p1_es, "q_xa", (128, E), F32)
    q_t1 = SB(p1_es, "q_t1", (128, E), F32)
    xq_sb = [SB(p1_es, f"xq{p}", (128, E), BF16) for p in range(2)]
    xnf = [SB(p1_es, f"xnf{p}", (128, 8 * XNF_P), BF16) for p in range(2)]
    cacc = SB(p1_es, "cacc", (128, 8 * 128), F32)
    xc = [SB(p1_es, "xc" + str(_p), (128, 8 * 128), BF16) for _p in range(2)]
    q_fm = SB(p1_es, "q_fm", (128, 512), F32)
    kg_fm = SB(p1_es, "kg_fm", (128, 512), F32)
    lsb = SB(p1_es, "lsb", (128, 512), F32)
    bcum = SB(p1_es, "bcum", (128, 512), F32)
    epos = SB(p1_es, "epos", (128, 512), F32)
    eneg = SB(p1_es, "eneg", (128, 512), F32)
    ekb = SB(p1_es, "ekb", (128, 512), F32)
    kfp = SB(p1_es, "kfp", (128, 512), F32)
    ktf = SB(p1_es, "ktf", (128, 512), F32)
    qt_b = SB(p1_es, "qt_b", (128, 512), BF16)
    kt_b = [SB(p1_es, "kt_b" + str(_p), (128, 512), BF16) for _p in range(2)]
    kh_b = [SB(p1_es, "kh_b" + str(_p), (128, 512), BF16) for _p in range(2)]
    khn = [SB(p1_es, f"khn{p}", (128, 512), BF16) for p in range(2)]
    qg_b = [SB(p1_es, f"qg{p}", (128, 512), BF16) for p in range(2)]
    ebt = [SB(p1_es, "ebt" + str(_p), (128, 4), F32) for _p in range(2)]
    ebrun = [SB(p1_es, "ebrun" + str(_p), (128, 4), F32) for _p in range(2)]
    m2b = SB(p1_es, "m2b", (128, 512), BF16)
    m1f = SB(p1_es, "m1f", (128, 512), F32)
    augw = [SB(p1_es, f"augw{p}", (128, 8), F32) for p in range(2)]
    v_nat = [SB(p1_es, f"vnat{p}", (128, E), BF16) for p in range(2)]
    g_raw = [SB(p1_es, f"graw{p}", (128, E), F32) for p in range(2)]
    o_acc = [SB(p1_es, f"oacc{p}", (128, E), F32) for p in range(2)]
    opre_b = [SB(p1_es, f"opre{p}", (128, E), BF16) for p in range(2)]
    sg_b = [SB(p1_es, f"sgb{p}", (128, E), BF16) for p in range(2)]
    atm = [SB(p1_es, f"atm{p}", (128, 128), BF16) for p in range(2)]
    augarg = SB(p1_es, "augarg", (128, E), BF16)

    s.add("sp", lambda e: dma(e, x_sb[0][0:16, :], xh_d), writes=["x0"],
          n_dma=1, tag="ld_xh")
    s.add("dve", lambda e: (e.bn_stats(bn6[1][0:16, 0:6], x_sb[0][0:16, 0:512]),
                            e.bn_stats(bn6[1][0:16, 6:12], x_sb[0][0:16, 512:1024]))[-1],
          reads=["x0"], writes=["bn61"], tag="h_bn", force=True)
    s.add("dve", lambda e: e.bn_aggr(bnag[1][0:16, :], bn6[1][0:16, :]),
          reads=["bn61"], writes=["bnag1"], tag="h_ag", force=True)
    s.add("act", lambda e: (e.activation(lntmp[0:16], bnag[1][0:16, 1:2], AF.Ln, bias=konst[0:16, 0:1]),
                            e.activation(rstd[1][0:16], lntmp[0:16], AF.Exp, scale=konst[0:16, 2:3]))[-1],
          reads=["bnag1", "konst"], writes=["rstd1"], tag="h_rstd", force=True,
          tset="E", cost=600)
    s.add("dve", lambda e: e.tensor_scalar(xn[0:16, :], x_sb[0][0:16, :],
                                           bnag[1][0:16, 0:1], rstd[1][0:16, :],
                                           ALU.subtract, ALU.mult),
          reads=["x0"], sreads=["bnag1", "rstd1"], writes=["xn"], tag="h_xn")
    s.add("dve", lambda e: e.tensor_copy(xnb[0:16, :], xn[0:16, :]),
          reads=["xn"], writes=["xnb"], tag="h_xnb")
    def h_tr(e):
        last = None
        for kt in range(8):
            last = nc.tensor.transpose(
                P[:, 64 * kt + 8 * kt: 64 * kt + 8 * kt + 8].bitcast(BF16),
                xnb[0:16, 128 * kt: 128 * (kt + 1)], identb[0:16, 0:16])
        return last
    s.add("pe", h_tr, reads=["xnb", "identb", "psb0", "psb1"],
          writes=["psb0", "psb1"], tag="h_tr")
    def h_ev(e):
        last = None
        for kt in range(8):
            last = e.copy(xnf[1][:, XNF_P * kt: XNF_P * kt + 16],
                          P[:, 72 * kt: 72 * kt + 8].bitcast(BF16))
        return last
    s.add("act", h_ev, reads=["psb0", "psb1"], writes=["xnfd1", "psb0", "psb1"],
          tag="h_ev")
    def h_cp(e):
        src = xnf[1].rearrange("p (k c) -> p k c", c=XNF_P)
        return e.tensor_copy(src[:, :, 141:144], src[:, :, 13:16])
    s.add("dve", h_cp, reads=["xnfd1"], writes=["xnfh1", "xnfd1"], tag="h_cp")

    def p1_tile(i, bn6=bn6, bnag=bnag, rstd=rstd,
                xc=xc,
                kt_b=kt_b, kh_b=kh_b, ebt=ebt, ebrun=ebrun):
        p = i % 2
        pp = 1 - p
        bn6 = bn6[p]; bnag = bnag[p]; rstd = rstd[p]
        xc = xc[p]
        kt_b = kt_b[p]; kh_b = kh_b[p]
        ebt = ebt[p]; ebrun = ebrun[p]
        xr = lambda buf: buf.rearrange("p (k c) -> p k c", c=XNF_P)

        s.add("sp", lambda e: dma(e, x_sb[p][:], x_d[C * i: C * (i + 1), :]),
              writes=[f"x{p}"], n_dma=1, tag=f"ldx{i}")
        s.add("dve", lambda e: (e.bn_stats(bn6[:, 0:6], x_sb[p][:, 0:512]),
                                e.bn_stats(bn6[:, 6:12], x_sb[p][:, 512:1024]))[-1],
              reads=[f"x{p}"], writes=["bn6" + str(p)], tag=f"bn{i}", force=True)
        s.add("dve", lambda e: e.bn_aggr(bnag[:], bn6[:]),
              reads=["bn6" + str(p)], writes=["bnag" + str(p)], tag=f"ag{i}", force=True)
        s.add("act", lambda e: (e.activation(lntmp[:], bnag[:, 1:2], AF.Ln, bias=K(0)),
                                e.activation(rstd[:], lntmp[:], AF.Exp, scale=K(2)))[-1],
              reads=["bnag" + str(p), "konst"], writes=["rstd" + str(p)], tag=f"rstd{i}", force=True,
              tset="E", cost=600)
        s.add("dve", lambda e: e.tensor_scalar(xn[:], x_sb[p][:], bnag[:, 0:1],
                                               rstd[:], ALU.subtract, ALU.mult),
              reads=[f"x{p}"], sreads=["bnag" + str(p), "rstd" + str(p)], writes=["xn"], tag=f"xn{i}")
        s.add("dve", lambda e: e.tensor_copy(xnb[:], xn[:]),
              reads=["xn"], writes=["xnb"], tag=f"xnb{i}")
        def trx(e):
            last = None
            for kt in range(8):
                last = nc.tensor.transpose(
                    P[:, 64 * kt: 64 * (kt + 1)].bitcast(BF16),
                    xnb[:, 128 * kt: 128 * (kt + 1)], identb[:])
            return last
        s.add("pe", trx, reads=["xnb", "identb", "psb0"], writes=["psb0"],
              tag=f"trx{i}", cost=2300)
        def evx(e):
            return e.copy(xr(xnf[p])[:, :, 16:144],
                          P[:, 0:512].bitcast(BF16).rearrange("q (k c) -> q k c", c=128))
        s.add("act", evx, reads=["psb0"], writes=[f"xnfd{p}", "psb0"], tag=f"evx{i}")
        def hcp(e):
            return e.tensor_copy(xr(xnf[p])[:, :, 13:16], xr(xnf[pp])[:, :, 141:144])
        s.add("dve", hcp, reads=[f"xnfd{pp}", f"xnfh{pp}"],
              writes=[f"xnfh{p}"], tag=f"hcp{i}")

        emit_quant(xn[:], xq_sb[p][:], 64, E, "qx",
                   q_xa[:], q_t1[:], q_bmax[:], q_sc[:], q_si[:],
                   "xn", f"xqv{p}")
        s.add("pool", lambda e: dma(e, xq_s[C * i: C * (i + 1), :], xq_sb[p][:]),
              reads=[f"xqv{p}"], writes=[f"xqs{i}"], n_dma=1, tag=f"stxq{i}")

        def conv(e):
            v = xr(xnf[p])
            last = None
            for kt in range(8):
                e.tensor_scalar_mul(cacc[:, 128 * kt: 128 * (kt + 1)],
                                    v[:, kt, 13:141], cw[:, 4 * kt: 4 * kt + 1])
                for d in range(1, 4):
                    last = e.scalar_tensor_tensor(
                        cacc[:, 128 * kt: 128 * (kt + 1)],
                        v[:, kt, 13 + d: 141 + d], cw[:, 4 * kt + d: 4 * kt + d + 1],
                        cacc[:, 128 * kt: 128 * (kt + 1)], ALU.mult, ALU.add)
            return last
        s.add("dve", conv, reads=[f"xnfd{p}", f"xnfh{p}"], sreads=["cw"],
              writes=["cacc"], tag=f"cv{i}", cost=6500)
        s.add("act", lambda e: e.activation(xc[:], cacc[:], AF.Silu),
              reads=["cacc"], writes=["xc" + str(p)], tag=f"cs{i}", tset="S", cost=1100)

        if i > 0:
            emit_aug_sgg(i - 1, pp)

        def qkg(e, bank, wsb, wname):
            last = None
            for mt in range(4):
                for kt in range(8):
                    last = nc.tensor.matmul(
                        BANK(bank, 128, 128 * mt),
                        wsb[:, kt * 512 + mt * 128: kt * 512 + (mt + 1) * 128],
                        xc[:, 128 * kt: 128 * (kt + 1)],
                        start=(kt == 0), stop=(kt == 7))
            return last
        s.add("pe", lambda e: qkg(e, 0, wq_sb, "wq"), reads=["xc" + str(p), "wq"],
              writes=["psb0"], tag=f"qk0_{i}", cost=3800)
        s.add("pe", lambda e: qkg(e, 1, wkg_sb, "wkg"), reads=["xc" + str(p), "wkg"],
              writes=["psb1"], tag=f"qk1_{i}", cost=3800)
        s.add("act", lambda e: (e.copy(q_fm[:], BANK(0)),
                                e.copy(kg_fm[:], BANK(1)))[-1],
              reads=["psb0", "psb1"], writes=["qfm", "kgfm", "psb0", "psb1"],
              tag=f"evqk{i}")

        def vg(e):
            last = None
            for half, bank in ((0, 2), (1, 3)):
                for kt in range(8):
                    last = nc.tensor.matmul(
                        BANK(bank), xc[:, 128 * kt: 128 * (kt + 1)],
                        wv_sb[:, kt * 1024 + 512 * half: kt * 1024 + 512 * (half + 1)],
                        start=(kt == 0), stop=(kt == 7))
            return last
        def gg(e):
            last = None
            for half, bank in ((0, 4), (1, 5)):
                for kt in range(8):
                    nc.tensor.matmul(
                        BANK(bank), xc[:, 128 * kt: 128 * (kt + 1)],
                        wg_sb[:, kt * 1024 + 512 * half: kt * 1024 + 512 * (half + 1)],
                        start=(kt == 0), stop=False)
                last = nc.tensor.matmul(
                    BANK(bank), ones1[:], bgb[:, 512 * half: 512 * (half + 1)],
                    start=False, stop=True)
            return last
        s.add("pe", vg, reads=["xc" + str(p), "wv"], writes=["psb2", "psb3"], tag=f"v{i}",
              cost=4400)
        s.add("pe", gg, reads=["xc" + str(p), "wg", "ones1", "bgb"],
              writes=["psb4", "psb5"], tag=f"g{i}", cost=4800)
        s.add("act", lambda e: (e.copy(v_nat[p][:, 0:512], BANK(2)),
                                e.copy(v_nat[p][:, 512:1024], BANK(3)))[-1],
              reads=["psb2", "psb3"], writes=[f"vnat{p}", "psb2", "psb3"],
              tag=f"evv{i}")
        s.add("act", lambda e: (e.copy(g_raw[p][:, 0:512], BANK(4)),
                                e.copy(g_raw[p][:, 512:1024], BANK(5)))[-1],
              reads=["psb4", "psb5"], writes=[f"graw{p}", "psb4", "psb5"],
              tag=f"evg{i}")

        s.add("act", lambda e: e.activation(ktf[:], kg_fm[:], AF.Abs),
              reads=["kgfm"], writes=["kgabs"], tag=f"ka{i}", cost=600)
        s.add("act", lambda e: (e.activation(epos[:], ktf[:], AF.Exp, scale=K(6)),
                                e.activation(lsb[:], epos[:], AF.Ln, bias=K(1)))[-1],
              reads=["kgabs", "konst"], writes=["lntm", "epos"], tag=f"lse{i}",
              tset="E", cost=1100)
        s.add("dve", lambda e: (e.tensor_single_scalar(eneg[:], kg_fm[:], 0.0, ALU.min),
                                e.tensor_sub(lsb[:], eneg[:], lsb[:]))[-1],
              reads=["kgfm", "lntm"], writes=["ls", "eneg"], tag=f"ls{i}")
        def scan(e):
            last = None
            for kt in range(4):
                sl = slice(128 * kt, 128 * (kt + 1))
                last = e.tensor_tensor_scan(bcum[:, sl], lsb[:, sl], lsb[:, sl],
                                            0.0, ALU.add, ALU.bypass)
            return last
        s.add("dve", scan, sreads=["ls"], writes=["bcum"], tag=f"sc{i}")
        def exps(e):
            e.activation(epos[:], bcum[:], AF.Exp, scale=K(3), bias=K(5))
            e.activation(eneg[:], bcum[:], AF.Exp, scale=K(4))
            e.activation(ekb[:], lsb[:], AF.Exp, scale=K(3))
            bc = bcum.rearrange("p (k c) -> p k c", c=128)
            e.activation(ebt[:], bc[:, :, 127:128], AF.Exp, scale=K(3))
            return e.activation(ebrun[:], Brun[:], AF.Exp, scale=K(3))
        s.add("act", exps, reads=["bcum", "ls", "Brun", "konst"],
              writes=["epos", "eneg", "ek", "ebt" + str(p), "ebrun" + str(p)], tag=f"ex{i}",
              tset="E", cost=2200)
        def kqt(e):
            e.tensor_scalar(kfp[:], ekb[:], -1.0, 1.0, ALU.mult, ALU.add)
            e.tensor_mul(qt_b[:], q_fm[:], epos[:])
            e.tensor_mul(ktf[:], kfp[:], eneg[:])
            e.tensor_copy(kt_b[:], ktf[:])
            eb = ebt[:, :].unsqueeze(2).to_broadcast([128, 4, 128])
            r4 = lambda ap: ap.rearrange("p (k c) -> p k c", c=128)
            return e.tensor_tensor(r4(kh_b[:]), r4(ktf[:]), eb, ALU.mult)
        s.add("dve", kqt, reads=["ek", "epos", "eneg", "qfm"], sreads=["ebt" + str(p)],
              writes=["kfp", "qtb", "ktb" + str(p), "khb" + str(p), "kgabs"], tag=f"kq{i}")
        def qg(e):
            eb = ebrun[:, :].unsqueeze(2).to_broadcast([128, 4, 128])
            r4 = lambda ap: ap.rearrange("p (k c) -> p k c", c=128)
            return e.tensor_tensor(r4(qg_b[p][:]), r4(qt_b[:]), eb, ALU.mult)
        s.add("dve", qg, reads=["qtb"], sreads=["ebrun" + str(p)], writes=[f"qgb{p}"],
              tag=f"qg{i}")
        s.add("pool", lambda e: dma(e, qg_s[C * i: C * (i + 1), :], qg_b[p][:]),
              reads=[f"qgb{p}"], n_dma=1, tag=f"stqg{i}")
        s.add("dve", lambda e: e.tensor_add(Brun[:],
                                            Brun[:],
                                            bcum.rearrange("p (k c) -> p k c", c=128)[:, :, 127]),
              reads=["bcum", "ebrun" + str(p)], writes=["Brun"], tag=f"br{i}")
        def trk(e):
            last = None
            for kt in range(4):
                last = nc.tensor.transpose(
                    P[:, 512 + 64 * kt: 512 + 64 * (kt + 1)].bitcast(BF16),
                    kh_b[:, 128 * kt: 128 * (kt + 1)], identb[:])
            return last
        s.add("pe", trk, reads=["khb" + str(p), "identb", "psb1"], writes=["psb1"],
              tag=f"trk{i}", cost=1200)
        def evk(e):
            return e.copy(khn[p][:], P[:, 512:768].bitcast(BF16))
        s.add("act", evk, reads=["psb1"], writes=[f"khn{p}", "psb1"], tag=f"evk{i}")

        s.add("dve", lambda e: e.tensor_copy(S_b16[:], S_sb[:]),
              reads=["S"], writes=["Sb16"], tag=f"sb16{i}")

        def m2(e):
            e.tensor_mul(m1f[:], q_fm[:], kfp[:])
            ab = augr[:, :].unsqueeze(2).to_broadcast([128, 4, 128])
            r4 = lambda ap: ap.rearrange("p (k c) -> p k c", c=128)
            return e.tensor_tensor(r4(m2b[:]), r4(m1f[:]), ab, ALU.mult)
        s.add("dve", m2, reads=["qfm", "kfp"], sreads=["augr"], writes=["m2b"],
              tag=f"m2{i}")
        def augmm(e):
            last = None
            for kt in range(4):
                last = nc.tensor.matmul(BANK(3, 2, 2 * kt),
                                        m2b[:, 128 * kt: 128 * (kt + 1)],
                                        hones[:], start=True, stop=True)
            return last
        s.add("pe", augmm, reads=["m2b", "hones", "psb3"], writes=["psb3"],
              tag=f"au{i}")
        s.add("dve", lambda e: e.tensor_copy(augw[p][:], BANK(3, 8, 0)),
              reads=["psb3"], writes=[f"augw{p}", "psb3"], tag=f"evau{i}")

        for h in range(H):
            hb = 64 * (h % 2)
            ht = 128 * (h // 2)
            ab = 4 + (h % 2)
            ob = 6 + (h // 4)
            oc = 128 * (h % 4)
            def at(e, h=h, hb=hb, ht=ht, ab=ab):
                return nc.tensor.matmul(BANK(ab, 128, 0),
                                        kt_b[hb:hb + 64, ht:ht + 128],
                                        qt_b[hb:hb + 64, ht:ht + 128],
                                        start=True, stop=True)
            s.add("pe", at, reads=["ktb" + str(p), "qtb", f"psb{ab}"], writes=[f"psb{ab}"],
                  tag=f"at{i}_{h}")
            def mask(e, h=h, ab=ab):
                return e.tensor_tensor(atm[h % 2][:], BANK(ab, 128, 0), tri[:],
                                       ALU.mult)
            s.add("dve", mask, reads=[f"psb{ab}", "tri"],
                  writes=[f"atm{h % 2}", f"psb{ab}"], tag=f"mk{i}_{h}")
            def omm(e, h=h, hb=hb, ht=ht, ob=ob, oc=oc):
                nc.tensor.matmul(BANK(ob, 128, oc), atm[h % 2][:],
                                 v_nat[p][:, 128 * h: 128 * (h + 1)],
                                 start=True, stop=False)
                return nc.tensor.matmul(BANK(ob, 128, oc),
                                        qt_b[hb:hb + 64, ht:ht + 128],
                                        S_b16[hb:hb + 64, ht:ht + 128],
                                        start=False, stop=True)
            s.add("pe", omm, reads=[f"atm{h % 2}", f"vnat{p}", "qtb", "Sb16",
                                    f"psb{ob}"],
                  writes=[f"psb{ob}"], tag=f"o{i}_{h}")
            def umm(e, h=h, hb=hb, ht=ht):
                return nc.tensor.matmul(
                    P[hb:hb + 64, 1024 + ht: 1024 + ht + 128],
                    khn[p][:, 64 * h: 64 * (h + 1)],
                    v_nat[p][:, 128 * h: 128 * (h + 1)],
                    start=True, stop=True)
            s.add("pe", umm, reads=[f"khn{p}", f"vnat{p}", "psb2"],
                  writes=["psb2"], tag=f"u{i}_{h}")

        def supd(e):
            last = None
            for kt in range(4):
                sl = slice(128 * kt, 128 * (kt + 1))
                last = e.scalar_tensor_tensor(S_sb[:, sl], S_sb[:, sl],
                                              ebt[:, kt: kt + 1],
                                              BANK(2, 128, 128 * kt),
                                              ALU.mult, ALU.add)
            return last
        s.add("dve", supd, reads=["psb2", "S", "Sb16"], sreads=["ebt" + str(p)],
              writes=["S", "psb2"], tag=f"su{i}")

        s.add("act", lambda e: (e.copy(o_acc[p][:, 0:512], BANK(6)),
                                e.copy(o_acc[p][:, 512:1024], BANK(7)))[-1],
              reads=["psb6", "psb7"], writes=[f"oacc{p}", "psb6", "psb7"],
              tag=f"evo{i}")

    def emit_aug_sgg(j, pj):
        def arg(e):
            last = None
            for h in range(H):
                last = e.tensor_scalar_mul(augarg[:, 128 * h: 128 * (h + 1)],
                                           v_nat[pj][:, 128 * h: 128 * (h + 1)],
                                           augw[pj][:, h: h + 1])
            return last
        s.add("dve", arg, reads=[f"vnat{pj}"], sreads=[f"augw{pj}"],
              writes=["augarg"], tag=f"aa{j}")
        s.add("act", lambda e: e.activation(augarg[:], augarg[:], AF.Tanh, scale=0.5),
              reads=["augarg"], writes=["augsig"], tag=f"as{j}", tset="S", cost=1100)
        def oad(e):
            e.tensor_scalar(augarg[:], augarg[:], 0.5, 0.5, ALU.mult, ALU.add)
            return e.tensor_add(opre_b[pj][:], o_acc[pj][:], augarg[:])
        s.add("dve", oad,
              reads=["augsig", f"oacc{pj}"], writes=[f"opre{pj}", "augsig", "augarg"],
              tag=f"oad{j}")
        s.add("pool", lambda e: dma(e, op_s[C * j: C * (j + 1), :], opre_b[pj][:]),
              reads=[f"opre{pj}"], n_dma=1, tag=f"stop{j}")
        s.add("act", lambda e: e.activation(sg_b[pj][:], g_raw[pj][:], AF.Silu),
              reads=[f"graw{pj}"], writes=[f"sgb{pj}"], tag=f"sg{j}", tset="S",
              cost=1100)
        s.add("pool", lambda e: dma(e, sg_s[C * j: C * (j + 1), :], sg_b[pj][:]),
              reads=[f"sgb{pj}"], n_dma=1, tag=f"stsg{j}")

    s.enabled = phase_limit >= 2
    for i in range(NT):
        p1_tile(i)
    emit_aug_sgg(NT - 1, (NT - 1) % 2)

    s.add("act", lambda e: e.activation(summ[:, 512:516], Brun[:], AF.Exp, scale=K(3)),
          reads=["Brun", "konst"], writes=["summD"], tag="dspan", tset="E", cost=300)
    s.add("dve", lambda e: e.tensor_copy(summ[:, 0:512], S_sb[:]),
          reads=["S"], writes=["summS"], tag="packS")
    s.add("pool", lambda e: dma(e, ag2_in, summ[:]),
          reads=["summD", "summS"], writes=["ag2in"], n_dma=1, tag="st_summ")
    def coll2(e):
        return e.collective_compute("AllGather", ALU.bypass,
                                    replica_groups=[list(range(NCORES))],
                                    ins=[ag2_in.opt()], outs=[ag2_out.opt()])
    s.add("pool", coll2, reads=["ag2in"], writes=["ag2"], is_coll=True, tag="coll2")

    s.enabled = True
    s.add("pool", lambda e: e.nop(), writes=["p1done"] + list(s.last_writer.keys()),
          tag="join1")
    p1_es.close()

    p3_es = ExitStack()
    wgT = SB(p3_es, "wgT", (128, 8 * I_FFN), BF16)
    wuT = SB(p3_es, "wuT", (128, 8 * I_FFN), BF16)
    wdT = SB(p3_es, "wdT", (128, 22 * 1024), BF16)
    xqf = [SB(p3_es, f"xqf{p}", (128, 8 * 128), BF16) for p in range(2)]
    hfull = SB(p3_es, "hfull", (128, I_FFN), F32)
    hq = [SB(p3_es, f"hq{p}", (128, I_FFN), BF16) for p in range(2)]
    hqf = [SB(p3_es, f"hqf{p}", (128, 22 * 128), BF16) for p in range(2)]
    hgate = SB(p3_es, "hgate", (128, 512), F32)
    ffn_b = [SB(p3_es, f"ffnb{p}", (128, E), BF16) for p in range(2)]
    f_bmax = [SB(p3_es, f"f_bmax{j}", (128, 64), F32) for j in range(2)]
    f_sc = [SB(p3_es, f"f_sc{j}", (128, 64), F32) for j in range(2)]
    f_si = [SB(p3_es, f"f_si{j}", (128, 64), F32) for j in range(2)]
    f_xa = [SB(p3_es, f"f_xa{j}", (128, 704), F32) for j in range(2)]
    f_t1 = [SB(p3_es, f"f_t1{j}", (128, 704), F32) for j in range(2)]


    s.enabled = phase_limit >= 3
    def ld_wgT(e, gsrc, dst):
        outs = []
        for kt in range(8):
            src = gsrc.rearrange("(r k) i -> k r i", k=E)[128 * kt: 128 * (kt + 1)]
            outs.append(e.dma_start(out=dst[:, I_FFN * kt: I_FFN * (kt + 1)], in_=src))
        return outs
    s.add("sp", lambda e: ld_wgT(e, g_wg, wgT), reads=["p1done", "gwg"],
          writes=["wgT"], n_dma=8, tag="ldwgT")
    s.add("sp", lambda e: ld_wgT(e, g_wu, wuT), reads=["p1done", "gwu"],
          writes=["wuT"], n_dma=8, tag="ldwuT")
    def ld_wdT(e):
        outs = []
        for it in range(22):
            src = g_wd.rearrange("(r i) h -> i r h", i=I_FFN)[128 * it: 128 * (it + 1)]
            outs.append(e.dma_start(out=wdT[:, 1024 * it: 1024 * (it + 1)], in_=src))
        return outs
    s.add("sp", ld_wdT, reads=["p1done", "gwd"], writes=["wdT"],
          n_dma=22, tag="ldwdT")
    IB = [(0, 512), (512, 512), (1024, 512), (1536, 512), (2048, 512), (2560, 256)]

    def p3_tile(i):
        p = i % 2
        def ldxq(e):
            return [e.dma_start(out=xqf[p][:, 128 * kt: 128 * (kt + 1)],
                                in_=xq_s[C * i: C * (i + 1), 128 * kt: 128 * (kt + 1)],
                                transpose=True)
                    for kt in range(8)]
        s.add("sp", ldxq, reads=[f"xqs{i}", "p1done"], writes=[f"xqf{p}"], n_dma=8,
              tag=f"lxq{i}")
        for b, (i0, w) in enumerate(IB):
            gb = (b % 2) * 2
            ub = (b % 2) * 2 + 1
            def gu(e, i0=i0, w=w, gb=gb, ub=ub):
                last = None
                for wsb, bank in ((wgT, gb), (wuT, ub)):
                    for kt in range(8):
                        last = nc.tensor.matmul(
                            BANK(bank, w, 0), xqf[p][:, 128 * kt: 128 * (kt + 1)],
                            wsb[:, I_FFN * kt + i0: I_FFN * kt + i0 + w],
                            start=(kt == 0), stop=(kt == 7))
                return last
            s.add("pe", gu, reads=[f"xqf{p}", "wgT", "wuT", f"psb{gb}", f"psb{ub}"],
                  writes=[f"psb{gb}", f"psb{ub}"], tag=f"gu{i}_{b}", cost=4400)
            def sl(e, i0=i0, w=w, gb=gb):
                return e.activation(hgate[:, 0:w], BANK(gb, w, 0), AF.Silu)
            s.add("act", sl, reads=[f"psb{gb}"], writes=["hgate", f"psb{gb}"],
                  tag=f"sl{i}_{b}", tset="S", cost=700)
            def hm(e, i0=i0, w=w, ub=ub):
                return e.tensor_tensor(hfull[:, i0:i0 + w], hgate[:, 0:w],
                                       BANK(ub, w, 0), ALU.mult)
            s.add("dve", hm, reads=["hgate", f"psb{ub}"],
                  writes=[f"hfb{b}", f"psb{ub}"], tag=f"hm{i}_{b}", cost=700,
                  force=True)
        QCH = [(0, 704, ("hfb0", "hfb1")), (704, 704, ("hfb1", "hfb2")),
               (1408, 704, ("hfb2", "hfb3", "hfb4")), (2112, 704, ("hfb4", "hfb5"))]
        for qc, (i0, w, hdeps) in enumerate(QCH):
            q_ = qc % 2
            emit_quant2(hfull[:, i0:i0 + w], hq[p][:, i0:i0 + w], 44, w,
                        f"qf{q_}", f_xa[q_][:, 0:w], f_t1[q_][:, 0:w],
                        f_bmax[q_][:, 0:44], f_sc[q_][:, 0:44],
                        f_si[q_][:, 0:44], list(hdeps), f"hqv{qc}_{p}")
        trh_deps = (("hqv0", "hqv1"), ("hqv1", "hqv2"), ("hqv2", "hqv3"))
        for bi, (bank, it0, itn) in enumerate(((6, 0, 8), (7, 8, 16), (6, 16, 22))):
            def trh_pe(e, bank=bank, it0=it0, itn=itn):
                last = None
                for j, it in enumerate(range(it0, itn)):
                    last = nc.tensor.transpose(
                        P[:, 512 * bank + 64 * j: 512 * bank + 64 * (j + 1)].bitcast(BF16),
                        hq[p][:, 128 * it: 128 * (it + 1)], identb[:])
                return last
            s.add("pe", trh_pe,
                  reads=[f"{d}_{p}" for d in trh_deps[bi]] + ["identb", f"psb{bank}"],
                  writes=[f"psb{bank}"], tag=f"trh{i}_{bi}", cost=2300)
            def evh(e, bank=bank, it0=it0, itn=itn):
                return e.copy(hqf[p][:, 128 * it0: 128 * itn],
                              P[:, 512 * bank: 512 * bank + 64 * (itn - it0)].bitcast(BF16))
            s.add("act", evh, reads=[f"psb{bank}"],
                  writes=[f"hqf{p}", f"psb{bank}"], tag=f"evh{i}_{bi}", cost=900)
        def down(e):
            last = None
            for half, bank in ((0, 4), (1, 5)):
                for it in range(22):
                    last = nc.tensor.matmul(
                        BANK(bank), hqf[p][:, 128 * it: 128 * (it + 1)],
                        wdT[:, 1024 * it + 512 * half: 1024 * it + 512 * (half + 1)],
                        start=(it == 0), stop=(it == 21))
            return last
        s.add("pe", down, reads=[f"hqf{p}", "wdT", "psb4", "psb5"],
              writes=["psb4", "psb5"], tag=f"dn{i}", cost=12000)
        s.add("dve", lambda e: (e.tensor_copy(ffn_b[p][:, 0:512], BANK(4)),
                                e.tensor_copy(ffn_b[p][:, 512:1024], BANK(5)))[-1],
              reads=["psb4", "psb5"], writes=[f"ffnb{p}", "psb4", "psb5"],
              tag=f"evf{i}")
        s.add("pool", lambda e: dma(e, ffn_s[C * i: C * (i + 1), :], ffn_b[p][:]),
              reads=[f"ffnb{p}"], n_dma=1, tag=f"stf{i}")

    for i in range(NT):
        p3_tile(i)

    s.enabled = True
    s.add("pool", lambda e: e.nop(), writes=["p3done"] + list(s.last_writer.keys()),
          tag="join3")
    p3_es.close()

    s.enabled = phase_limit >= 4
    p2_es = ExitStack()
    wout_sb = SB(p2_es, "wout_sb", (128, 8 * 1024), BF16)
    summ_all = SB(p2_es, "summ_all", (128, NCORES * 516), F32)
    Sin = SB(p2_es, "Sin", (128, 512), F32)
    Sin_b = SB(p2_es, "Sinb", (128, 512), BF16)
    Dt = SB(p2_es, "Dt", (128, 4), F32)
    x2 = [SB(p2_es, f"x2_{p}", (128, E), F32) for p in range(2)]
    opre2 = [SB(p2_es, f"opre2_{p}", (128, E), BF16) for p in range(2)]
    qg2 = [SB(p2_es, f"qg2_{p}", (128, 512), BF16) for p in range(2)]
    sg2 = [SB(p2_es, f"sg2_{p}", (128, E), BF16) for p in range(2)]
    ffn2 = [SB(p2_es, f"ffn2_{p}", (128, E), BF16) for p in range(2)]
    ofull = SB(p2_es, "ofull", (128, E), F32)
    onorm = SB(p2_es, "onorm", (128, E), F32)
    sgo = SB(p2_es, "sgo", (128, E), BF16)
    sgof = [SB(p2_es, f"sgof{p}", (128, 8 * 128), BF16) for p in range(2)]
    out_sb = [SB(p2_es, f"out{p}", (128, E), F32) for p in range(2)]
    bn8 = SB(p2_es, "bn8", (128, 48), F32)
    oag = SB(p2_es, "oag", (128, 16), F32)
    orst = SB(p2_es, "orst", (128, 8), F32)
    otmp = SB(p2_es, "otmp", (128, 8), F32)

    s.add("pool", lambda e: cast_w(e, wout_sb[:], wout_d, 1024),
          reads=["p3done"], writes=["woutb"], n_dma=1, tag="ldwout")
    s.add("act", lambda e: dma(e, summ_all.rearrange("p (r f) -> p r f", f=516),
                               ag2_out.rearrange("(r p) f -> p r f", p=128)),
          reads=["ag2", "p3done"], writes=["sall"], n_dma=1, tag="ldsumm")

    s.add("dve", lambda e: e.memset(Sin[:], 0.0), reads=["p3done"],
          writes=["Sin"], tag="sin0")
    for r in range(NCORES - 1):
        base = 516 * r
        def dt_f(e, base=base, r=r):
            e.tensor_scalar_sub(Dt[:], summ_all[:, base + 512: base + 516], 1.0)
            return e.tensor_scalar(Dt[:], Dt[:], msk[:, r: r + 1], 1.0,
                                   ALU.mult, ALU.add)
        s.add("dve", dt_f, reads=["sall"], sreads=["msk"], writes=["Dt"],
              tag=f"dt{r}", force=True)
        def comb(e, base=base, r=r):
            last = None
            for kt in range(4):
                sl = slice(128 * kt, 128 * (kt + 1))
                e.tensor_scalar_mul(onorm[:, sl], summ_all[:, base + 128 * kt:
                                                           base + 128 * (kt + 1)],
                                    msk[:, r: r + 1])
                last = e.scalar_tensor_tensor(Sin[:, sl], Sin[:, sl],
                                              Dt[:, kt: kt + 1], onorm[:, sl],
                                              ALU.mult, ALU.add)
            return last
        s.add("dve", comb, reads=["sall", "Sin"], sreads=["Dt", "msk"],
              writes=["Sin"], tag=f"cb{r}", force=True)
    s.add("dve", lambda e: e.tensor_copy(Sin_b[:], Sin[:]), reads=["Sin"],
          writes=["Sinb"], tag="sinb")

    def p2_tile(i):
        p = i % 2
        def gate(n):
            s.enabled = (phase_limit >= 4) and (p2_parts >= n)
        gate(1)
        s.add("sp", lambda e: dma(e, x2[p][:], x_d[C * i: C * (i + 1), :]),
              reads=["p3done"], writes=[f"x2_{p}"], n_dma=1, tag=f"lx2{i}")
        def lds(e):
            return [dma(e, opre2[p][:], op_s[C * i: C * (i + 1), :]),
                    dma(e, qg2[p][:], qg_s[C * i: C * (i + 1), :]),
                    dma(e, sg2[p][:], sg_s[C * i: C * (i + 1), :]),
                    dma(e, ffn2[p][:], ffn_s[C * i: C * (i + 1), :])]
        s.add("act", lds, reads=["p3done"],
              writes=[f"opre2_{p}", f"qg2_{p}", f"sg2_{p}", f"ffn2_{p}"],
              n_dma=4, tag=f"lds{i}")
        gate(2)
        corr_pos = lambda h: (h % 2, 128 * (h // 2))
        def corr(e):
            last = None
            for h in (0, 2, 4, 6, 1, 3, 5, 7):
                hb = 64 * (h % 2)
                ht = 128 * (h // 2)
                ob, oc = corr_pos(h)
                last = nc.tensor.matmul(BANK(ob, 128, oc),
                                        qg2[p][hb:hb + 64, ht:ht + 128],
                                        Sin_b[hb:hb + 64, ht:ht + 128],
                                        start=True, stop=True)
            return last
        s.add("pe", corr, reads=[f"qg2_{p}", "Sinb", "psb0", "psb1"],
              writes=["psb0", "psb1"], tag=f"cr{i}", cost=1400)
        gate(2.5)
        def of_f(e):
            last = None
            for h in range(H):
                ob, oc = corr_pos(h)
                last = e.tensor_add(ofull[:, 128 * h: 128 * (h + 1)],
                                    opre2[p][:, 128 * h: 128 * (h + 1)],
                                    BANK(ob, 128, oc))
            return last
        s.add("dve", of_f,
              reads=["psb0", "psb1", f"opre2_{p}"],
              writes=["ofull", "psb0", "psb1"], tag=f"of{i}", force=True, cost=1400)
        gate(3)
        def lno(e):
            last = None
            for h in range(H):
                last = e.bn_stats(bn8[:, 6 * h: 6 * (h + 1)],
                                  ofull[:, 128 * h: 128 * (h + 1)])
            return last
        s.add("dve", lno, reads=["ofull"], writes=["bn8"], tag=f"lno{i}", force=True)
        def lna(e):
            last = None
            for h in range(H):
                last = e.bn_aggr(oag[:, 2 * h: 2 * (h + 1)], bn8[:, 6 * h: 6 * (h + 1)])
            return last
        s.add("dve", lna, reads=["bn8"], writes=["oag"], tag=f"lna{i}", force=True)
        gate(4)
        def ors(e):
            e.activation(otmp[:], oag.rearrange("p (h two) -> p h two", two=2)[:, :, 1],
                         AF.Ln, bias=K(0))
            return e.activation(orst[:], otmp[:], AF.Exp, scale=K(2))
        s.add("act", ors, reads=["oag", "konst"], writes=["orst"], tag=f"ors{i}",
              force=True, tset="E", cost=400)
        gate(5)
        def nrm(e):
            last = None
            for h in range(H):
                last = e.tensor_scalar(
                    onorm[:, 128 * h: 128 * (h + 1)],
                    ofull[:, 128 * h: 128 * (h + 1)],
                    oag[:, 2 * h: 2 * h + 1], orst[:, h: h + 1],
                    ALU.subtract, ALU.mult)
            return last
        s.add("dve", nrm, reads=["ofull"], sreads=["oag", "orst"],
              writes=["onorm2"], tag=f"nr{i}")
        s.add("dve", lambda e: e.tensor_mul(sgo[:], onorm[:], sg2[p][:]),
              reads=["onorm2", f"sg2_{p}"], writes=["sgo"], tag=f"sgo{i}")
        gate(6)
        def trs(e):
            last = None
            for kt in range(8):
                last = nc.tensor.transpose(
                    P[:, 64 * kt: 64 * (kt + 1)].bitcast(BF16),
                    sgo[:, 128 * kt: 128 * (kt + 1)], identb[:])
            return last
        s.add("pe", trs, reads=["sgo", "identb", "psb0"], writes=["psb0"],
              tag=f"trs{i}", cost=2300)
        def evs(e):
            return e.copy(sgof[p][:], P[:, 0:512].bitcast(BF16))
        s.add("act", evs, reads=["psb0"], writes=[f"sgof{p}", "psb0"], tag=f"evs{i}")
        gate(7)
        def wo(e):
            last = None
            for half, bank in ((0, 2), (1, 3)):
                for kt in range(8):
                    last = nc.tensor.matmul(
                        BANK(bank), sgof[p][:, 128 * kt: 128 * (kt + 1)],
                        wout_sb[:, kt * 1024 + 512 * half: kt * 1024 + 512 * (half + 1)],
                        start=(kt == 0), stop=(kt == 7))
            return last
        s.add("pe", wo, reads=[f"sgof{p}", "woutb", "psb2", "psb3"],
              writes=["psb2", "psb3"], tag=f"wo{i}", cost=4400)
        gate(8)
        def fin(e):
            e.tensor_add(out_sb[p][:, 0:512], BANK(2), x2[p][:, 0:512])
            e.tensor_add(out_sb[p][:, 512:1024], BANK(3), x2[p][:, 512:1024])
            e.tensor_add(out_sb[p][:, 0:512], out_sb[p][:, 0:512], ffn2[p][:, 0:512])
            return e.tensor_add(out_sb[p][:, 512:1024], out_sb[p][:, 512:1024],
                                ffn2[p][:, 512:1024])
        s.add("dve", fin, reads=["psb2", "psb3", f"x2_{p}", f"ffn2_{p}"],
              writes=[f"out{p}", "psb2", "psb3"], tag=f"fin{i}")
        gate(9)
        s.add("pool", lambda e: dma(e, y_d[C * i: C * (i + 1), :], out_sb[p][:]),
              reads=[f"out{p}"], writes=[f"ydma{i}"], n_dma=1, tag=f"sty{i}")
        s.enabled = phase_limit >= 4

    for i in range(NT):
        p2_tile(i)

    if taps:
        s.enabled = True
        tap_list = [("t_xq", xq_s, [SPAN, E], BF16, "none"),
                    ("t_sg", sg_s, [SPAN, E], BF16, "none"),
                    ("t_op", op_s, [SPAN, E], BF16, "none"),
                    ("t_qg", qg_s, [SPAN, 512], BF16, "none"),
                    ("t_ffn", ffn_s, [SPAN, E], BF16, "none"),
                    ("t_ag2i", ag2_in, [128, 516], F32, "none"),
                    ("t_gwg", g_wg, [NCORES * E, 352], BF16, "none"),
                    ("t_gwd", g_wd, [NCORES * I_FFN, 128], BF16, "none"),
                    ("t_ag2o", ag2_out, [NCORES * 128, 516], F32, "none")]
        for nm, srcd, shp, dt, _ in tap_list:
            td = nc.dram_tensor(nm, shp, dt, kind="ExternalOutput").ap()
            def cp(e, td=td, srcd=srcd):
                return dma(e, td, srcd)
            s.add("pool", cp, reads=["p3done", "ag2"], writes=[nm], n_dma=1, tag=nm)
        t_sin = nc.dram_tensor("t_sin", [128, 512], F32, kind="ExternalOutput").ap()
        s.add("pool", lambda e: dma(e, t_sin, Sin[:]), reads=["Sin"],
              writes=["t_sin"], n_dma=1, tag="t_sin")

    s.enabled = True
    s.add("pool", lambda e: e.nop(),
          reads=([f"ydma{i}" for i in range(NT)] if (phase_limit >= 4 and p2_parts >= 9) else []),
          writes=list(s.last_writer.keys()), tag="fence_out")

    s.finalize()
    with nc.Block() as block:
        s.emit(nc, block)
    p2_es.close()
    main_es.close()
    return nc


_NC_CACHE = None


def kernel(**inputs):
    import ml_dtypes
    BF = ml_dtypes.bfloat16
    F = np.float32
    g = lambda k: np.ascontiguousarray(np.asarray(inputs[k], dtype=F))
    x = g("L0_attn_x").reshape(16384, E)
    conv_w = g("L0_attn_conv_w")
    wq = g("L0_attn_wq"); wkg = g("L0_attn_wkg"); wv = g("L0_attn_wv")
    wg = g("L0_attn_wg"); bg = g("L0_attn_bg"); wout = g("L0_attn_wout")
    aug = g("L0_attn_aug_balance")
    w_gate = g("L0_ffn_w_gate"); w_up = g("L0_ffn_w_up"); w_down = g("L0_ffn_w_down")

    tri = np.triu(np.ones((128, 128), dtype=F))
    hones = np.zeros((128, 2), dtype=BF)
    hones[0:64, 0] = 1; hones[64:128, 1] = 1
    identb = np.eye(128, dtype=BF)
    cw_r = np.ascontiguousarray(conv_w.reshape(8, 128, 4).transpose(1, 0, 2).reshape(128, 32))
    aug_r = np.ascontiguousarray(aug.reshape(4, 128).T)
    bg1 = bg.reshape(1, E)

    global _NC_CACHE
    if _NC_CACHE is None:
        _NC_CACHE = build_program()
    nc = _NC_CACHE

    in_maps = []
    for j in range(NCORES):
        t0 = j * SPAN
        xh = np.zeros((16, E), dtype=F)
        if j > 0:
            xh[13:16] = x[t0 - 3: t0]
        mskj = np.zeros((128, NCORES), dtype=F)
        mskj[:, :j] = 1.0
        in_maps.append(dict(
            x=np.ascontiguousarray(x[t0: t0 + SPAN]),
            xh=xh, msk=mskj, tri=tri, hones=hones, identb=identb,
            cw=cw_r, augr=aug_r, bg=bg1,
            wq=wq, wkg=wkg, wv=wv, wg=wg, wout=wout,
            wgsh=np.ascontiguousarray(w_gate[352 * j: 352 * (j + 1)]),
            wush=np.ascontiguousarray(w_up[352 * j: 352 * (j + 1)]),
            wdsh=np.ascontiguousarray(w_down[128 * j: 128 * (j + 1)]),
        ))
    res = run_bass_kernel_spmd(nc, in_maps, core_ids=list(range(NCORES)))
    out = np.concatenate([np.asarray(r["y"]) for r in res.results], axis=0)
    return out.reshape(1, 16384, E).astype(np.float32)


# version 22
# speedup vs baseline: 1.2777x; 1.2580x over previous
import numpy as np
from contextlib import ExitStack

import concourse.bass as bass
import concourse.mybir as mybir
from concourse.bass_utils import run_bass_kernel_spmd

F32 = mybir.dt.float32
BF16 = mybir.dt.bfloat16
AF = mybir.ActivationFunctionType
ALU = mybir.AluOpType
AX = mybir.AxisListType

NCORES = 8
SPAN = 2048
C = 128
NT = SPAN // C
E = 1024
H = 8
DK = 64
DV = 128
I_FFN = 2816
LN16 = 1.0 / 16.0
LOG0125 = float(np.log(0.125))

ENG_ENGINES = ("sp", "act", "dve", "pe", "pool")
QUEUE_OF_ENGINE = {"sp": "qsp", "act": "qact", "pool": "qpool"}
NSEM_POOL = 16


class _Op:
    __slots__ = ("idx", "engine", "fn", "reads", "writes", "sreads", "n_dma",
                 "tag", "deps", "seq", "is_dma", "is_coll", "force", "queue",
                 "cost", "tset", "order", "prio")

    def __init__(self):
        self.deps = {}


class Sched:

    def __init__(self):
        self.ops = []
        self.last_writer = {}
        self.readers = {}
        self.enabled = True

    def add(self, engine, fn, reads=(), writes=(), sreads=(), n_dma=0,
            tag="", force=False, is_coll=False, cost=None, tset=None):
        if not self.enabled:
            return -1
        op = _Op()
        op.cost = cost if cost is not None else (1500 if n_dma else 500)
        op.tset = tset
        import builtins
        op.prio = getattr(builtins, 'op_prio_base', [0])[0] + len(self.ops)
        op.idx = len(self.ops)
        op.engine = engine
        op.fn = fn
        op.reads = tuple(reads) + tuple(sreads)
        op.sreads = tuple(sreads)
        op.writes = tuple(writes)
        op.n_dma = n_dma
        op.tag = tag
        op.is_dma = n_dma > 0
        op.is_coll = is_coll
        op.force = force
        deps = set()
        for r in op.reads:
            w = self.last_writer.get(r)
            if w is not None:
                deps.add(w)
        for w_ in op.writes:
            pw = self.last_writer.get(w_)
            if pw is not None:
                deps.add(pw)
            for rd in self.readers.get(w_, ()):
                deps.add(rd)
        deps.discard(op.idx)
        op.deps = deps
        for r in op.reads:
            self.readers.setdefault(r, []).append(op.idx)
        for w_ in op.writes:
            self.last_writer[w_] = op.idx
            self.readers[w_] = []
        self.ops.append(op)
        return op.idx

    def schedule(self):
        import heapq
        n = len(self.ops)
        succs = [[] for _ in range(n)]
        ndeps = [0] * n
        for op in self.ops:
            ndeps[op.idx] = len(op.deps)
            for d in op.deps:
                succs[d].append(op.idx)
        ready = {e: [] for e in ENG_ENGINES}
        for op in self.ops:
            if ndeps[op.idx] == 0:
                heapq.heappush(ready[op.engine], (op.prio, op.idx))
        eng_free = {e: 0.0 for e in ENG_ENGINES}
        act_set = [None]
        done_t = [0.0] * n
        events = []
        order = {e: [] for e in ENG_ENGINES}
        scheduled = 0
        while scheduled < n:
            progressed = False
            for e in ENG_ENGINES:
                if not ready[e]:
                    continue
                cand = heapq.nsmallest(12, ready[e])
                best_key, best_c = None, None
                for c in cand:
                    opc = self.ops[c[1]]
                    stc = max(eng_free[e],
                              max((done_t[d] for d in opc.deps), default=0.0))
                    if (e == "act" and act_set[0] is not None
                            and opc.tset is not None and opc.tset != act_set[0]):
                        stc += 1300
                    key = (stc, c[0])
                    if best_key is None or key < best_key:
                        best_key, best_c = key, c
                ready[e].remove(best_c)
                heapq.heapify(ready[e])
                pick = best_c[1]
                op = self.ops[pick]
                start = max(eng_free[e],
                            max((done_t[d] for d in op.deps), default=0.0))
                cost = op.cost
                if e == "act" and op.tset is not None and op.tset != act_set[0]:
                    cost += 1300
                    act_set[0] = op.tset
                fin = start + cost
                eng_free[e] = fin
                done_t[pick] = fin
                order[e].append(pick)
                scheduled += 1
                progressed = True
                for s_ in succs[pick]:
                    ndeps[s_] -= 1
                    if ndeps[s_] == 0:
                        heapq.heappush(ready[self.ops[s_].engine],
                                       (self.ops[s_].prio, s_))
            if not progressed:
                raise RuntimeError("scheduler stuck (cyclic deps?)")
        for e in ENG_ENGINES:
            for pos, oi in enumerate(order[e]):
                self.ops[oi].order = pos
        self.makespan = max(done_t) if n else 0.0

    def finalize(self):
        self.schedule()
        self.eng_seq = {e: [] for e in ENG_ENGINES}
        self.q_count = {}
        self.eng_order = {e: sorted((o for o in self.ops if o.engine == e),
                                    key=lambda o: o.order)
                          for e in ENG_ENGINES}
        qn = {}
        ncoll = 0
        for e in ENG_ENGINES:
            for op in self.eng_order[e]:
                if op.is_dma:
                    q = QUEUE_OF_ENGINE[op.engine]
                    nq = qn.get(q, 0)
                    qn[q] = nq + 1
                    slot = nq % NSEM_POOL
                    key = (q, slot)
                    base = self.q_count.get(key, 0)
                    self.q_count[key] = base + op.n_dma
                    op.queue = (q, slot, (base + op.n_dma) * 16)
                elif op.is_coll:
                    op.queue = ("coll", ncoll, 1)
                    ncoll += 1
                else:
                    op.queue = None
                    self.eng_seq[op.engine].append(op.idx)
                    op.seq = len(self.eng_seq[op.engine])

        done = [False] * len(self.ops)
        heads = {e: 0 for e in ENG_ENGINES}
        streams = self.eng_order
        progress = True
        ndone = 0
        while progress:
            progress = False
            for e in ENG_ENGINES:
                while heads[e] < len(streams[e]):
                    op = streams[e][heads[e]]
                    if all(done[d] for d in op.deps):
                        done[op.idx] = True
                        heads[e] += 1
                        ndone += 1
                        progress = True
                    else:
                        break
        if ndone != len(self.ops):
            stuck = [o for o in self.ops if not done[o.idx]][:5]
            raise RuntimeError("deadlock in op graph; first stuck: " +
                               ", ".join(f"{o.tag}@{o.engine}" for o in stuck))

    def emit(self, nc, block):
        es = ExitStack()
        eng_sem = {e: es.enter_context(nc.semaphore(f"es_{e}")) for e in ENG_ENGINES}
        pool_sems = {}
        for q in ("qsp", "qact", "qpool"):
            pool_sems[q] = [es.enter_context(nc.semaphore(f"{q}_{i}"))
                            for i in range(NSEM_POOL)]
        n_coll = sum(1 for o in self.ops if o.is_coll)
        coll_sems = [es.enter_context(nc.semaphore(f"coll{i}"))
                     for i in range(max(1, n_coll))]
        self._es = es
        sched = self

        def dep_points(op):
            pts = {}
            for d in sorted(op.deps):
                p = sched.ops[d]
                if p.is_dma:
                    q, slot, val = p.queue
                    key = ("q", q, slot)
                    pts[key] = max(pts.get(key, 0), val)
                elif p.is_coll:
                    _, cidx, val = p.queue
                    pts[("coll", cidx)] = max(pts.get(("coll", cidx), 0), val)
                else:
                    key = ("e", p.engine)
                    pts[key] = max(pts.get(key, 0), p.seq)
            return pts

        waited = {e: {} for e in ENG_ENGINES}

        def run_stream(eng_name, eng):
            for op in sched.eng_order[eng_name]:
                for key, val in sorted(dep_points(op).items(), key=str):
                    if waited[eng_name].get(key, 0) >= val:
                        continue
                    if key[0] == "q":
                        sem = pool_sems[key[1]][key[2]]
                    elif key[0] == "coll":
                        sem = coll_sems[key[1]]
                    else:
                        sem = eng_sem[key[1]]
                    eng.wait_ge(sem, val)
                    waited[eng_name][key] = val
                if op.is_dma:
                    q, slot, _ = op.queue
                    insts = op.fn(eng)
                    if not isinstance(insts, (list, tuple)):
                        insts = [insts]
                    assert len(insts) == op.n_dma, f"{op.tag}: n_dma mismatch {len(insts)} != {op.n_dma}"
                    for ins in insts:
                        ins.then_inc(pool_sems[q][slot], 16)
                elif op.is_coll:
                    ins = op.fn(eng)
                    ins.then_inc(coll_sems[op.queue[1]])
                else:
                    ins = op.fn(eng)
                    assert ins is not None, f"{op.tag}: fn returned None"
                    ins.then_inc(eng_sem[op.engine], 1)

        @block.sync
        def _(sync):
            run_stream("sp", sync)

        @block.scalar
        def _(scalar):
            run_stream("act", scalar)

        @block.vector
        def _(vector):
            run_stream("dve", vector)

        @block.tensor
        def _(tensor):
            run_stream("pe", tensor)

        @block.gpsimd
        def _(gpsimd):
            run_stream("pool", gpsimd)


def build_program(phase_limit=4, p2_parts=9, taps=False):
    nc = bass.Bass("TRN2", target_bir_lowering=False)
    s = Sched()
    op_prio_base = [0]
    import builtins
    builtins.op_prio_base = op_prio_base

    din = {}
    def IN(name, shape, dt=F32):
        din[name] = nc.dram_tensor(name, list(shape), dt, kind="ExternalInput").ap()
        return din[name]

    x_d = IN("x", (SPAN, E))
    xh_d = IN("xh", (16, E))
    msk_d = IN("msk", (128, NCORES))
    tri_d = IN("tri", (128, 128))
    hones_d = IN("hones", (128, 2), BF16)
    identb_d = IN("identb", (128, 128), BF16)
    cw_d = IN("cw", (128, 32))
    aug_d = IN("augr", (128, 4))
    bg_d = IN("bg", (1, E))
    wq_d = IN("wq", (E, 512))
    wkg_d = IN("wkg", (E, 512))
    wv_d = IN("wv", (E, E))
    wg_d = IN("wg", (E, E))
    wout_d = IN("wout", (E, E))
    wgsh_d = IN("wgsh", (I_FFN // 8, E))
    wush_d = IN("wush", (I_FFN // 8, E))
    wdsh_d = IN("wdsh", (E // 8, I_FFN))

    y_d = nc.dram_tensor("y", [SPAN, E], F32, kind="ExternalOutput").ap()

    xq_s = nc.dram_tensor("xq_s", [SPAN, E], BF16).ap()
    sg_s = nc.dram_tensor("sg_s", [SPAN, E], BF16).ap()
    op_s = nc.dram_tensor("op_s", [SPAN, E], BF16).ap()
    qg_s = nc.dram_tensor("qg_s", [SPAN, 512], BF16).ap()
    ffn_s = nc.dram_tensor("ffn_s", [SPAN, E], BF16).ap()
    wgt_st = nc.dram_tensor("wgt_st", [E, 352], BF16).ap()
    wut_st = nc.dram_tensor("wut_st", [E, 352], BF16).ap()
    wdt_st = nc.dram_tensor("wdt_st", [I_FFN, 128], BF16).ap()
    g_wg = nc.dram_tensor("g_wg", [NCORES * E, 352], BF16, addr_space="Shared").ap()
    g_wu = nc.dram_tensor("g_wu", [NCORES * E, 352], BF16, addr_space="Shared").ap()
    g_wd = nc.dram_tensor("g_wd", [NCORES * I_FFN, 128], BF16, addr_space="Shared").ap()
    ag2_in = nc.dram_tensor("ag2_in", [128, 516], F32).ap()
    ag2_out = nc.dram_tensor("ag2_out", [NCORES * 128, 516], F32, addr_space="Shared").ap()

    main_es = ExitStack()

    def SB(stack, name, shape, dt):
        return stack.enter_context(nc.sbuf_tensor("s_" + name, list(shape), dt))

    P = main_es.enter_context(nc.psum_tensor("P", [128, 4096], F32))
    def BANK(b, cols=512, off=0):
        return P[:, 512 * b + off: 512 * b + off + cols]

    tri = SB(main_es, "tri", (128, 128), F32)
    hones = SB(main_es, "hones", (128, 2), BF16)
    identb = SB(main_es, "identb", (128, 128), BF16)
    msk = SB(main_es, "msk", (128, NCORES), F32)
    cw = SB(main_es, "cw", (128, 32), F32)
    augr = SB(main_es, "augr", (128, 4), F32)
    bgf = SB(main_es, "bgf", (1, E), F32)
    bgb = SB(main_es, "bgb", (1, E), BF16)
    ones1 = SB(main_es, "ones1", (1, 128), BF16)
    S_sb = SB(main_es, "S_sb", (128, 512), F32)
    S_b16 = SB(main_es, "S_b16", (128, 512), BF16)
    Brun = SB(main_es, "Brun", (128, 4), F32)
    summ = SB(main_es, "summ", (128, 516), F32)
    konst = SB(main_es, "konst", (128, 8), F32)
    qgt2 = SB(main_es, "qgt2", (128, 512), F32)

    def dma(eng, out, in_):
        return eng.dma_start(out=out, in_=in_)

    s.add("sp", lambda e: [dma(e, tri[:], tri_d), dma(e, hones[:], hones_d),
                           dma(e, identb[:], identb_d), dma(e, msk[:], msk_d),
                           dma(e, cw[:], cw_d), dma(e, augr[:], aug_d),
                           dma(e, bgf[:], bg_d)],
          writes=["tri", "hones", "identb", "msk", "cw", "augr", "bgf"],
          n_dma=7, tag="ld_consts")
    s.add("dve", lambda e: e.tensor_copy(bgb[:], bgf[:]), reads=["bgf"],
          writes=["bgb"], tag="bgb")
    s.add("dve", lambda e: e.memset(ones1[:], 1.0), writes=["ones1"], tag="ones1")
    s.add("dve", lambda e: e.memset(S_sb[:], 0.0), writes=["S"], tag="S0")
    s.add("dve", lambda e: e.memset(S_b16[:], 0.0), writes=["Sb16"], tag="Sb0")
    s.add("dve", lambda e: e.memset(Brun[:], 0.0), writes=["Brun"], tag="Br0")
    KV = {0: 1e-5, 1: 1.0, 2: -0.5, 3: LN16, 4: -LN16, 5: LOG0125, 6: -1.0}
    def kmemset(e):
        last = None
        for i, v in KV.items():
            last = e.memset(konst[:, i:i + 1], v)
        return last
    s.add("dve", kmemset, writes=["konst"], tag="konst")
    K = lambda i: konst[:, i:i + 1]

    p1_es = ExitStack()
    wq_sb = SB(p1_es, "wq_sb", (128, 8 * 512), BF16)
    wkg_sb = SB(p1_es, "wkg_sb", (128, 8 * 512), BF16)
    wv_sb = SB(p1_es, "wv_sb", (128, 8 * 1024), BF16)
    wg_sb = SB(p1_es, "wg_sb", (128, 8 * 1024), BF16)

    def cast_w(eng, dst, src, mcols):
        return dma(eng, dst.rearrange("p (kt m) -> p kt m", m=mcols),
                   src.rearrange("(kt p) m -> p kt m", p=128))
    s.add("pool", lambda e: [cast_w(e, wq_sb[:], wq_d, 512),
                             cast_w(e, wkg_sb[:], wkg_d, 512)],
          writes=["wq", "wkg"], n_dma=2, tag="ld_wqkg")
    s.add("pool", lambda e: [cast_w(e, wv_sb[:], wv_d, 1024),
                             cast_w(e, wg_sb[:], wg_d, 1024)],
          writes=["wv", "wg"], n_dma=2, tag="ld_wvg")

    wsh = SB(p1_es, "wsh", (128, 3 * 1024), F32)
    wshq = SB(p1_es, "wshq", (128, 3 * 1024), BF16)
    qt_bmax = SB(p1_es, "qt_bmax", (128, 192), F32)
    qt_sc = SB(p1_es, "qt_sc", (128, 192), F32)
    qt_si = SB(p1_es, "qt_si", (128, 192), F32)
    qt_xa = SB(p1_es, "qt_xa", (128, 3 * 1024), F32)
    qt_t1 = SB(p1_es, "qt_t1", (128, 3 * 1024), F32)
    stg = SB(p1_es, "stg", (128, 352), BF16)

    def _chain_ops(bufset, src_r3, xa, t1, gt2, si, sc, dst_r3, cols, blk,
                   rds, wr, cost_scale=1.0):
        MAGIC = float(2 ** 23)
        cd = cols
        def f_chain(e):
            e.tensor_tensor(src_r3(xa), src_r3(xa), si, ALU.mult)
            xd, td = xa[:, 0:cd], t1[:, 0:cd]
            e.tensor_scalar(td, xd, 2.0, MAGIC, ALU.mult, ALU.add)
            e.tensor_scalar(td, td, MAGIC, 0.5, ALU.subtract, ALU.mult)
            e.tensor_scalar_min(td, td, 2.0)
            e.scalar_tensor_tensor(td, xd, 2.5, td, ALU.is_gt, ALU.add)
            e.scalar_tensor_tensor(td, xd, 3.5, td, ALU.is_gt, ALU.add)
            e.scalar_tensor_tensor(td, xd, 5.0, td, ALU.is_gt, ALU.add)
            return e.scalar_tensor_tensor(td, xd, 5.0, td, ALU.is_gt, ALU.add)
        s.add("dve", f_chain, reads=[bufset + "_xa"], sreads=[bufset + "_si"],
              writes=[bufset + "_t1", bufset + "_xa"], tag=bufset + "q1",
              cost=int((cols + cd * 6.0) * cost_scale + 500))
        if cd < cols:
            def g_chain(e):
                xg, tg, t2g = xa[:, cd:cols], t1[:, cd:cols], gt2[:, 0:cols - cd]
                e.tensor_scalar(tg, xg, 2.0, MAGIC, ALU.mult, ALU.add)
                e.tensor_scalar(tg, tg, MAGIC, 0.5, ALU.subtract, ALU.mult)
                e.tensor_scalar_min(tg, tg, 2.0)
                for thr, rep in ((2.5, 1), (3.5, 1), (5.0, 2)):
                    e.tensor_single_scalar(t2g, xg, thr, ALU.is_gt)
                    for _ in range(rep):
                        e.tensor_add(tg, tg, t2g)
                return e.tensor_copy(gt2[:, 0:1], tg[:, 0:1])
            s.add("pool", g_chain, reads=[bufset + "_xa"],
                  writes=[bufset + "_t1g", bufset + "_gt2"], tag=bufset + "q1g",
                  cost=int((cols - cd) * 11.0 + 800))
        return cd

    def emit_quant(src_ap, dst_ap, nblk, cols, bufset, xa, t1, bmax, sc, si,
                   rd, wr, blk=16):
        rds_ = [rd]
        r3 = lambda ap: ap.rearrange("p (b i) -> p b i", i=blk)
        bca = lambda ap: ap[:, :].unsqueeze(2).to_broadcast([128, nblk, blk])
        def f_scale(e):
            e.tensor_reduce(bmax, r3(src_ap), AX.X, ALU.max, apply_absolute_value=True)
            return e.tensor_scalar(sc, bmax, 1.0 / 6.0, 1e-12, ALU.mult, ALU.max)
        s.add("dve", f_scale, sreads=rds_, writes=[bufset + "_sc"], tag=bufset + "q0",
              force=True, cost=int(cols * 1.4 + 150))
        def f_recip(e):
            return e.reciprocal(si, sc)
        s.add("dve", f_recip, sreads=[bufset + "_sc"], writes=[bufset + "_si"],
              tag=bufset + "qr", cost=300)
        def f_abs(e):
            return e.activation(xa, src_ap, AF.Abs)
        s.add("act", f_abs, reads=rds_, writes=[bufset + "_xa"], tag=bufset + "qa",
              cost=int(cols * 0.9 + 200))
        def xa_r3(ap):
            return ap.rearrange("p (b i) -> p b i", i=blk)
        cd = _chain_ops(bufset, xa_r3, xa, t1, qgt2, bca(si), sc, None, cols, blk,
                        rds_, wr)
        def f_sign(e):
            return e.activation(xa, src_ap, AF.Sign)
        s.add("act", f_sign, reads=rds_, writes=[bufset + "_xa"], tag=bufset + "qs",
              cost=int(cols * 0.9 + 200))
        def f_fin(e):
            e.tensor_mul(t1, t1, xa)
            return e.tensor_tensor(r3(dst_ap), r3(t1), bca(sc), ALU.mult)
        s.add("dve", f_fin, reads=[bufset + "_xa", bufset + "_t1g"],
              sreads=[bufset + "_sc", bufset + "_t1"],
              writes=[wr, bufset + "_t1"], tag=bufset + "q2",
              cost=int(cols * 2.2 + 300))

    def emit_quant2(src_ap, dst_ap, nblk, cols, bufset, xa, t1, bmax, sc, si,
                   rds, wr, blk=16):
        rds_ = rds
        r3 = lambda ap: ap.rearrange("p (b i) -> p b i", i=blk)
        bca = lambda ap: ap[:, :].unsqueeze(2).to_broadcast([128, nblk, blk])
        def f_scale(e):
            e.tensor_reduce(bmax, r3(src_ap), AX.X, ALU.max, apply_absolute_value=True)
            return e.tensor_scalar(sc, bmax, 1.0 / 6.0, 1e-12, ALU.mult, ALU.max)
        s.add("dve", f_scale, sreads=rds_, writes=[bufset + "_sc"], tag=bufset + "q0",
              force=True, cost=int(cols * 1.4 + 150))
        def f_recip(e):
            return e.reciprocal(si, sc)
        s.add("dve", f_recip, sreads=[bufset + "_sc"], writes=[bufset + "_si"],
              tag=bufset + "qr", cost=300)
        def f_abs(e):
            return e.activation(xa, src_ap, AF.Abs)
        s.add("act", f_abs, reads=rds_, writes=[bufset + "_xa"], tag=bufset + "qa",
              cost=int(cols * 0.9 + 200))
        def xa_r3(ap):
            return ap.rearrange("p (b i) -> p b i", i=blk)
        cd = _chain_ops(bufset, xa_r3, xa, t1, qgt2, bca(si), sc, None, cols, blk,
                        rds_, wr)
        def f_sign(e):
            return e.activation(xa, src_ap, AF.Sign)
        s.add("act", f_sign, reads=rds_, writes=[bufset + "_xa"], tag=bufset + "qs",
              cost=int(cols * 0.9 + 200))
        def f_fin(e):
            e.tensor_mul(t1, t1, xa)
            return e.tensor_tensor(r3(dst_ap), r3(t1), bca(sc), ALU.mult)
        s.add("dve", f_fin, reads=[bufset + "_xa", bufset + "_t1g"],
              sreads=[bufset + "_sc", bufset + "_t1"],
              writes=[wr, bufset + "_t1"], tag=bufset + "q2",
              cost=int(cols * 2.2 + 300))

    def prep_gate_like(src_d, stage_d, name):
        s.add("dve", lambda e: e.memset(wsh[:], 0.0), writes=["wsh"], tag=f"{name}z")
        def ld(e):
            return [dma(e, wsh[0:128, 0:1024], src_d[0:128, :]),
                    dma(e, wsh[0:128, 1024:2048], src_d[128:256, :]),
                    dma(e, wsh[0:96, 2048:3072], src_d[256:352, :])]
        s.add("sp", ld, writes=["wsh"], n_dma=3, tag=f"{name}ld")
        emit_quant(wsh[:, 0:3072], wshq[:, 0:3072], 192, 3072, "qw",
                   qt_xa[:, 0:3072], qt_t1[:, 0:3072], qt_bmax[:, 0:192],
                   qt_sc[:, 0:192], qt_si[:, 0:192], "wsh", "wshq")
        pbf = BANK(0, 176, 0).bitcast(BF16)
        for kt in range(8):
            def tr(e, kt=kt):
                last = None
                for pt in range(3):
                    pp = 128 if pt < 2 else 96
                    last = nc.tensor.transpose(
                        pbf[:, 128 * pt: 128 * pt + pp],
                        wshq[0:pp, 1024 * pt + 128 * kt: 1024 * pt + 128 * (kt + 1)],
                        identb[0:pp, 0:pp])
                return last
            s.add("pe", tr, reads=["wshq", "identb", "psb0"], writes=["psb0"],
                  tag=f"{name}t{kt}")
            def ev(e, kt=kt):
                return e.copy(stg[:, 0:352], pbf[:, 0:352])
            s.add("act", ev, reads=["psb0"], writes=["stg", "psb0"], tag=f"{name}e{kt}")
            def st(e, kt=kt, stage_d=stage_d):
                return dma(e, stage_d[128 * kt: 128 * (kt + 1), :], stg[:, 0:352])
            s.add("pool", st, reads=["stg"], writes=[f"{name}st{kt}"], n_dma=1,
                  tag=f"{name}s{kt}")

    op_prio_base[0] = 100000
    prep_gate_like(wgsh_d, wgt_st, "pwg")
    prep_gate_like(wush_d, wut_st, "pwu")

    s.add("sp", lambda e: dma(e, wsh[:, 0:2816], wdsh_d), writes=["wsh"],
          n_dma=1, tag="pwdld")
    emit_quant(wsh[:, 0:2816], wshq[:, 0:2816], 176, 2816, "qw",
               qt_xa[:, 0:2816], qt_t1[:, 0:2816], qt_bmax[:, 0:176],
               qt_sc[:, 0:176], qt_si[:, 0:176], "wsh", "wshq")
    pbf2 = BANK(0, 64, 0).bitcast(BF16)
    for it in range(22):
        def tr(e, it=it):
            return nc.tensor.transpose(pbf2[:, 0:128],
                                       wshq[:, 128 * it: 128 * (it + 1)],
                                       identb[:])
        s.add("pe", tr, reads=["wshq", "identb", "psb0"], writes=["psb0"],
              tag=f"pwdt{it}")
        def ev(e, it=it):
            return e.copy(stg[:, 0:128], pbf2[:, 0:128])
        s.add("act", ev, reads=["psb0"], writes=["stg", "psb0"], tag=f"pwde{it}")
        def st(e, it=it):
            return dma(e, wdt_st[128 * it: 128 * (it + 1), :], stg[:, 0:128])
        s.add("pool", st, reads=["stg"], writes=[f"pwdst{it}"], n_dma=1,
              tag=f"pwds{it}")

    def coll(kind_ins, kind_outs, tag, deps, wres):
        def f(e):
            return e.collective_compute(
                "AllGather", ALU.bypass,
                replica_groups=[list(range(NCORES))],
                ins=[kind_ins.opt()], outs=[kind_outs.opt()])
        s.add("pool", f, reads=deps, writes=[wres], is_coll=True, tag=tag)
    coll(wgt_st, g_wg, "coll_wg", [f"pwgst{k}" for k in range(8)], "gwg")
    coll(wut_st, g_wu, "coll_wu", [f"pwust{k}" for k in range(8)], "gwu")
    coll(wdt_st, g_wd, "coll_wd", [f"pwdst{k}" for k in range(22)], "gwd")
    op_prio_base[0] = 0

    x_sb = [SB(p1_es, f"x{p}", (128, E), F32) for p in range(2)]
    bn6 = [SB(p1_es, "bn6" + str(_p), (128, 12), F32) for _p in range(2)]
    bnag = [SB(p1_es, "bnag" + str(_p), (128, 2), F32) for _p in range(2)]
    rstd = [SB(p1_es, "rstd" + str(_p), (128, 1), F32) for _p in range(2)]
    lntmp = SB(p1_es, "lntmp", (128, 1), F32)
    xn = SB(p1_es, "xn", (128, E), F32)
    xnb = SB(p1_es, "xnb", (128, E), BF16)
    XNF_P = 144
    q_bmax = SB(p1_es, "q_bmax", (128, 64), F32)
    q_sc = SB(p1_es, "q_sc", (128, 64), F32)
    q_si = SB(p1_es, "q_si", (128, 64), F32)
    q_xa = SB(p1_es, "q_xa", (128, E), F32)
    q_t1 = SB(p1_es, "q_t1", (128, E), F32)
    xq_sb = [SB(p1_es, f"xq{p}", (128, E), BF16) for p in range(2)]
    xnf = [SB(p1_es, f"xnf{p}", (128, 8 * XNF_P), BF16) for p in range(2)]
    cacc = SB(p1_es, "cacc", (128, 8 * 128), F32)
    xc = [SB(p1_es, "xc" + str(_p), (128, 8 * 128), BF16) for _p in range(2)]
    q_fm = SB(p1_es, "q_fm", (128, 512), F32)
    kg_fm = SB(p1_es, "kg_fm", (128, 512), F32)
    lsb = SB(p1_es, "lsb", (128, 512), F32)
    bcum = SB(p1_es, "bcum", (128, 512), F32)
    epos = SB(p1_es, "epos", (128, 512), F32)
    eneg = SB(p1_es, "eneg", (128, 512), F32)
    ekb = SB(p1_es, "ekb", (128, 512), F32)
    kfp = SB(p1_es, "kfp", (128, 512), F32)
    ktf = SB(p1_es, "ktf", (128, 512), F32)
    qt_b = SB(p1_es, "qt_b", (128, 512), BF16)
    kt_b = [SB(p1_es, "kt_b" + str(_p), (128, 512), BF16) for _p in range(2)]
    kh_b = [SB(p1_es, "kh_b" + str(_p), (128, 512), BF16) for _p in range(2)]
    khn = [SB(p1_es, f"khn{p}", (128, 512), BF16) for p in range(2)]
    qg_b = [SB(p1_es, f"qg{p}", (128, 512), BF16) for p in range(2)]
    ebt = [SB(p1_es, "ebt" + str(_p), (128, 4), F32) for _p in range(2)]
    ebrun = [SB(p1_es, "ebrun" + str(_p), (128, 4), F32) for _p in range(2)]
    m2b = SB(p1_es, "m2b", (128, 512), BF16)
    m1f = SB(p1_es, "m1f", (128, 512), F32)
    augw = [SB(p1_es, f"augw{p}", (128, 8), F32) for p in range(2)]
    v_nat = [SB(p1_es, f"vnat{p}", (128, E), BF16) for p in range(2)]
    g_raw = [SB(p1_es, f"graw{p}", (128, E), F32) for p in range(2)]
    o_acc = [SB(p1_es, f"oacc{p}", (128, E), F32) for p in range(2)]
    opre_b = [SB(p1_es, f"opre{p}", (128, E), BF16) for p in range(2)]
    sg_b = [SB(p1_es, f"sgb{p}", (128, E), BF16) for p in range(2)]
    atm = [SB(p1_es, f"atm{p}", (128, 128), BF16) for p in range(2)]
    augarg = SB(p1_es, "augarg", (128, E), BF16)

    s.add("sp", lambda e: dma(e, x_sb[0][0:16, :], xh_d), writes=["x0"],
          n_dma=1, tag="ld_xh")
    s.add("dve", lambda e: (e.bn_stats(bn6[1][0:16, 0:6], x_sb[0][0:16, 0:512]),
                            e.bn_stats(bn6[1][0:16, 6:12], x_sb[0][0:16, 512:1024]))[-1],
          reads=["x0"], writes=["bn61"], tag="h_bn", force=True)
    s.add("dve", lambda e: e.bn_aggr(bnag[1][0:16, :], bn6[1][0:16, :]),
          reads=["bn61"], writes=["bnag1"], tag="h_ag", force=True)
    s.add("act", lambda e: (e.activation(lntmp[0:16], bnag[1][0:16, 1:2], AF.Ln, bias=konst[0:16, 0:1]),
                            e.activation(rstd[1][0:16], lntmp[0:16], AF.Exp, scale=konst[0:16, 2:3]))[-1],
          reads=["bnag1", "konst"], writes=["rstd1"], tag="h_rstd", force=True,
          tset="E", cost=600)
    s.add("dve", lambda e: e.tensor_scalar(xn[0:16, :], x_sb[0][0:16, :],
                                           bnag[1][0:16, 0:1], rstd[1][0:16, :],
                                           ALU.subtract, ALU.mult),
          reads=["x0"], sreads=["bnag1", "rstd1"], writes=["xn"], tag="h_xn")
    s.add("dve", lambda e: e.tensor_copy(xnb[0:16, :], xn[0:16, :]),
          reads=["xn"], writes=["xnb"], tag="h_xnb")
    def h_tr(e):
        last = None
        for kt in range(8):
            last = nc.tensor.transpose(
                P[:, 64 * kt + 8 * kt: 64 * kt + 8 * kt + 8].bitcast(BF16),
                xnb[0:16, 128 * kt: 128 * (kt + 1)], identb[0:16, 0:16])
        return last
    s.add("pe", h_tr, reads=["xnb", "identb", "psb0", "psb1"],
          writes=["psb0", "psb1"], tag="h_tr")
    def h_ev(e):
        last = None
        for kt in range(8):
            last = e.copy(xnf[1][:, XNF_P * kt: XNF_P * kt + 16],
                          P[:, 72 * kt: 72 * kt + 8].bitcast(BF16))
        return last
    s.add("act", h_ev, reads=["psb0", "psb1"], writes=["xnfd1", "psb0", "psb1"],
          tag="h_ev")
    def h_cp(e):
        src = xnf[1].rearrange("p (k c) -> p k c", c=XNF_P)
        return e.tensor_copy(src[:, :, 141:144], src[:, :, 13:16])
    s.add("dve", h_cp, reads=["xnfd1"], writes=["xnfh1", "xnfd1"], tag="h_cp")

    def p1_tile(i, bn6=bn6, bnag=bnag, rstd=rstd,
                xc=xc,
                kt_b=kt_b, kh_b=kh_b, ebt=ebt, ebrun=ebrun):
        p = i % 2
        pp = 1 - p
        bn6 = bn6[p]; bnag = bnag[p]; rstd = rstd[p]
        xc = xc[p]
        kt_b = kt_b[p]; kh_b = kh_b[p]
        ebt = ebt[p]; ebrun = ebrun[p]
        xr = lambda buf: buf.rearrange("p (k c) -> p k c", c=XNF_P)

        s.add("sp", lambda e: dma(e, x_sb[p][:], x_d[C * i: C * (i + 1), :]),
              writes=[f"x{p}"], n_dma=1, tag=f"ldx{i}")
        s.add("dve", lambda e: (e.bn_stats(bn6[:, 0:6], x_sb[p][:, 0:512]),
                                e.bn_stats(bn6[:, 6:12], x_sb[p][:, 512:1024]))[-1],
              reads=[f"x{p}"], writes=["bn6" + str(p)], tag=f"bn{i}", force=True)
        s.add("dve", lambda e: e.bn_aggr(bnag[:], bn6[:]),
              reads=["bn6" + str(p)], writes=["bnag" + str(p)], tag=f"ag{i}", force=True)
        s.add("act", lambda e: (e.activation(lntmp[:], bnag[:, 1:2], AF.Ln, bias=K(0)),
                                e.activation(rstd[:], lntmp[:], AF.Exp, scale=K(2)))[-1],
              reads=["bnag" + str(p), "konst"], writes=["rstd" + str(p)], tag=f"rstd{i}", force=True,
              tset="E", cost=600)
        s.add("dve", lambda e: e.tensor_scalar(xn[:], x_sb[p][:], bnag[:, 0:1],
                                               rstd[:], ALU.subtract, ALU.mult),
              reads=[f"x{p}"], sreads=["bnag" + str(p), "rstd" + str(p)], writes=["xn"], tag=f"xn{i}")
        s.add("dve", lambda e: e.tensor_copy(xnb[:], xn[:]),
              reads=["xn"], writes=["xnb"], tag=f"xnb{i}")
        def trx(e):
            last = None
            for kt in range(8):
                last = nc.tensor.transpose(
                    P[:, 64 * kt: 64 * (kt + 1)].bitcast(BF16),
                    xnb[:, 128 * kt: 128 * (kt + 1)], identb[:])
            return last
        s.add("pe", trx, reads=["xnb", "identb", "psb0"], writes=["psb0"],
              tag=f"trx{i}", cost=2300)
        def evx(e):
            return e.copy(xr(xnf[p])[:, :, 16:144],
                          P[:, 0:512].bitcast(BF16).rearrange("q (k c) -> q k c", c=128))
        s.add("act", evx, reads=["psb0"], writes=[f"xnfd{p}", "psb0"], tag=f"evx{i}")
        def hcp(e):
            return e.tensor_copy(xr(xnf[p])[:, :, 13:16], xr(xnf[pp])[:, :, 141:144])
        s.add("dve", hcp, reads=[f"xnfd{pp}", f"xnfh{pp}"],
              writes=[f"xnfh{p}"], tag=f"hcp{i}")

        emit_quant(xn[:], xq_sb[p][:], 64, E, "qx",
                   q_xa[:], q_t1[:], q_bmax[:], q_sc[:], q_si[:],
                   "xn", f"xqv{p}")
        s.add("pool", lambda e: dma(e, xq_s[C * i: C * (i + 1), :], xq_sb[p][:]),
              reads=[f"xqv{p}"], writes=[f"xqs{i}"], n_dma=1, tag=f"stxq{i}")

        def conv(e):
            v = xr(xnf[p])
            last = None
            for kt in range(8):
                e.tensor_scalar_mul(cacc[:, 128 * kt: 128 * (kt + 1)],
                                    v[:, kt, 13:141], cw[:, 4 * kt: 4 * kt + 1])
                for d in range(1, 4):
                    last = e.scalar_tensor_tensor(
                        cacc[:, 128 * kt: 128 * (kt + 1)],
                        v[:, kt, 13 + d: 141 + d], cw[:, 4 * kt + d: 4 * kt + d + 1],
                        cacc[:, 128 * kt: 128 * (kt + 1)], ALU.mult, ALU.add)
            return last
        s.add("dve", conv, reads=[f"xnfd{p}", f"xnfh{p}"], sreads=["cw"],
              writes=["cacc"], tag=f"cv{i}", cost=6500)
        s.add("act", lambda e: e.activation(xc[:], cacc[:], AF.Silu),
              reads=["cacc"], writes=["xc" + str(p)], tag=f"cs{i}", tset="S", cost=1100)

        if i > 0:
            emit_aug_sgg(i - 1, pp)

        def qkg(e, bank, wsb, wname):
            last = None
            for mt in range(4):
                for kt in range(8):
                    last = nc.tensor.matmul(
                        BANK(bank, 128, 128 * mt),
                        wsb[:, kt * 512 + mt * 128: kt * 512 + (mt + 1) * 128],
                        xc[:, 128 * kt: 128 * (kt + 1)],
                        start=(kt == 0), stop=(kt == 7))
            return last
        s.add("pe", lambda e: qkg(e, 0, wq_sb, "wq"), reads=["xc" + str(p), "wq"],
              writes=["psb0"], tag=f"qk0_{i}", cost=3800)
        s.add("pe", lambda e: qkg(e, 1, wkg_sb, "wkg"), reads=["xc" + str(p), "wkg"],
              writes=["psb1"], tag=f"qk1_{i}", cost=3800)
        s.add("act", lambda e: (e.copy(q_fm[:], BANK(0)),
                                e.copy(kg_fm[:], BANK(1)))[-1],
              reads=["psb0", "psb1"], writes=["qfm", "kgfm", "psb0", "psb1"],
              tag=f"evqk{i}")

        def vg(e):
            last = None
            for half, bank in ((0, 2), (1, 3)):
                for kt in range(8):
                    last = nc.tensor.matmul(
                        BANK(bank), xc[:, 128 * kt: 128 * (kt + 1)],
                        wv_sb[:, kt * 1024 + 512 * half: kt * 1024 + 512 * (half + 1)],
                        start=(kt == 0), stop=(kt == 7))
            return last
        def gg(e):
            last = None
            for half, bank in ((0, 4), (1, 5)):
                for kt in range(8):
                    nc.tensor.matmul(
                        BANK(bank), xc[:, 128 * kt: 128 * (kt + 1)],
                        wg_sb[:, kt * 1024 + 512 * half: kt * 1024 + 512 * (half + 1)],
                        start=(kt == 0), stop=False)
                last = nc.tensor.matmul(
                    BANK(bank), ones1[:], bgb[:, 512 * half: 512 * (half + 1)],
                    start=False, stop=True)
            return last
        s.add("pe", vg, reads=["xc" + str(p), "wv"], writes=["psb2", "psb3"], tag=f"v{i}",
              cost=4400)
        s.add("pe", gg, reads=["xc" + str(p), "wg", "ones1", "bgb"],
              writes=["psb4", "psb5"], tag=f"g{i}", cost=4800)
        s.add("act", lambda e: (e.copy(v_nat[p][:, 0:512], BANK(2)),
                                e.copy(v_nat[p][:, 512:1024], BANK(3)))[-1],
              reads=["psb2", "psb3"], writes=[f"vnat{p}", "psb2", "psb3"],
              tag=f"evv{i}")
        s.add("act", lambda e: (e.copy(g_raw[p][:, 0:512], BANK(4)),
                                e.copy(g_raw[p][:, 512:1024], BANK(5)))[-1],
              reads=["psb4", "psb5"], writes=[f"graw{p}", "psb4", "psb5"],
              tag=f"evg{i}")

        s.add("act", lambda e: e.activation(ktf[:], kg_fm[:], AF.Abs),
              reads=["kgfm"], writes=["kgabs"], tag=f"ka{i}", cost=600)
        s.add("act", lambda e: (e.activation(epos[:], ktf[:], AF.Exp, scale=K(6)),
                                e.activation(lsb[:], epos[:], AF.Ln, bias=K(1)))[-1],
              reads=["kgabs", "konst"], writes=["lntm", "epos"], tag=f"lse{i}",
              tset="E", cost=1100)
        s.add("dve", lambda e: (e.tensor_single_scalar(eneg[:], kg_fm[:], 0.0, ALU.min),
                                e.tensor_sub(lsb[:], eneg[:], lsb[:]))[-1],
              reads=["kgfm", "lntm"], writes=["ls", "eneg"], tag=f"ls{i}")
        def scan(e):
            last = None
            for kt in range(4):
                sl = slice(128 * kt, 128 * (kt + 1))
                last = e.tensor_tensor_scan(bcum[:, sl], lsb[:, sl], lsb[:, sl],
                                            0.0, ALU.add, ALU.bypass)
            return last
        s.add("dve", scan, sreads=["ls"], writes=["bcum"], tag=f"sc{i}")
        def exps(e):
            e.activation(epos[:], bcum[:], AF.Exp, scale=K(3), bias=K(5))
            e.activation(eneg[:], bcum[:], AF.Exp, scale=K(4))
            e.activation(ekb[:], lsb[:], AF.Exp, scale=K(3))
            bc = bcum.rearrange("p (k c) -> p k c", c=128)
            e.activation(ebt[:], bc[:, :, 127:128], AF.Exp, scale=K(3))
            return e.activation(ebrun[:], Brun[:], AF.Exp, scale=K(3))
        s.add("act", exps, reads=["bcum", "ls", "Brun", "konst"],
              writes=["epos", "eneg", "ek", "ebt" + str(p), "ebrun" + str(p)], tag=f"ex{i}",
              tset="E", cost=2200)
        def kqt(e):
            e.tensor_scalar(kfp[:], ekb[:], -1.0, 1.0, ALU.mult, ALU.add)
            e.tensor_mul(qt_b[:], q_fm[:], epos[:])
            e.tensor_mul(ktf[:], kfp[:], eneg[:])
            e.tensor_copy(kt_b[:], ktf[:])
            eb = ebt[:, :].unsqueeze(2).to_broadcast([128, 4, 128])
            r4 = lambda ap: ap.rearrange("p (k c) -> p k c", c=128)
            return e.tensor_tensor(r4(kh_b[:]), r4(ktf[:]), eb, ALU.mult)
        s.add("dve", kqt, reads=["ek", "epos", "eneg", "qfm"], sreads=["ebt" + str(p)],
              writes=["kfp", "qtb", "ktb" + str(p), "khb" + str(p), "kgabs"], tag=f"kq{i}")
        def qg(e):
            eb = ebrun[:, :].unsqueeze(2).to_broadcast([128, 4, 128])
            r4 = lambda ap: ap.rearrange("p (k c) -> p k c", c=128)
            return e.tensor_tensor(r4(qg_b[p][:]), r4(qt_b[:]), eb, ALU.mult)
        s.add("dve", qg, reads=["qtb"], sreads=["ebrun" + str(p)], writes=[f"qgb{p}"],
              tag=f"qg{i}")
        s.add("pool", lambda e: dma(e, qg_s[C * i: C * (i + 1), :], qg_b[p][:]),
              reads=[f"qgb{p}"], n_dma=1, tag=f"stqg{i}")
        s.add("dve", lambda e: e.tensor_add(Brun[:],
                                            Brun[:],
                                            bcum.rearrange("p (k c) -> p k c", c=128)[:, :, 127]),
              reads=["bcum", "ebrun" + str(p)], writes=["Brun"], tag=f"br{i}")
        def trk(e):
            last = None
            for kt in range(4):
                last = nc.tensor.transpose(
                    P[:, 512 + 64 * kt: 512 + 64 * (kt + 1)].bitcast(BF16),
                    kh_b[:, 128 * kt: 128 * (kt + 1)], identb[:])
            return last
        s.add("pe", trk, reads=["khb" + str(p), "identb", "psb1"], writes=["psb1"],
              tag=f"trk{i}", cost=1200)
        def evk(e):
            return e.copy(khn[p][:], P[:, 512:768].bitcast(BF16))
        s.add("act", evk, reads=["psb1"], writes=[f"khn{p}", "psb1"], tag=f"evk{i}")

        s.add("dve", lambda e: e.tensor_copy(S_b16[:], S_sb[:]),
              reads=["S"], writes=["Sb16"], tag=f"sb16{i}")

        def m2(e):
            e.tensor_mul(m1f[:], q_fm[:], kfp[:])
            ab = augr[:, :].unsqueeze(2).to_broadcast([128, 4, 128])
            r4 = lambda ap: ap.rearrange("p (k c) -> p k c", c=128)
            return e.tensor_tensor(r4(m2b[:]), r4(m1f[:]), ab, ALU.mult)
        s.add("dve", m2, reads=["qfm", "kfp"], sreads=["augr"], writes=["m2b"],
              tag=f"m2{i}")
        def augmm(e):
            last = None
            for kt in range(4):
                last = nc.tensor.matmul(BANK(3, 2, 2 * kt),
                                        m2b[:, 128 * kt: 128 * (kt + 1)],
                                        hones[:], start=True, stop=True)
            return last
        s.add("pe", augmm, reads=["m2b", "hones", "psb3"], writes=["psb3"],
              tag=f"au{i}")
        s.add("dve", lambda e: e.tensor_copy(augw[p][:], BANK(3, 8, 0)),
              reads=["psb3"], writes=[f"augw{p}", "psb3"], tag=f"evau{i}")

        for h in range(H):
            hb = 64 * (h % 2)
            ht = 128 * (h // 2)
            ab = 4 + (h % 2)
            ob = 6 + (h // 4)
            oc = 128 * (h % 4)
            def at(e, h=h, hb=hb, ht=ht, ab=ab):
                return nc.tensor.matmul(BANK(ab, 128, 0),
                                        kt_b[hb:hb + 64, ht:ht + 128],
                                        qt_b[hb:hb + 64, ht:ht + 128],
                                        start=True, stop=True)
            s.add("pe", at, reads=["ktb" + str(p), "qtb", f"psb{ab}"], writes=[f"psb{ab}"],
                  tag=f"at{i}_{h}")
            def mask(e, h=h, ab=ab):
                return e.tensor_tensor(atm[h % 2][:], BANK(ab, 128, 0), tri[:],
                                       ALU.mult)
            s.add("dve", mask, reads=[f"psb{ab}", "tri"],
                  writes=[f"atm{h % 2}", f"psb{ab}"], tag=f"mk{i}_{h}")
            def omm(e, h=h, hb=hb, ht=ht, ob=ob, oc=oc):
                nc.tensor.matmul(BANK(ob, 128, oc), atm[h % 2][:],
                                 v_nat[p][:, 128 * h: 128 * (h + 1)],
                                 start=True, stop=False)
                return nc.tensor.matmul(BANK(ob, 128, oc),
                                        qt_b[hb:hb + 64, ht:ht + 128],
                                        S_b16[hb:hb + 64, ht:ht + 128],
                                        start=False, stop=True)
            s.add("pe", omm, reads=[f"atm{h % 2}", f"vnat{p}", "qtb", "Sb16",
                                    f"psb{ob}"],
                  writes=[f"psb{ob}"], tag=f"o{i}_{h}")
            def umm(e, h=h, hb=hb, ht=ht):
                return nc.tensor.matmul(
                    P[hb:hb + 64, 1024 + ht: 1024 + ht + 128],
                    khn[p][:, 64 * h: 64 * (h + 1)],
                    v_nat[p][:, 128 * h: 128 * (h + 1)],
                    start=True, stop=True)
            s.add("pe", umm, reads=[f"khn{p}", f"vnat{p}", "psb2"],
                  writes=["psb2"], tag=f"u{i}_{h}")

        def supd(e):
            last = None
            for kt in range(4):
                sl = slice(128 * kt, 128 * (kt + 1))
                last = e.scalar_tensor_tensor(S_sb[:, sl], S_sb[:, sl],
                                              ebt[:, kt: kt + 1],
                                              BANK(2, 128, 128 * kt),
                                              ALU.mult, ALU.add)
            return last
        s.add("dve", supd, reads=["psb2", "S", "Sb16"], sreads=["ebt" + str(p)],
              writes=["S", "psb2"], tag=f"su{i}")

        s.add("act", lambda e: (e.copy(o_acc[p][:, 0:512], BANK(6)),
                                e.copy(o_acc[p][:, 512:1024], BANK(7)))[-1],
              reads=["psb6", "psb7"], writes=[f"oacc{p}", "psb6", "psb7"],
              tag=f"evo{i}")

    def emit_aug_sgg(j, pj):
        def arg(e):
            last = None
            for h in range(H):
                last = e.tensor_scalar_mul(augarg[:, 128 * h: 128 * (h + 1)],
                                           v_nat[pj][:, 128 * h: 128 * (h + 1)],
                                           augw[pj][:, h: h + 1])
            return last
        s.add("dve", arg, reads=[f"vnat{pj}"], sreads=[f"augw{pj}"],
              writes=["augarg"], tag=f"aa{j}")
        s.add("act", lambda e: e.activation(augarg[:], augarg[:], AF.Tanh, scale=0.5),
              reads=["augarg"], writes=["augsig"], tag=f"as{j}", tset="S", cost=1100)
        def oad(e):
            e.tensor_scalar(augarg[:], augarg[:], 0.5, 0.5, ALU.mult, ALU.add)
            return e.tensor_add(opre_b[pj][:], o_acc[pj][:], augarg[:])
        s.add("dve", oad,
              reads=["augsig", f"oacc{pj}"], writes=[f"opre{pj}", "augsig", "augarg"],
              tag=f"oad{j}")
        s.add("pool", lambda e: dma(e, op_s[C * j: C * (j + 1), :], opre_b[pj][:]),
              reads=[f"opre{pj}"], n_dma=1, tag=f"stop{j}")
        s.add("act", lambda e: e.activation(sg_b[pj][:], g_raw[pj][:], AF.Silu),
              reads=[f"graw{pj}"], writes=[f"sgb{pj}"], tag=f"sg{j}", tset="S",
              cost=1100)
        s.add("pool", lambda e: dma(e, sg_s[C * j: C * (j + 1), :], sg_b[pj][:]),
              reads=[f"sgb{pj}"], n_dma=1, tag=f"stsg{j}")

    s.enabled = phase_limit >= 2
    for i in range(NT):
        p1_tile(i)
    emit_aug_sgg(NT - 1, (NT - 1) % 2)

    s.add("act", lambda e: e.activation(summ[:, 512:516], Brun[:], AF.Exp, scale=K(3)),
          reads=["Brun", "konst"], writes=["summD"], tag="dspan", tset="E", cost=300)
    s.add("dve", lambda e: e.tensor_copy(summ[:, 0:512], S_sb[:]),
          reads=["S"], writes=["summS"], tag="packS")
    s.add("pool", lambda e: dma(e, ag2_in, summ[:]),
          reads=["summD", "summS"], writes=["ag2in"], n_dma=1, tag="st_summ")
    def coll2(e):
        return e.collective_compute("AllGather", ALU.bypass,
                                    replica_groups=[list(range(NCORES))],
                                    ins=[ag2_in.opt()], outs=[ag2_out.opt()])
    s.add("pool", coll2, reads=["ag2in"], writes=["ag2"], is_coll=True, tag="coll2")

    s.enabled = True
    s.add("pool", lambda e: e.nop(), writes=["p1done"] + list(s.last_writer.keys()),
          tag="join1")
    p1_es.close()

    p3_es = ExitStack()
    wgT = SB(p3_es, "wgT", (128, 8 * I_FFN), BF16)
    wuT = SB(p3_es, "wuT", (128, 8 * I_FFN), BF16)
    wdT = SB(p3_es, "wdT", (128, 22 * 1024), BF16)
    xqf = [SB(p3_es, f"xqf{p}", (128, 8 * 128), BF16) for p in range(2)]
    hfull = SB(p3_es, "hfull", (128, I_FFN), F32)
    hq = [SB(p3_es, f"hq{p}", (128, I_FFN), BF16) for p in range(2)]
    hqf = [SB(p3_es, f"hqf{p}", (128, 22 * 128), BF16) for p in range(2)]
    hgate = SB(p3_es, "hgate", (128, 512), F32)
    ffn_b = [SB(p3_es, f"ffnb{p}", (128, E), BF16) for p in range(2)]
    f_bmax = [SB(p3_es, f"f_bmax{j}", (128, 64), F32) for j in range(2)]
    f_sc = [SB(p3_es, f"f_sc{j}", (128, 64), F32) for j in range(2)]
    f_si = [SB(p3_es, f"f_si{j}", (128, 64), F32) for j in range(2)]
    f_xa = [SB(p3_es, f"f_xa{j}", (128, 704), F32) for j in range(2)]
    f_t1 = [SB(p3_es, f"f_t1{j}", (128, 704), F32) for j in range(2)]


    s.enabled = phase_limit >= 3
    def ld_wgT(e, gsrc, dst):
        outs = []
        for kt in range(8):
            src = gsrc.rearrange("(r k) i -> k r i", k=E)[128 * kt: 128 * (kt + 1)]
            outs.append(e.dma_start(out=dst[:, I_FFN * kt: I_FFN * (kt + 1)], in_=src))
        return outs
    s.add("sp", lambda e: ld_wgT(e, g_wg, wgT), reads=["p1done", "gwg"],
          writes=["wgT"], n_dma=8, tag="ldwgT")
    s.add("sp", lambda e: ld_wgT(e, g_wu, wuT), reads=["p1done", "gwu"],
          writes=["wuT"], n_dma=8, tag="ldwuT")
    def ld_wdT(e):
        outs = []
        for it in range(22):
            src = g_wd.rearrange("(r i) h -> i r h", i=I_FFN)[128 * it: 128 * (it + 1)]
            outs.append(e.dma_start(out=wdT[:, 1024 * it: 1024 * (it + 1)], in_=src))
        return outs
    s.add("sp", ld_wdT, reads=["p1done", "gwd"], writes=["wdT"],
          n_dma=22, tag="ldwdT")
    IB = [(0, 512), (512, 512), (1024, 512), (1536, 512), (2048, 512), (2560, 256)]

    def p3_tile(i):
        p = i % 2
        def ldxq(e):
            return [e.dma_start(out=xqf[p][:, 128 * kt: 128 * (kt + 1)],
                                in_=xq_s[C * i: C * (i + 1), 128 * kt: 128 * (kt + 1)],
                                transpose=True)
                    for kt in range(8)]
        s.add("sp", ldxq, reads=[f"xqs{i}", "p1done"], writes=[f"xqf{p}"], n_dma=8,
              tag=f"lxq{i}")
        for b, (i0, w) in enumerate(IB):
            gb = (b % 2) * 2
            ub = (b % 2) * 2 + 1
            def gu(e, i0=i0, w=w, gb=gb, ub=ub):
                last = None
                for wsb, bank in ((wgT, gb), (wuT, ub)):
                    for kt in range(8):
                        last = nc.tensor.matmul(
                            BANK(bank, w, 0), xqf[p][:, 128 * kt: 128 * (kt + 1)],
                            wsb[:, I_FFN * kt + i0: I_FFN * kt + i0 + w],
                            start=(kt == 0), stop=(kt == 7))
                return last
            s.add("pe", gu, reads=[f"xqf{p}", "wgT", "wuT", f"psb{gb}", f"psb{ub}"],
                  writes=[f"psb{gb}", f"psb{ub}"], tag=f"gu{i}_{b}", cost=4400)
            def sl(e, i0=i0, w=w, gb=gb):
                return e.activation(hgate[:, 0:w], BANK(gb, w, 0), AF.Silu)
            s.add("act", sl, reads=[f"psb{gb}"], writes=["hgate", f"psb{gb}"],
                  tag=f"sl{i}_{b}", tset="S", cost=700)
            def hm(e, i0=i0, w=w, ub=ub):
                return e.tensor_tensor(hfull[:, i0:i0 + w], hgate[:, 0:w],
                                       BANK(ub, w, 0), ALU.mult)
            s.add("dve", hm, reads=["hgate", f"psb{ub}"],
                  writes=[f"hfb{b}", f"psb{ub}"], tag=f"hm{i}_{b}", cost=700,
                  force=True)
        QCH = [(0, 704, ("hfb0", "hfb1")), (704, 704, ("hfb1", "hfb2")),
               (1408, 704, ("hfb2", "hfb3", "hfb4")), (2112, 704, ("hfb4", "hfb5"))]
        for qc, (i0, w, hdeps) in enumerate(QCH):
            q_ = qc % 2
            emit_quant2(hfull[:, i0:i0 + w], hq[p][:, i0:i0 + w], 44, w,
                        f"qf{q_}", f_xa[q_][:, 0:w], f_t1[q_][:, 0:w],
                        f_bmax[q_][:, 0:44], f_sc[q_][:, 0:44],
                        f_si[q_][:, 0:44], list(hdeps), f"hqv{qc}_{p}")
        trh_deps = (("hqv0", "hqv1"), ("hqv1", "hqv2"), ("hqv2", "hqv3"))
        for bi, (bank, it0, itn) in enumerate(((6, 0, 8), (7, 8, 16), (6, 16, 22))):
            def trh_pe(e, bank=bank, it0=it0, itn=itn):
                last = None
                for j, it in enumerate(range(it0, itn)):
                    last = nc.tensor.transpose(
                        P[:, 512 * bank + 64 * j: 512 * bank + 64 * (j + 1)].bitcast(BF16),
                        hq[p][:, 128 * it: 128 * (it + 1)], identb[:])
                return last
            s.add("pe", trh_pe,
                  reads=[f"{d}_{p}" for d in trh_deps[bi]] + ["identb", f"psb{bank}"],
                  writes=[f"psb{bank}"], tag=f"trh{i}_{bi}", cost=2300)
            def evh(e, bank=bank, it0=it0, itn=itn):
                return e.copy(hqf[p][:, 128 * it0: 128 * itn],
                              P[:, 512 * bank: 512 * bank + 64 * (itn - it0)].bitcast(BF16))
            s.add("act", evh, reads=[f"psb{bank}"],
                  writes=[f"hqf{p}", f"psb{bank}"], tag=f"evh{i}_{bi}", cost=900)
        def down(e):
            last = None
            for half, bank in ((0, 4), (1, 5)):
                for it in range(22):
                    last = nc.tensor.matmul(
                        BANK(bank), hqf[p][:, 128 * it: 128 * (it + 1)],
                        wdT[:, 1024 * it + 512 * half: 1024 * it + 512 * (half + 1)],
                        start=(it == 0), stop=(it == 21))
            return last
        s.add("pe", down, reads=[f"hqf{p}", "wdT", "psb4", "psb5"],
              writes=["psb4", "psb5"], tag=f"dn{i}", cost=12000)
        s.add("dve", lambda e: (e.tensor_copy(ffn_b[p][:, 0:512], BANK(4)),
                                e.tensor_copy(ffn_b[p][:, 512:1024], BANK(5)))[-1],
              reads=["psb4", "psb5"], writes=[f"ffnb{p}", "psb4", "psb5"],
              tag=f"evf{i}")
        s.add("pool", lambda e: dma(e, ffn_s[C * i: C * (i + 1), :], ffn_b[p][:]),
              reads=[f"ffnb{p}"], n_dma=1, tag=f"stf{i}")

    for i in range(NT):
        p3_tile(i)

    s.enabled = True
    s.add("pool", lambda e: e.nop(), writes=["p3done"] + list(s.last_writer.keys()),
          tag="join3")
    p3_es.close()

    s.enabled = phase_limit >= 4
    p2_es = ExitStack()
    wout_sb = SB(p2_es, "wout_sb", (128, 8 * 1024), BF16)
    summ_all = SB(p2_es, "summ_all", (128, NCORES * 516), F32)
    Sin = SB(p2_es, "Sin", (128, 512), F32)
    Sin_b = SB(p2_es, "Sinb", (128, 512), BF16)
    Dt = SB(p2_es, "Dt", (128, 4), F32)
    x2 = [SB(p2_es, f"x2_{p}", (128, E), F32) for p in range(2)]
    opre2 = [SB(p2_es, f"opre2_{p}", (128, E), BF16) for p in range(2)]
    qg2 = [SB(p2_es, f"qg2_{p}", (128, 512), BF16) for p in range(2)]
    sg2 = [SB(p2_es, f"sg2_{p}", (128, E), BF16) for p in range(2)]
    ffn2 = [SB(p2_es, f"ffn2_{p}", (128, E), BF16) for p in range(2)]
    ofull = SB(p2_es, "ofull", (128, E), F32)
    onorm = SB(p2_es, "onorm", (128, E), F32)
    sgo = SB(p2_es, "sgo", (128, E), BF16)
    sgof = [SB(p2_es, f"sgof{p}", (128, 8 * 128), BF16) for p in range(2)]
    out_sb = [SB(p2_es, f"out{p}", (128, E), F32) for p in range(2)]
    bn8 = SB(p2_es, "bn8", (128, 48), F32)
    oag = SB(p2_es, "oag", (128, 16), F32)
    orst = SB(p2_es, "orst", (128, 8), F32)
    otmp = SB(p2_es, "otmp", (128, 8), F32)

    s.add("pool", lambda e: cast_w(e, wout_sb[:], wout_d, 1024),
          reads=["p3done"], writes=["woutb"], n_dma=1, tag="ldwout")
    s.add("act", lambda e: dma(e, summ_all.rearrange("p (r f) -> p r f", f=516),
                               ag2_out.rearrange("(r p) f -> p r f", p=128)),
          reads=["ag2", "p3done"], writes=["sall"], n_dma=1, tag="ldsumm")

    s.add("dve", lambda e: e.memset(Sin[:], 0.0), reads=["p3done"],
          writes=["Sin"], tag="sin0")
    for r in range(NCORES - 1):
        base = 516 * r
        def dt_f(e, base=base, r=r):
            e.tensor_scalar_sub(Dt[:], summ_all[:, base + 512: base + 516], 1.0)
            return e.tensor_scalar(Dt[:], Dt[:], msk[:, r: r + 1], 1.0,
                                   ALU.mult, ALU.add)
        s.add("dve", dt_f, reads=["sall"], sreads=["msk"], writes=["Dt"],
              tag=f"dt{r}", force=True)
        def comb(e, base=base, r=r):
            last = None
            for kt in range(4):
                sl = slice(128 * kt, 128 * (kt + 1))
                e.tensor_scalar_mul(onorm[:, sl], summ_all[:, base + 128 * kt:
                                                           base + 128 * (kt + 1)],
                                    msk[:, r: r + 1])
                last = e.scalar_tensor_tensor(Sin[:, sl], Sin[:, sl],
                                              Dt[:, kt: kt + 1], onorm[:, sl],
                                              ALU.mult, ALU.add)
            return last
        s.add("dve", comb, reads=["sall", "Sin"], sreads=["Dt", "msk"],
              writes=["Sin"], tag=f"cb{r}", force=True)
    s.add("dve", lambda e: e.tensor_copy(Sin_b[:], Sin[:]), reads=["Sin"],
          writes=["Sinb"], tag="sinb")

    def p2_tile(i):
        p = i % 2
        def gate(n):
            s.enabled = (phase_limit >= 4) and (p2_parts >= n)
        gate(1)
        s.add("sp", lambda e: dma(e, x2[p][:], x_d[C * i: C * (i + 1), :]),
              reads=["p3done"], writes=[f"x2_{p}"], n_dma=1, tag=f"lx2{i}")
        def lds(e):
            return [dma(e, opre2[p][:], op_s[C * i: C * (i + 1), :]),
                    dma(e, qg2[p][:], qg_s[C * i: C * (i + 1), :]),
                    dma(e, sg2[p][:], sg_s[C * i: C * (i + 1), :]),
                    dma(e, ffn2[p][:], ffn_s[C * i: C * (i + 1), :])]
        s.add("act", lds, reads=["p3done"],
              writes=[f"opre2_{p}", f"qg2_{p}", f"sg2_{p}", f"ffn2_{p}"],
              n_dma=4, tag=f"lds{i}")
        gate(2)
        corr_pos = lambda h: (h % 2, 128 * (h // 2))
        def corr(e):
            last = None
            for h in (0, 2, 4, 6, 1, 3, 5, 7):
                hb = 64 * (h % 2)
                ht = 128 * (h // 2)
                ob, oc = corr_pos(h)
                last = nc.tensor.matmul(BANK(ob, 128, oc),
                                        qg2[p][hb:hb + 64, ht:ht + 128],
                                        Sin_b[hb:hb + 64, ht:ht + 128],
                                        start=True, stop=True)
            return last
        s.add("pe", corr, reads=[f"qg2_{p}", "Sinb", "psb0", "psb1"],
              writes=["psb0", "psb1"], tag=f"cr{i}", cost=1400)
        gate(2.5)
        def of_f(e):
            last = None
            for h in range(H):
                ob, oc = corr_pos(h)
                last = e.tensor_add(ofull[:, 128 * h: 128 * (h + 1)],
                                    opre2[p][:, 128 * h: 128 * (h + 1)],
                                    BANK(ob, 128, oc))
            return last
        s.add("dve", of_f,
              reads=["psb0", "psb1", f"opre2_{p}"],
              writes=["ofull", "psb0", "psb1"], tag=f"of{i}", force=True, cost=1400)
        gate(3)
        def lno(e):
            last = None
            for h in range(H):
                last = e.bn_stats(bn8[:, 6 * h: 6 * (h + 1)],
                                  ofull[:, 128 * h: 128 * (h + 1)])
            return last
        s.add("dve", lno, reads=["ofull"], writes=["bn8"], tag=f"lno{i}", force=True)
        def lna(e):
            last = None
            for h in range(H):
                last = e.bn_aggr(oag[:, 2 * h: 2 * (h + 1)], bn8[:, 6 * h: 6 * (h + 1)])
            return last
        s.add("dve", lna, reads=["bn8"], writes=["oag"], tag=f"lna{i}", force=True)
        gate(4)
        def ors(e):
            e.activation(otmp[:], oag.rearrange("p (h two) -> p h two", two=2)[:, :, 1],
                         AF.Ln, bias=K(0))
            return e.activation(orst[:], otmp[:], AF.Exp, scale=K(2))
        s.add("act", ors, reads=["oag", "konst"], writes=["orst"], tag=f"ors{i}",
              force=True, tset="E", cost=400)
        gate(5)
        def nrm(e):
            last = None
            for h in range(H):
                last = e.tensor_scalar(
                    onorm[:, 128 * h: 128 * (h + 1)],
                    ofull[:, 128 * h: 128 * (h + 1)],
                    oag[:, 2 * h: 2 * h + 1], orst[:, h: h + 1],
                    ALU.subtract, ALU.mult)
            return last
        s.add("dve", nrm, reads=["ofull"], sreads=["oag", "orst"],
              writes=["onorm2"], tag=f"nr{i}")
        s.add("dve", lambda e: e.tensor_mul(sgo[:], onorm[:], sg2[p][:]),
              reads=["onorm2", f"sg2_{p}"], writes=["sgo"], tag=f"sgo{i}")
        gate(6)
        def trs(e):
            last = None
            for kt in range(8):
                last = nc.tensor.transpose(
                    P[:, 64 * kt: 64 * (kt + 1)].bitcast(BF16),
                    sgo[:, 128 * kt: 128 * (kt + 1)], identb[:])
            return last
        s.add("pe", trs, reads=["sgo", "identb", "psb0"], writes=["psb0"],
              tag=f"trs{i}", cost=2300)
        def evs(e):
            return e.copy(sgof[p][:], P[:, 0:512].bitcast(BF16))
        s.add("act", evs, reads=["psb0"], writes=[f"sgof{p}", "psb0"], tag=f"evs{i}")
        gate(7)
        def wo(e):
            last = None
            for half, bank in ((0, 2), (1, 3)):
                for kt in range(8):
                    last = nc.tensor.matmul(
                        BANK(bank), sgof[p][:, 128 * kt: 128 * (kt + 1)],
                        wout_sb[:, kt * 1024 + 512 * half: kt * 1024 + 512 * (half + 1)],
                        start=(kt == 0), stop=(kt == 7))
            return last
        s.add("pe", wo, reads=[f"sgof{p}", "woutb", "psb2", "psb3"],
              writes=["psb2", "psb3"], tag=f"wo{i}", cost=4400)
        gate(8)
        def fin(e):
            e.tensor_add(out_sb[p][:, 0:512], BANK(2), x2[p][:, 0:512])
            e.tensor_add(out_sb[p][:, 512:1024], BANK(3), x2[p][:, 512:1024])
            e.tensor_add(out_sb[p][:, 0:512], out_sb[p][:, 0:512], ffn2[p][:, 0:512])
            return e.tensor_add(out_sb[p][:, 512:1024], out_sb[p][:, 512:1024],
                                ffn2[p][:, 512:1024])
        s.add("dve", fin, reads=["psb2", "psb3", f"x2_{p}", f"ffn2_{p}"],
              writes=[f"out{p}", "psb2", "psb3"], tag=f"fin{i}")
        gate(9)
        s.add("pool", lambda e: dma(e, y_d[C * i: C * (i + 1), :], out_sb[p][:]),
              reads=[f"out{p}"], writes=[f"ydma{i}"], n_dma=1, tag=f"sty{i}")
        s.enabled = phase_limit >= 4

    for i in range(NT):
        p2_tile(i)

    if taps:
        s.enabled = True
        tap_list = [("t_xq", xq_s, [SPAN, E], BF16, "none"),
                    ("t_sg", sg_s, [SPAN, E], BF16, "none"),
                    ("t_op", op_s, [SPAN, E], BF16, "none"),
                    ("t_qg", qg_s, [SPAN, 512], BF16, "none"),
                    ("t_ffn", ffn_s, [SPAN, E], BF16, "none"),
                    ("t_ag2i", ag2_in, [128, 516], F32, "none"),
                    ("t_gwg", g_wg, [NCORES * E, 352], BF16, "none"),
                    ("t_gwd", g_wd, [NCORES * I_FFN, 128], BF16, "none"),
                    ("t_ag2o", ag2_out, [NCORES * 128, 516], F32, "none")]
        for nm, srcd, shp, dt, _ in tap_list:
            td = nc.dram_tensor(nm, shp, dt, kind="ExternalOutput").ap()
            def cp(e, td=td, srcd=srcd):
                return dma(e, td, srcd)
            s.add("pool", cp, reads=["p3done", "ag2"], writes=[nm], n_dma=1, tag=nm)
        t_sin = nc.dram_tensor("t_sin", [128, 512], F32, kind="ExternalOutput").ap()
        s.add("pool", lambda e: dma(e, t_sin, Sin[:]), reads=["Sin"],
              writes=["t_sin"], n_dma=1, tag="t_sin")

    s.enabled = True
    s.add("pool", lambda e: e.nop(),
          reads=([f"ydma{i}" for i in range(NT)] if (phase_limit >= 4 and p2_parts >= 9) else []),
          writes=list(s.last_writer.keys()), tag="fence_out")

    s.finalize()
    with nc.Block() as block:
        s.emit(nc, block)
    p2_es.close()
    main_es.close()
    return nc


_NC_CACHE = None


def kernel(**inputs):
    import ml_dtypes
    BF = ml_dtypes.bfloat16
    F = np.float32
    g = lambda k: np.ascontiguousarray(np.asarray(inputs[k], dtype=F))
    x = g("L0_attn_x").reshape(16384, E)
    conv_w = g("L0_attn_conv_w")
    wq = g("L0_attn_wq"); wkg = g("L0_attn_wkg"); wv = g("L0_attn_wv")
    wg = g("L0_attn_wg"); bg = g("L0_attn_bg"); wout = g("L0_attn_wout")
    aug = g("L0_attn_aug_balance")
    w_gate = g("L0_ffn_w_gate"); w_up = g("L0_ffn_w_up"); w_down = g("L0_ffn_w_down")

    tri = np.triu(np.ones((128, 128), dtype=F))
    hones = np.zeros((128, 2), dtype=BF)
    hones[0:64, 0] = 1; hones[64:128, 1] = 1
    identb = np.eye(128, dtype=BF)
    cw_r = np.ascontiguousarray(conv_w.reshape(8, 128, 4).transpose(1, 0, 2).reshape(128, 32))
    aug_r = np.ascontiguousarray(aug.reshape(4, 128).T)
    bg1 = bg.reshape(1, E)

    global _NC_CACHE
    if _NC_CACHE is None:
        _NC_CACHE = build_program()
    nc = _NC_CACHE

    in_maps = []
    for j in range(NCORES):
        t0 = j * SPAN
        xh = np.zeros((16, E), dtype=F)
        if j > 0:
            xh[13:16] = x[t0 - 3: t0]
        mskj = np.zeros((128, NCORES), dtype=F)
        mskj[:, :j] = 1.0
        in_maps.append(dict(
            x=np.ascontiguousarray(x[t0: t0 + SPAN]),
            xh=xh, msk=mskj, tri=tri, hones=hones, identb=identb,
            cw=cw_r, augr=aug_r, bg=bg1,
            wq=wq, wkg=wkg, wv=wv, wg=wg, wout=wout,
            wgsh=np.ascontiguousarray(w_gate[352 * j: 352 * (j + 1)]),
            wush=np.ascontiguousarray(w_up[352 * j: 352 * (j + 1)]),
            wdsh=np.ascontiguousarray(w_down[128 * j: 128 * (j + 1)]),
        ))
    res = run_bass_kernel_spmd(nc, in_maps, core_ids=list(range(NCORES)))
    out = np.concatenate([np.asarray(r["y"]) for r in res.results], axis=0)
    return out.reshape(1, 16384, E).astype(np.float32)


# version 26 (final)
# speedup vs baseline: 1.2890x; 1.0088x over previous
import numpy as np
from contextlib import ExitStack

import concourse.bass as bass
import concourse.mybir as mybir
from concourse.bass_utils import run_bass_kernel_spmd

F32 = mybir.dt.float32
BF16 = mybir.dt.bfloat16
AF = mybir.ActivationFunctionType
ALU = mybir.AluOpType
AX = mybir.AxisListType

NCORES = 8
SPAN = 2048
C = 128
NT = SPAN // C
E = 1024
H = 8
DK = 64
DV = 128
I_FFN = 2816
LN16 = 1.0 / 16.0
LOG0125 = float(np.log(0.125))

ENG_ENGINES = ("sp", "act", "dve", "pe", "pool")
QUEUE_OF_ENGINE = {"sp": "qsp", "act": "qact", "pool": "qpool"}
NSEM_POOL = 16


class _Op:
    __slots__ = ("idx", "engine", "fn", "reads", "writes", "sreads", "n_dma",
                 "tag", "deps", "seq", "is_dma", "is_coll", "force", "queue",
                 "cost", "tset", "order", "prio")

    def __init__(self):
        self.deps = {}


class Sched:

    def __init__(self):
        self.ops = []
        self.last_writer = {}
        self.readers = {}
        self.enabled = True

    def add(self, engine, fn, reads=(), writes=(), sreads=(), n_dma=0,
            tag="", force=False, is_coll=False, cost=None, tset=None):
        if not self.enabled:
            return -1
        op = _Op()
        op.cost = cost if cost is not None else (1500 if n_dma else 500)
        op.tset = tset
        import builtins
        op.prio = getattr(builtins, 'op_prio_base', [0])[0] + len(self.ops)
        op.idx = len(self.ops)
        op.engine = engine
        op.fn = fn
        op.reads = tuple(reads) + tuple(sreads)
        op.sreads = tuple(sreads)
        op.writes = tuple(writes)
        op.n_dma = n_dma
        op.tag = tag
        op.is_dma = n_dma > 0
        op.is_coll = is_coll
        op.force = force
        deps = set()
        for r in op.reads:
            w = self.last_writer.get(r)
            if w is not None:
                deps.add(w)
        for w_ in op.writes:
            pw = self.last_writer.get(w_)
            if pw is not None:
                deps.add(pw)
            for rd in self.readers.get(w_, ()):
                deps.add(rd)
        deps.discard(op.idx)
        op.deps = deps
        for r in op.reads:
            self.readers.setdefault(r, []).append(op.idx)
        for w_ in op.writes:
            self.last_writer[w_] = op.idx
            self.readers[w_] = []
        self.ops.append(op)
        return op.idx

    def schedule(self):
        import heapq
        n = len(self.ops)
        succs = [[] for _ in range(n)]
        ndeps = [0] * n
        for op in self.ops:
            ndeps[op.idx] = len(op.deps)
            for d in op.deps:
                succs[d].append(op.idx)
        ready = {e: [] for e in ENG_ENGINES}
        for op in self.ops:
            if ndeps[op.idx] == 0:
                heapq.heappush(ready[op.engine], (op.prio, op.idx))
        eng_free = {e: 0.0 for e in ENG_ENGINES}
        act_set = [None]
        done_t = [0.0] * n
        events = []
        order = {e: [] for e in ENG_ENGINES}
        scheduled = 0
        while scheduled < n:
            progressed = False
            for e in ENG_ENGINES:
                if not ready[e]:
                    continue
                cand = heapq.nsmallest(40, ready[e])
                best_key, best_c = None, None
                for c in cand:
                    opc = self.ops[c[1]]
                    stc = max(eng_free[e],
                              max((done_t[d] for d in opc.deps), default=0.0))
                    if (e == "act" and act_set[0] is not None
                            and opc.tset is not None and opc.tset != act_set[0]):
                        stc += 1300
                    key = (stc, c[0])
                    if best_key is None or key < best_key:
                        best_key, best_c = key, c
                ready[e].remove(best_c)
                heapq.heapify(ready[e])
                pick = best_c[1]
                op = self.ops[pick]
                start = max(eng_free[e],
                            max((done_t[d] for d in op.deps), default=0.0))
                cost = op.cost
                if e == "act" and op.tset is not None and op.tset != act_set[0]:
                    cost += 1300
                    act_set[0] = op.tset
                fin = start + cost
                eng_free[e] = fin
                done_t[pick] = fin
                order[e].append(pick)
                scheduled += 1
                progressed = True
                for s_ in succs[pick]:
                    ndeps[s_] -= 1
                    if ndeps[s_] == 0:
                        heapq.heappush(ready[self.ops[s_].engine],
                                       (self.ops[s_].prio, s_))
            if not progressed:
                raise RuntimeError("scheduler stuck (cyclic deps?)")
        for e in ENG_ENGINES:
            for pos, oi in enumerate(order[e]):
                self.ops[oi].order = pos
        self.makespan = max(done_t) if n else 0.0

    def finalize(self):
        self.schedule()
        self.eng_seq = {e: [] for e in ENG_ENGINES}
        self.q_count = {}
        self.eng_order = {e: sorted((o for o in self.ops if o.engine == e),
                                    key=lambda o: o.order)
                          for e in ENG_ENGINES}
        qn = {}
        ncoll = 0
        for e in ENG_ENGINES:
            for op in self.eng_order[e]:
                if op.is_dma:
                    q = QUEUE_OF_ENGINE[op.engine]
                    nq = qn.get(q, 0)
                    qn[q] = nq + 1
                    slot = nq % NSEM_POOL
                    key = (q, slot)
                    base = self.q_count.get(key, 0)
                    self.q_count[key] = base + op.n_dma
                    op.queue = (q, slot, (base + op.n_dma) * 16)
                elif op.is_coll:
                    op.queue = ("coll", ncoll, 1)
                    ncoll += 1
                else:
                    op.queue = None
                    self.eng_seq[op.engine].append(op.idx)
                    op.seq = len(self.eng_seq[op.engine])

        done = [False] * len(self.ops)
        heads = {e: 0 for e in ENG_ENGINES}
        streams = self.eng_order
        progress = True
        ndone = 0
        while progress:
            progress = False
            for e in ENG_ENGINES:
                while heads[e] < len(streams[e]):
                    op = streams[e][heads[e]]
                    if all(done[d] for d in op.deps):
                        done[op.idx] = True
                        heads[e] += 1
                        ndone += 1
                        progress = True
                    else:
                        break
        if ndone != len(self.ops):
            stuck = [o for o in self.ops if not done[o.idx]][:5]
            raise RuntimeError("deadlock in op graph; first stuck: " +
                               ", ".join(f"{o.tag}@{o.engine}" for o in stuck))

    def emit(self, nc, block):
        es = ExitStack()
        eng_sem = {e: es.enter_context(nc.semaphore(f"es_{e}")) for e in ENG_ENGINES}
        pool_sems = {}
        for q in ("qsp", "qact", "qpool"):
            pool_sems[q] = [es.enter_context(nc.semaphore(f"{q}_{i}"))
                            for i in range(NSEM_POOL)]
        n_coll = sum(1 for o in self.ops if o.is_coll)
        coll_sems = [es.enter_context(nc.semaphore(f"coll{i}"))
                     for i in range(max(1, n_coll))]
        self._es = es
        sched = self

        def dep_points(op):
            pts = {}
            for d in sorted(op.deps):
                p = sched.ops[d]
                if p.is_dma:
                    q, slot, val = p.queue
                    key = ("q", q, slot)
                    pts[key] = max(pts.get(key, 0), val)
                elif p.is_coll:
                    _, cidx, val = p.queue
                    pts[("coll", cidx)] = max(pts.get(("coll", cidx), 0), val)
                else:
                    key = ("e", p.engine)
                    pts[key] = max(pts.get(key, 0), p.seq)
            return pts

        waited = {e: {} for e in ENG_ENGINES}

        def run_stream(eng_name, eng):
            for op in sched.eng_order[eng_name]:
                for key, val in sorted(dep_points(op).items(), key=str):
                    if waited[eng_name].get(key, 0) >= val:
                        continue
                    if key[0] == "q":
                        sem = pool_sems[key[1]][key[2]]
                    elif key[0] == "coll":
                        sem = coll_sems[key[1]]
                    else:
                        sem = eng_sem[key[1]]
                    eng.wait_ge(sem, val)
                    waited[eng_name][key] = val
                if op.is_dma:
                    q, slot, _ = op.queue
                    insts = op.fn(eng)
                    if not isinstance(insts, (list, tuple)):
                        insts = [insts]
                    assert len(insts) == op.n_dma, f"{op.tag}: n_dma mismatch {len(insts)} != {op.n_dma}"
                    for ins in insts:
                        ins.then_inc(pool_sems[q][slot], 16)
                elif op.is_coll:
                    ins = op.fn(eng)
                    ins.then_inc(coll_sems[op.queue[1]])
                else:
                    ins = op.fn(eng)
                    assert ins is not None, f"{op.tag}: fn returned None"
                    ins.then_inc(eng_sem[op.engine], 1)

        @block.sync
        def _(sync):
            run_stream("sp", sync)

        @block.scalar
        def _(scalar):
            run_stream("act", scalar)

        @block.vector
        def _(vector):
            run_stream("dve", vector)

        @block.tensor
        def _(tensor):
            run_stream("pe", tensor)

        @block.gpsimd
        def _(gpsimd):
            run_stream("pool", gpsimd)


def build_program(phase_limit=4, p2_parts=9, taps=False):
    nc = bass.Bass("TRN2", target_bir_lowering=False)
    s = Sched()
    op_prio_base = [0]
    import builtins
    builtins.op_prio_base = op_prio_base

    din = {}
    def IN(name, shape, dt=F32):
        din[name] = nc.dram_tensor(name, list(shape), dt, kind="ExternalInput").ap()
        return din[name]

    x_d = IN("x", (SPAN, E))
    xh_d = IN("xh", (16, E))
    msk_d = IN("msk", (128, NCORES))
    tri_d = IN("tri", (128, 128))
    hones_d = IN("hones", (128, 2), BF16)
    identb_d = IN("identb", (128, 128), BF16)
    cw_d = IN("cw", (128, 32))
    aug_d = IN("augr", (128, 4))
    bg_d = IN("bg", (1, E))
    wq_d = IN("wq", (E, 512))
    wkg_d = IN("wkg", (E, 512))
    wv_d = IN("wv", (E, E))
    wg_d = IN("wg", (E, E))
    wout_d = IN("wout", (E, E))
    wgsh_d = IN("wgsh", (I_FFN // 8, E))
    wush_d = IN("wush", (I_FFN // 8, E))
    wdsh_d = IN("wdsh", (E // 8, I_FFN))

    y_d = nc.dram_tensor("y", [SPAN, E], F32, kind="ExternalOutput").ap()

    xq_s = nc.dram_tensor("xq_s", [SPAN, E], BF16).ap()
    sg_s = nc.dram_tensor("sg_s", [SPAN, E], BF16).ap()
    op_s = nc.dram_tensor("op_s", [SPAN, E], BF16).ap()
    qg_s = nc.dram_tensor("qg_s", [SPAN, 512], BF16).ap()
    ffn_s = nc.dram_tensor("ffn_s", [SPAN, E], BF16).ap()
    wgt_st = nc.dram_tensor("wgt_st", [E, 352], BF16).ap()
    wut_st = nc.dram_tensor("wut_st", [E, 352], BF16).ap()
    wdt_st = nc.dram_tensor("wdt_st", [I_FFN, 128], BF16).ap()
    g_wg = nc.dram_tensor("g_wg", [NCORES * E, 352], BF16, addr_space="Shared").ap()
    g_wu = nc.dram_tensor("g_wu", [NCORES * E, 352], BF16, addr_space="Shared").ap()
    g_wd = nc.dram_tensor("g_wd", [NCORES * I_FFN, 128], BF16, addr_space="Shared").ap()
    ag2_in = nc.dram_tensor("ag2_in", [128, 516], F32).ap()
    ag2_out = nc.dram_tensor("ag2_out", [NCORES * 128, 516], F32, addr_space="Shared").ap()

    main_es = ExitStack()

    def SB(stack, name, shape, dt):
        return stack.enter_context(nc.sbuf_tensor("s_" + name, list(shape), dt))

    P = main_es.enter_context(nc.psum_tensor("P", [128, 4096], F32))
    def BANK(b, cols=512, off=0):
        return P[:, 512 * b + off: 512 * b + off + cols]

    tri = SB(main_es, "tri", (128, 128), F32)
    hones = SB(main_es, "hones", (128, 2), BF16)
    identb = SB(main_es, "identb", (128, 128), BF16)
    msk = SB(main_es, "msk", (128, NCORES), F32)
    cw = SB(main_es, "cw", (128, 32), F32)
    augr = SB(main_es, "augr", (128, 4), F32)
    bgf = SB(main_es, "bgf", (1, E), F32)
    bgb = SB(main_es, "bgb", (1, E), BF16)
    ones1 = SB(main_es, "ones1", (1, 128), BF16)
    S_sb = SB(main_es, "S_sb", (128, 512), F32)
    S_b16 = SB(main_es, "S_b16", (128, 512), BF16)
    Brun = SB(main_es, "Brun", (128, 4), F32)
    summ = SB(main_es, "summ", (128, 516), F32)
    konst = SB(main_es, "konst", (128, 8), F32)
    qgt2 = SB(main_es, "qgt2", (128, 512), F32)

    def dma(eng, out, in_):
        return eng.dma_start(out=out, in_=in_)

    s.add("sp", lambda e: [dma(e, tri[:], tri_d), dma(e, hones[:], hones_d),
                           dma(e, identb[:], identb_d), dma(e, msk[:], msk_d),
                           dma(e, cw[:], cw_d), dma(e, augr[:], aug_d),
                           dma(e, bgf[:], bg_d)],
          writes=["tri", "hones", "identb", "msk", "cw", "augr", "bgf"],
          n_dma=7, tag="ld_consts")
    s.add("dve", lambda e: e.tensor_copy(bgb[:], bgf[:]), reads=["bgf"],
          writes=["bgb"], tag="bgb")
    s.add("dve", lambda e: e.memset(ones1[:], 1.0), writes=["ones1"], tag="ones1")
    s.add("dve", lambda e: e.memset(S_sb[:], 0.0), writes=["S"], tag="S0")
    s.add("dve", lambda e: e.memset(S_b16[:], 0.0), writes=["Sb16"], tag="Sb0")
    s.add("dve", lambda e: e.memset(Brun[:], 0.0), writes=["Brun"], tag="Br0")
    KV = {0: 1e-5, 1: 1.0, 2: -0.5, 3: LN16, 4: -LN16, 5: LOG0125, 6: -1.0}
    def kmemset(e):
        last = None
        for i, v in KV.items():
            last = e.memset(konst[:, i:i + 1], v)
        return last
    s.add("dve", kmemset, writes=["konst"], tag="konst")
    K = lambda i: konst[:, i:i + 1]

    p1_es = ExitStack()
    wq_sb = SB(p1_es, "wq_sb", (128, 8 * 512), BF16)
    wkg_sb = SB(p1_es, "wkg_sb", (128, 8 * 512), BF16)
    wv_sb = SB(p1_es, "wv_sb", (128, 8 * 1024), BF16)
    wg_sb = SB(p1_es, "wg_sb", (128, 8 * 1024), BF16)

    def cast_w(eng, dst, src, mcols):
        return dma(eng, dst.rearrange("p (kt m) -> p kt m", m=mcols),
                   src.rearrange("(kt p) m -> p kt m", p=128))
    s.add("pool", lambda e: [cast_w(e, wq_sb[:], wq_d, 512),
                             cast_w(e, wkg_sb[:], wkg_d, 512)],
          writes=["wq", "wkg"], n_dma=2, tag="ld_wqkg")
    s.add("pool", lambda e: [cast_w(e, wv_sb[:], wv_d, 1024),
                             cast_w(e, wg_sb[:], wg_d, 1024)],
          writes=["wv", "wg"], n_dma=2, tag="ld_wvg")

    wsh = SB(p1_es, "wsh", (128, 3 * 1024), F32)
    wshq = SB(p1_es, "wshq", (128, 3 * 1024), BF16)
    qt_bmax = SB(p1_es, "qt_bmax", (128, 192), F32)
    qt_sc = SB(p1_es, "qt_sc", (128, 192), F32)
    qt_si = SB(p1_es, "qt_si", (128, 192), F32)
    qt_xa = SB(p1_es, "qt_xa", (128, 3 * 1024), F32)
    qt_t1 = SB(p1_es, "qt_t1", (128, 3 * 1024), F32)
    stg = SB(p1_es, "stg", (128, 352), BF16)

    def _chain_ops(bufset, src_r3, xa, t1, gt2, si, sc, dst_r3, cols, blk,
                   rds, wr, cost_scale=1.0):
        MAGIC = float(2 ** 23)
        cd = cols
        def f_chain(e):
            e.tensor_tensor(src_r3(xa), src_r3(xa), si, ALU.mult)
            xd, td = xa[:, 0:cd], t1[:, 0:cd]
            e.tensor_scalar(td, xd, 2.0, MAGIC, ALU.mult, ALU.add)
            e.tensor_scalar(td, td, MAGIC, 0.5, ALU.subtract, ALU.mult)
            e.tensor_scalar_min(td, td, 2.0)
            e.scalar_tensor_tensor(td, xd, 2.5, td, ALU.is_gt, ALU.add)
            e.scalar_tensor_tensor(td, xd, 3.5, td, ALU.is_gt, ALU.add)
            e.scalar_tensor_tensor(td, xd, 5.0, td, ALU.is_gt, ALU.add)
            return e.scalar_tensor_tensor(td, xd, 5.0, td, ALU.is_gt, ALU.add)
        s.add("dve", f_chain, reads=[bufset + "_xa"], sreads=[bufset + "_si"],
              writes=[bufset + "_t1", bufset + "_xa"], tag=bufset + "q1",
              cost=int((cols + cd * 6.0) * cost_scale + 500))
        if cd < cols:
            def g_chain(e):
                xg, tg, t2g = xa[:, cd:cols], t1[:, cd:cols], gt2[:, 0:cols - cd]
                e.tensor_scalar(tg, xg, 2.0, MAGIC, ALU.mult, ALU.add)
                e.tensor_scalar(tg, tg, MAGIC, 0.5, ALU.subtract, ALU.mult)
                e.tensor_scalar_min(tg, tg, 2.0)
                for thr, rep in ((2.5, 1), (3.5, 1), (5.0, 2)):
                    e.tensor_single_scalar(t2g, xg, thr, ALU.is_gt)
                    for _ in range(rep):
                        e.tensor_add(tg, tg, t2g)
                return e.tensor_copy(gt2[:, 0:1], tg[:, 0:1])
            s.add("pool", g_chain, reads=[bufset + "_xa"],
                  writes=[bufset + "_t1g", bufset + "_gt2"], tag=bufset + "q1g",
                  cost=int((cols - cd) * 11.0 + 800))
        return cd

    def emit_quant(src_ap, dst_ap, nblk, cols, bufset, xa, t1, bmax, sc, si,
                   rd, wr, blk=16):
        rds_ = [rd]
        r3 = lambda ap: ap.rearrange("p (b i) -> p b i", i=blk)
        bca = lambda ap: ap[:, :].unsqueeze(2).to_broadcast([128, nblk, blk])
        def f_scale(e):
            e.tensor_reduce(bmax, r3(src_ap), AX.X, ALU.max, apply_absolute_value=True)
            return e.tensor_scalar(sc, bmax, 1.0 / 6.0, 1e-12, ALU.mult, ALU.max)
        s.add("dve", f_scale, sreads=rds_, writes=[bufset + "_sc"], tag=bufset + "q0",
              force=True, cost=int(cols * 1.4 + 150))
        def f_recip(e):
            return e.reciprocal(si, sc)
        s.add("dve", f_recip, sreads=[bufset + "_sc"], writes=[bufset + "_si"],
              tag=bufset + "qr", cost=300)
        def f_abs(e):
            return e.activation(xa, src_ap, AF.Abs)
        s.add("act", f_abs, reads=rds_, writes=[bufset + "_xa"], tag=bufset + "qa",
              cost=int(cols * 0.9 + 200))
        def xa_r3(ap):
            return ap.rearrange("p (b i) -> p b i", i=blk)
        cd = _chain_ops(bufset, xa_r3, xa, t1, qgt2, bca(si), sc, None, cols, blk,
                        rds_, wr)
        def f_sign(e):
            return e.activation(xa, src_ap, AF.Sign)
        s.add("act", f_sign, reads=rds_, writes=[bufset + "_xa"], tag=bufset + "qs",
              cost=int(cols * 0.9 + 200))
        def f_fin(e):
            e.tensor_mul(t1, t1, xa)
            return e.tensor_tensor(r3(dst_ap), r3(t1), bca(sc), ALU.mult)
        s.add("dve", f_fin, reads=[bufset + "_xa", bufset + "_t1g"],
              sreads=[bufset + "_sc", bufset + "_t1"],
              writes=[wr, bufset + "_t1"], tag=bufset + "q2",
              cost=int(cols * 2.2 + 300))

    def emit_quant2(src_ap, dst_ap, nblk, cols, bufset, xa, t1, bmax, sc, si,
                   rds, wr, blk=16):
        rds_ = rds
        r3 = lambda ap: ap.rearrange("p (b i) -> p b i", i=blk)
        bca = lambda ap: ap[:, :].unsqueeze(2).to_broadcast([128, nblk, blk])
        def f_scale(e):
            e.tensor_reduce(bmax, r3(src_ap), AX.X, ALU.max, apply_absolute_value=True)
            return e.tensor_scalar(sc, bmax, 1.0 / 6.0, 1e-12, ALU.mult, ALU.max)
        s.add("dve", f_scale, sreads=rds_, writes=[bufset + "_sc"], tag=bufset + "q0",
              force=True, cost=int(cols * 1.4 + 150))
        def f_recip(e):
            return e.reciprocal(si, sc)
        s.add("dve", f_recip, sreads=[bufset + "_sc"], writes=[bufset + "_si"],
              tag=bufset + "qr", cost=300)
        def f_abs(e):
            return e.activation(xa, src_ap, AF.Abs)
        s.add("act", f_abs, reads=rds_, writes=[bufset + "_xa"], tag=bufset + "qa",
              cost=int(cols * 0.9 + 200))
        def xa_r3(ap):
            return ap.rearrange("p (b i) -> p b i", i=blk)
        cd = _chain_ops(bufset, xa_r3, xa, t1, qgt2, bca(si), sc, None, cols, blk,
                        rds_, wr)
        def f_sign(e):
            return e.activation(xa, src_ap, AF.Sign)
        s.add("act", f_sign, reads=rds_, writes=[bufset + "_xa"], tag=bufset + "qs",
              cost=int(cols * 0.9 + 200))
        def f_fin(e):
            e.tensor_mul(t1, t1, xa)
            return e.tensor_tensor(r3(dst_ap), r3(t1), bca(sc), ALU.mult)
        s.add("dve", f_fin, reads=[bufset + "_xa", bufset + "_t1g"],
              sreads=[bufset + "_sc", bufset + "_t1"],
              writes=[wr, bufset + "_t1"], tag=bufset + "q2",
              cost=int(cols * 2.2 + 300))

    def prep_gate_like(src_d, stage_d, name):
        s.add("dve", lambda e: e.memset(wsh[:], 0.0), writes=["wsh"], tag=f"{name}z")
        def ld(e):
            return [dma(e, wsh[0:128, 0:1024], src_d[0:128, :]),
                    dma(e, wsh[0:128, 1024:2048], src_d[128:256, :]),
                    dma(e, wsh[0:96, 2048:3072], src_d[256:352, :])]
        s.add("sp", ld, writes=["wsh"], n_dma=3, tag=f"{name}ld")
        emit_quant(wsh[:, 0:3072], wshq[:, 0:3072], 192, 3072, "qw",
                   qt_xa[:, 0:3072], qt_t1[:, 0:3072], qt_bmax[:, 0:192],
                   qt_sc[:, 0:192], qt_si[:, 0:192], "wsh", "wshq")
        pbf = BANK(0, 176, 0).bitcast(BF16)
        for kt in range(8):
            def tr(e, kt=kt):
                last = None
                for pt in range(3):
                    pp = 128 if pt < 2 else 96
                    last = nc.tensor.transpose(
                        pbf[:, 128 * pt: 128 * pt + pp],
                        wshq[0:pp, 1024 * pt + 128 * kt: 1024 * pt + 128 * (kt + 1)],
                        identb[0:pp, 0:pp])
                return last
            s.add("pe", tr, reads=["wshq", "identb", "psb0"], writes=["psb0"],
                  tag=f"{name}t{kt}")
            def ev(e, kt=kt):
                return e.copy(stg[:, 0:352], pbf[:, 0:352])
            s.add("act", ev, reads=["psb0"], writes=["stg", "psb0"], tag=f"{name}e{kt}")
            def st(e, kt=kt, stage_d=stage_d):
                return dma(e, stage_d[128 * kt: 128 * (kt + 1), :], stg[:, 0:352])
            s.add("pool", st, reads=["stg"], writes=[f"{name}st{kt}"], n_dma=1,
                  tag=f"{name}s{kt}")

    op_prio_base[0] = 100000
    prep_gate_like(wgsh_d, wgt_st, "pwg")
    prep_gate_like(wush_d, wut_st, "pwu")

    s.add("sp", lambda e: dma(e, wsh[:, 0:2816], wdsh_d), writes=["wsh"],
          n_dma=1, tag="pwdld")
    emit_quant(wsh[:, 0:2816], wshq[:, 0:2816], 176, 2816, "qw",
               qt_xa[:, 0:2816], qt_t1[:, 0:2816], qt_bmax[:, 0:176],
               qt_sc[:, 0:176], qt_si[:, 0:176], "wsh", "wshq")
    pbf2 = BANK(0, 64, 0).bitcast(BF16)
    for it in range(22):
        def tr(e, it=it):
            return nc.tensor.transpose(pbf2[:, 0:128],
                                       wshq[:, 128 * it: 128 * (it + 1)],
                                       identb[:])
        s.add("pe", tr, reads=["wshq", "identb", "psb0"], writes=["psb0"],
              tag=f"pwdt{it}")
        def ev(e, it=it):
            return e.copy(stg[:, 0:128], pbf2[:, 0:128])
        s.add("act", ev, reads=["psb0"], writes=["stg", "psb0"], tag=f"pwde{it}")
        def st(e, it=it):
            return dma(e, wdt_st[128 * it: 128 * (it + 1), :], stg[:, 0:128])
        s.add("pool", st, reads=["stg"], writes=[f"pwdst{it}"], n_dma=1,
              tag=f"pwds{it}")

    def coll(kind_ins, kind_outs, tag, deps, wres):
        def f(e):
            return e.collective_compute(
                "AllGather", ALU.bypass,
                replica_groups=[list(range(NCORES))],
                ins=[kind_ins.opt()], outs=[kind_outs.opt()])
        s.add("pool", f, reads=deps, writes=[wres], is_coll=True, tag=tag)
    coll(wgt_st, g_wg, "coll_wg", [f"pwgst{k}" for k in range(8)], "gwg")
    coll(wut_st, g_wu, "coll_wu", [f"pwust{k}" for k in range(8)], "gwu")
    coll(wdt_st, g_wd, "coll_wd", [f"pwdst{k}" for k in range(22)], "gwd")
    op_prio_base[0] = 0

    x_sb = [SB(p1_es, f"x{p}", (128, E), F32) for p in range(2)]
    bn6 = [SB(p1_es, "bn6" + str(_p), (128, 12), F32) for _p in range(2)]
    bnag = [SB(p1_es, "bnag" + str(_p), (128, 2), F32) for _p in range(2)]
    rstd = [SB(p1_es, "rstd" + str(_p), (128, 1), F32) for _p in range(2)]
    lntmp = SB(p1_es, "lntmp", (128, 1), F32)
    xn = SB(p1_es, "xn", (128, E), F32)
    xnb = SB(p1_es, "xnb", (128, E), BF16)
    XNF_P = 144
    q_bmax = SB(p1_es, "q_bmax", (128, 64), F32)
    q_sc = SB(p1_es, "q_sc", (128, 64), F32)
    q_si = SB(p1_es, "q_si", (128, 64), F32)
    q_xa = SB(p1_es, "q_xa", (128, E), F32)
    q_t1 = SB(p1_es, "q_t1", (128, E), F32)
    xq_sb = [SB(p1_es, f"xq{p}", (128, E), BF16) for p in range(2)]
    xnf = [SB(p1_es, f"xnf{p}", (128, 8 * XNF_P), BF16) for p in range(2)]
    cacc = SB(p1_es, "cacc", (128, 8 * 128), F32)
    xc = [SB(p1_es, "xc" + str(_p), (128, 8 * 128), BF16) for _p in range(2)]
    q_fm = SB(p1_es, "q_fm", (128, 512), F32)
    kg_fm = SB(p1_es, "kg_fm", (128, 512), F32)
    lsb = SB(p1_es, "lsb", (128, 512), F32)
    bcum = SB(p1_es, "bcum", (128, 512), F32)
    epos = SB(p1_es, "epos", (128, 512), F32)
    eneg = SB(p1_es, "eneg", (128, 512), F32)
    ekb = SB(p1_es, "ekb", (128, 512), F32)
    kfp = SB(p1_es, "kfp", (128, 512), F32)
    ktf = SB(p1_es, "ktf", (128, 512), F32)
    qt_b = SB(p1_es, "qt_b", (128, 512), BF16)
    kt_b = [SB(p1_es, "kt_b" + str(_p), (128, 512), BF16) for _p in range(2)]
    kh_b = [SB(p1_es, "kh_b" + str(_p), (128, 512), BF16) for _p in range(2)]
    khn = [SB(p1_es, f"khn{p}", (128, 512), BF16) for p in range(2)]
    qg_b = [SB(p1_es, f"qg{p}", (128, 512), BF16) for p in range(2)]
    ebt = [SB(p1_es, "ebt" + str(_p), (128, 4), F32) for _p in range(2)]
    ebrun = [SB(p1_es, "ebrun" + str(_p), (128, 4), F32) for _p in range(2)]
    m2b = SB(p1_es, "m2b", (128, 512), BF16)
    m1f = SB(p1_es, "m1f", (128, 512), F32)
    augw = [SB(p1_es, f"augw{p}", (128, 8), F32) for p in range(2)]
    v_nat = [SB(p1_es, f"vnat{p}", (128, E), BF16) for p in range(2)]
    g_raw = [SB(p1_es, f"graw{p}", (128, E), F32) for p in range(2)]
    o_acc = [SB(p1_es, f"oacc{p}", (128, E), F32) for p in range(2)]
    opre_b = [SB(p1_es, f"opre{p}", (128, E), BF16) for p in range(2)]
    sg_b = [SB(p1_es, f"sgb{p}", (128, E), BF16) for p in range(2)]
    atm = [SB(p1_es, f"atm{p}", (128, 128), BF16) for p in range(2)]
    augarg = SB(p1_es, "augarg", (128, E), BF16)

    s.add("sp", lambda e: dma(e, x_sb[0][0:16, :], xh_d), writes=["x0"],
          n_dma=1, tag="ld_xh")
    s.add("dve", lambda e: (e.bn_stats(bn6[1][0:16, 0:6], x_sb[0][0:16, 0:512]),
                            e.bn_stats(bn6[1][0:16, 6:12], x_sb[0][0:16, 512:1024]))[-1],
          reads=["x0"], writes=["bn61"], tag="h_bn", force=True)
    s.add("dve", lambda e: e.bn_aggr(bnag[1][0:16, :], bn6[1][0:16, :]),
          reads=["bn61"], writes=["bnag1"], tag="h_ag", force=True)
    s.add("act", lambda e: (e.activation(lntmp[0:16], bnag[1][0:16, 1:2], AF.Ln, bias=konst[0:16, 0:1]),
                            e.activation(rstd[1][0:16], lntmp[0:16], AF.Exp, scale=konst[0:16, 2:3]))[-1],
          reads=["bnag1", "konst"], writes=["rstd1"], tag="h_rstd", force=True,
          tset="E", cost=600)
    s.add("dve", lambda e: e.tensor_scalar(xn[0:16, :], x_sb[0][0:16, :],
                                           bnag[1][0:16, 0:1], rstd[1][0:16, :],
                                           ALU.subtract, ALU.mult),
          reads=["x0"], sreads=["bnag1", "rstd1"], writes=["xn"], tag="h_xn")
    s.add("dve", lambda e: e.tensor_copy(xnb[0:16, :], xn[0:16, :]),
          reads=["xn"], writes=["xnb"], tag="h_xnb")
    def h_tr(e):
        last = None
        for kt in range(8):
            last = nc.tensor.transpose(
                P[:, 64 * kt + 8 * kt: 64 * kt + 8 * kt + 8].bitcast(BF16),
                xnb[0:16, 128 * kt: 128 * (kt + 1)], identb[0:16, 0:16])
        return last
    s.add("pe", h_tr, reads=["xnb", "identb", "psb0", "psb1"],
          writes=["psb0", "psb1"], tag="h_tr")
    def h_ev(e):
        last = None
        for kt in range(8):
            last = e.copy(xnf[1][:, XNF_P * kt: XNF_P * kt + 16],
                          P[:, 72 * kt: 72 * kt + 8].bitcast(BF16))
        return last
    s.add("act", h_ev, reads=["psb0", "psb1"], writes=["xnfd1", "psb0", "psb1"],
          tag="h_ev")
    def h_cp(e):
        src = xnf[1].rearrange("p (k c) -> p k c", c=XNF_P)
        return e.tensor_copy(src[:, :, 141:144], src[:, :, 13:16])
    s.add("dve", h_cp, reads=["xnfd1"], writes=["xnfh1", "xnfd1"], tag="h_cp")

    def p1_tile(i, bn6=bn6, bnag=bnag, rstd=rstd,
                xc=xc,
                kt_b=kt_b, kh_b=kh_b, ebt=ebt, ebrun=ebrun):
        p = i % 2
        pp = 1 - p
        bn6 = bn6[p]; bnag = bnag[p]; rstd = rstd[p]
        xc = xc[p]
        kt_b = kt_b[p]; kh_b = kh_b[p]
        ebt = ebt[p]; ebrun = ebrun[p]
        xr = lambda buf: buf.rearrange("p (k c) -> p k c", c=XNF_P)

        s.add("sp", lambda e: dma(e, x_sb[p][:], x_d[C * i: C * (i + 1), :]),
              writes=[f"x{p}"], n_dma=1, tag=f"ldx{i}")
        s.add("dve", lambda e: (e.bn_stats(bn6[:, 0:6], x_sb[p][:, 0:512]),
                                e.bn_stats(bn6[:, 6:12], x_sb[p][:, 512:1024]))[-1],
              reads=[f"x{p}"], writes=["bn6" + str(p)], tag=f"bn{i}", force=True)
        s.add("dve", lambda e: e.bn_aggr(bnag[:], bn6[:]),
              reads=["bn6" + str(p)], writes=["bnag" + str(p)], tag=f"ag{i}", force=True)
        s.add("act", lambda e: (e.activation(lntmp[:], bnag[:, 1:2], AF.Ln, bias=K(0)),
                                e.activation(rstd[:], lntmp[:], AF.Exp, scale=K(2)))[-1],
              reads=["bnag" + str(p), "konst"], writes=["rstd" + str(p)], tag=f"rstd{i}", force=True,
              tset="E", cost=600)
        s.add("dve", lambda e: e.tensor_scalar(xn[:], x_sb[p][:], bnag[:, 0:1],
                                               rstd[:], ALU.subtract, ALU.mult),
              reads=[f"x{p}"], sreads=["bnag" + str(p), "rstd" + str(p)], writes=["xn"], tag=f"xn{i}")
        s.add("dve", lambda e: e.tensor_copy(xnb[:], xn[:]),
              reads=["xn"], writes=["xnb"], tag=f"xnb{i}")
        def trx(e):
            last = None
            for kt in range(8):
                last = nc.tensor.transpose(
                    P[:, 64 * kt: 64 * (kt + 1)].bitcast(BF16),
                    xnb[:, 128 * kt: 128 * (kt + 1)], identb[:])
            return last
        s.add("pe", trx, reads=["xnb", "identb", "psb0"], writes=["psb0"],
              tag=f"trx{i}", cost=2300)
        def evx(e):
            return e.copy(xr(xnf[p])[:, :, 16:144],
                          P[:, 0:512].bitcast(BF16).rearrange("q (k c) -> q k c", c=128))
        s.add("act", evx, reads=["psb0"], writes=[f"xnfd{p}", "psb0"], tag=f"evx{i}")
        def hcp(e):
            return e.tensor_copy(xr(xnf[p])[:, :, 13:16], xr(xnf[pp])[:, :, 141:144])
        s.add("dve", hcp, reads=[f"xnfd{pp}", f"xnfh{pp}"],
              writes=[f"xnfh{p}"], tag=f"hcp{i}")

        emit_quant(xn[:], xq_sb[p][:], 64, E, "qx",
                   q_xa[:], q_t1[:], q_bmax[:], q_sc[:], q_si[:],
                   "xn", f"xqv{p}")
        s.add("pool", lambda e: dma(e, xq_s[C * i: C * (i + 1), :], xq_sb[p][:]),
              reads=[f"xqv{p}"], writes=[f"xqs{i}"], n_dma=1, tag=f"stxq{i}")

        def conv(e):
            v = xr(xnf[p])
            last = None
            for kt in range(8):
                e.tensor_scalar_mul(cacc[:, 128 * kt: 128 * (kt + 1)],
                                    v[:, kt, 13:141], cw[:, 4 * kt: 4 * kt + 1])
                for d in range(1, 4):
                    last = e.scalar_tensor_tensor(
                        cacc[:, 128 * kt: 128 * (kt + 1)],
                        v[:, kt, 13 + d: 141 + d], cw[:, 4 * kt + d: 4 * kt + d + 1],
                        cacc[:, 128 * kt: 128 * (kt + 1)], ALU.mult, ALU.add)
            return last
        s.add("dve", conv, reads=[f"xnfd{p}", f"xnfh{p}"], sreads=["cw"],
              writes=["cacc"], tag=f"cv{i}", cost=6500)
        s.add("act", lambda e: e.activation(xc[:], cacc[:], AF.Silu),
              reads=["cacc"], writes=["xc" + str(p)], tag=f"cs{i}", tset="S", cost=1100)

        if i > 0:
            emit_aug_sgg(i - 1, pp)

        def qkg(e, bank, wsb, wname):
            last = None
            for mt in range(4):
                for kt in range(8):
                    last = nc.tensor.matmul(
                        BANK(bank, 128, 128 * mt),
                        wsb[:, kt * 512 + mt * 128: kt * 512 + (mt + 1) * 128],
                        xc[:, 128 * kt: 128 * (kt + 1)],
                        start=(kt == 0), stop=(kt == 7))
            return last
        s.add("pe", lambda e: qkg(e, 0, wq_sb, "wq"), reads=["xc" + str(p), "wq"],
              writes=["psb0"], tag=f"qk0_{i}", cost=3800)
        s.add("pe", lambda e: qkg(e, 1, wkg_sb, "wkg"), reads=["xc" + str(p), "wkg"],
              writes=["psb1"], tag=f"qk1_{i}", cost=3800)
        s.add("act", lambda e: (e.copy(q_fm[:], BANK(0)),
                                e.copy(kg_fm[:], BANK(1)))[-1],
              reads=["psb0", "psb1"], writes=["qfm", "kgfm", "psb0", "psb1"],
              tag=f"evqk{i}")

        def vg(e):
            last = None
            for half, bank in ((0, 2), (1, 3)):
                for kt in range(8):
                    last = nc.tensor.matmul(
                        BANK(bank), xc[:, 128 * kt: 128 * (kt + 1)],
                        wv_sb[:, kt * 1024 + 512 * half: kt * 1024 + 512 * (half + 1)],
                        start=(kt == 0), stop=(kt == 7))
            return last
        def gg(e):
            last = None
            for half, bank in ((0, 4), (1, 5)):
                for kt in range(8):
                    nc.tensor.matmul(
                        BANK(bank), xc[:, 128 * kt: 128 * (kt + 1)],
                        wg_sb[:, kt * 1024 + 512 * half: kt * 1024 + 512 * (half + 1)],
                        start=(kt == 0), stop=False)
                last = nc.tensor.matmul(
                    BANK(bank), ones1[:], bgb[:, 512 * half: 512 * (half + 1)],
                    start=False, stop=True)
            return last
        s.add("pe", vg, reads=["xc" + str(p), "wv"], writes=["psb2", "psb3"], tag=f"v{i}",
              cost=4400)
        s.add("pe", gg, reads=["xc" + str(p), "wg", "ones1", "bgb"],
              writes=["psb4", "psb5"], tag=f"g{i}", cost=4800)
        s.add("act", lambda e: (e.copy(v_nat[p][:, 0:512], BANK(2)),
                                e.copy(v_nat[p][:, 512:1024], BANK(3)))[-1],
              reads=["psb2", "psb3"], writes=[f"vnat{p}", "psb2", "psb3"],
              tag=f"evv{i}")
        s.add("act", lambda e: (e.copy(g_raw[p][:, 0:512], BANK(4)),
                                e.copy(g_raw[p][:, 512:1024], BANK(5)))[-1],
              reads=["psb4", "psb5"], writes=[f"graw{p}", "psb4", "psb5"],
              tag=f"evg{i}")

        s.add("act", lambda e: e.activation(ktf[:], kg_fm[:], AF.Abs),
              reads=["kgfm"], writes=["kgabs"], tag=f"ka{i}", cost=600)
        s.add("act", lambda e: (e.activation(epos[:], ktf[:], AF.Exp, scale=K(6)),
                                e.activation(lsb[:], epos[:], AF.Ln, bias=K(1)))[-1],
              reads=["kgabs", "konst"], writes=["lntm", "epos"], tag=f"lse{i}",
              tset="E", cost=1100)
        s.add("dve", lambda e: (e.tensor_single_scalar(eneg[:], kg_fm[:], 0.0, ALU.min),
                                e.tensor_sub(lsb[:], eneg[:], lsb[:]))[-1],
              reads=["kgfm", "lntm"], writes=["ls", "eneg"], tag=f"ls{i}")
        def scan(e):
            last = None
            for kt in range(4):
                sl = slice(128 * kt, 128 * (kt + 1))
                last = e.tensor_tensor_scan(bcum[:, sl], lsb[:, sl], lsb[:, sl],
                                            0.0, ALU.add, ALU.bypass)
            return last
        s.add("dve", scan, sreads=["ls"], writes=["bcum"], tag=f"sc{i}")
        def exps(e):
            e.activation(epos[:], bcum[:], AF.Exp, scale=K(3), bias=K(5))
            e.activation(eneg[:], bcum[:], AF.Exp, scale=K(4))
            e.activation(ekb[:], lsb[:], AF.Exp, scale=K(3))
            bc = bcum.rearrange("p (k c) -> p k c", c=128)
            e.activation(ebt[:], bc[:, :, 127:128], AF.Exp, scale=K(3))
            return e.activation(ebrun[:], Brun[:], AF.Exp, scale=K(3))
        s.add("act", exps, reads=["bcum", "ls", "Brun", "konst"],
              writes=["epos", "eneg", "ek", "ebt" + str(p), "ebrun" + str(p)], tag=f"ex{i}",
              tset="E", cost=2200)
        def kqt(e):
            e.tensor_scalar(kfp[:], ekb[:], -1.0, 1.0, ALU.mult, ALU.add)
            e.tensor_mul(qt_b[:], q_fm[:], epos[:])
            e.tensor_mul(ktf[:], kfp[:], eneg[:])
            e.tensor_copy(kt_b[:], ktf[:])
            eb = ebt[:, :].unsqueeze(2).to_broadcast([128, 4, 128])
            r4 = lambda ap: ap.rearrange("p (k c) -> p k c", c=128)
            return e.tensor_tensor(r4(kh_b[:]), r4(ktf[:]), eb, ALU.mult)
        s.add("dve", kqt, reads=["ek", "epos", "eneg", "qfm"], sreads=["ebt" + str(p)],
              writes=["kfp", "qtb", "ktb" + str(p), "khb" + str(p), "kgabs"], tag=f"kq{i}")
        def qg(e):
            eb = ebrun[:, :].unsqueeze(2).to_broadcast([128, 4, 128])
            r4 = lambda ap: ap.rearrange("p (k c) -> p k c", c=128)
            return e.tensor_tensor(r4(qg_b[p][:]), r4(qt_b[:]), eb, ALU.mult)
        s.add("dve", qg, reads=["qtb"], sreads=["ebrun" + str(p)], writes=[f"qgb{p}"],
              tag=f"qg{i}")
        s.add("pool", lambda e: dma(e, qg_s[C * i: C * (i + 1), :], qg_b[p][:]),
              reads=[f"qgb{p}"], n_dma=1, tag=f"stqg{i}")
        s.add("dve", lambda e: e.tensor_add(Brun[:],
                                            Brun[:],
                                            bcum.rearrange("p (k c) -> p k c", c=128)[:, :, 127]),
              reads=["bcum", "ebrun" + str(p)], writes=["Brun"], tag=f"br{i}")
        def trk(e):
            last = None
            for kt in range(4):
                last = nc.tensor.transpose(
                    P[:, 512 + 64 * kt: 512 + 64 * (kt + 1)].bitcast(BF16),
                    kh_b[:, 128 * kt: 128 * (kt + 1)], identb[:])
            return last
        s.add("pe", trk, reads=["khb" + str(p), "identb", "psb1"], writes=["psb1"],
              tag=f"trk{i}", cost=1200)
        def evk(e):
            return e.copy(khn[p][:], P[:, 512:768].bitcast(BF16))
        s.add("act", evk, reads=["psb1"], writes=[f"khn{p}", "psb1"], tag=f"evk{i}")

        s.add("dve", lambda e: e.tensor_copy(S_b16[:], S_sb[:]),
              reads=["S"], writes=["Sb16"], tag=f"sb16{i}")

        def m2(e):
            e.tensor_mul(m1f[:], q_fm[:], kfp[:])
            ab = augr[:, :].unsqueeze(2).to_broadcast([128, 4, 128])
            r4 = lambda ap: ap.rearrange("p (k c) -> p k c", c=128)
            return e.tensor_tensor(r4(m2b[:]), r4(m1f[:]), ab, ALU.mult)
        s.add("dve", m2, reads=["qfm", "kfp"], sreads=["augr"], writes=["m2b"],
              tag=f"m2{i}")
        def augmm(e):
            last = None
            for kt in range(4):
                last = nc.tensor.matmul(BANK(3, 2, 2 * kt),
                                        m2b[:, 128 * kt: 128 * (kt + 1)],
                                        hones[:], start=True, stop=True)
            return last
        s.add("pe", augmm, reads=["m2b", "hones", "psb3"], writes=["psb3"],
              tag=f"au{i}")
        s.add("dve", lambda e: e.tensor_copy(augw[p][:], BANK(3, 8, 0)),
              reads=["psb3"], writes=[f"augw{p}", "psb3"], tag=f"evau{i}")

        for h in range(H):
            hb = 64 * (h % 2)
            ht = 128 * (h // 2)
            ab = 4 + (h % 2)
            ob = 6 + (h // 4)
            oc = 128 * (h % 4)
            def at(e, h=h, hb=hb, ht=ht, ab=ab):
                return nc.tensor.matmul(BANK(ab, 128, 0),
                                        kt_b[hb:hb + 64, ht:ht + 128],
                                        qt_b[hb:hb + 64, ht:ht + 128],
                                        start=True, stop=True)
            s.add("pe", at, reads=["ktb" + str(p), "qtb", f"psb{ab}"], writes=[f"psb{ab}"],
                  tag=f"at{i}_{h}")
            def mask(e, h=h, ab=ab):
                return e.tensor_tensor(atm[h % 2][:], BANK(ab, 128, 0), tri[:],
                                       ALU.mult)
            s.add("dve", mask, reads=[f"psb{ab}", "tri"],
                  writes=[f"atm{h % 2}", f"psb{ab}"], tag=f"mk{i}_{h}")
            def omm(e, h=h, hb=hb, ht=ht, ob=ob, oc=oc):
                nc.tensor.matmul(BANK(ob, 128, oc), atm[h % 2][:],
                                 v_nat[p][:, 128 * h: 128 * (h + 1)],
                                 start=True, stop=False)
                return nc.tensor.matmul(BANK(ob, 128, oc),
                                        qt_b[hb:hb + 64, ht:ht + 128],
                                        S_b16[hb:hb + 64, ht:ht + 128],
                                        start=False, stop=True)
            s.add("pe", omm, reads=[f"atm{h % 2}", f"vnat{p}", "qtb", "Sb16",
                                    f"psb{ob}"],
                  writes=[f"psb{ob}"], tag=f"o{i}_{h}")
            def umm(e, h=h, hb=hb, ht=ht):
                return nc.tensor.matmul(
                    P[hb:hb + 64, 1024 + ht: 1024 + ht + 128],
                    khn[p][:, 64 * h: 64 * (h + 1)],
                    v_nat[p][:, 128 * h: 128 * (h + 1)],
                    start=True, stop=True)
            s.add("pe", umm, reads=[f"khn{p}", f"vnat{p}", "psb2"],
                  writes=["psb2"], tag=f"u{i}_{h}")

        def supd(e):
            last = None
            for kt in range(4):
                sl = slice(128 * kt, 128 * (kt + 1))
                last = e.scalar_tensor_tensor(S_sb[:, sl], S_sb[:, sl],
                                              ebt[:, kt: kt + 1],
                                              BANK(2, 128, 128 * kt),
                                              ALU.mult, ALU.add)
            return last
        s.add("dve", supd, reads=["psb2", "S", "Sb16"], sreads=["ebt" + str(p)],
              writes=["S", "psb2"], tag=f"su{i}")

        s.add("act", lambda e: (e.copy(o_acc[p][:, 0:512], BANK(6)),
                                e.copy(o_acc[p][:, 512:1024], BANK(7)))[-1],
              reads=["psb6", "psb7"], writes=[f"oacc{p}", "psb6", "psb7"],
              tag=f"evo{i}")

    def emit_aug_sgg(j, pj):
        def arg(e):
            last = None
            for h in range(H):
                last = e.tensor_scalar_mul(augarg[:, 128 * h: 128 * (h + 1)],
                                           v_nat[pj][:, 128 * h: 128 * (h + 1)],
                                           augw[pj][:, h: h + 1])
            return last
        s.add("dve", arg, reads=[f"vnat{pj}"], sreads=[f"augw{pj}"],
              writes=["augarg"], tag=f"aa{j}")
        s.add("act", lambda e: e.activation(augarg[:], augarg[:], AF.Tanh, scale=0.5),
              reads=["augarg"], writes=["augsig"], tag=f"as{j}", tset="S", cost=1100)
        def oad(e):
            e.tensor_scalar(augarg[:], augarg[:], 0.5, 0.5, ALU.mult, ALU.add)
            return e.tensor_add(opre_b[pj][:], o_acc[pj][:], augarg[:])
        s.add("dve", oad,
              reads=["augsig", f"oacc{pj}"], writes=[f"opre{pj}", "augsig", "augarg"],
              tag=f"oad{j}")
        s.add("pool", lambda e: dma(e, op_s[C * j: C * (j + 1), :], opre_b[pj][:]),
              reads=[f"opre{pj}"], n_dma=1, tag=f"stop{j}")
        s.add("act", lambda e: e.activation(sg_b[pj][:], g_raw[pj][:], AF.Silu),
              reads=[f"graw{pj}"], writes=[f"sgb{pj}"], tag=f"sg{j}", tset="S",
              cost=1100)
        s.add("pool", lambda e: dma(e, sg_s[C * j: C * (j + 1), :], sg_b[pj][:]),
              reads=[f"sgb{pj}"], n_dma=1, tag=f"stsg{j}")

    s.enabled = phase_limit >= 2
    for i in range(NT):
        p1_tile(i)
    emit_aug_sgg(NT - 1, (NT - 1) % 2)

    s.add("act", lambda e: e.activation(summ[:, 512:516], Brun[:], AF.Exp, scale=K(3)),
          reads=["Brun", "konst"], writes=["summD"], tag="dspan", tset="E", cost=300)
    s.add("dve", lambda e: e.tensor_copy(summ[:, 0:512], S_sb[:]),
          reads=["S"], writes=["summS"], tag="packS")
    s.add("pool", lambda e: dma(e, ag2_in, summ[:]),
          reads=["summD", "summS"], writes=["ag2in"], n_dma=1, tag="st_summ")
    def coll2(e):
        return e.collective_compute("AllGather", ALU.bypass,
                                    replica_groups=[list(range(NCORES))],
                                    ins=[ag2_in.opt()], outs=[ag2_out.opt()])
    s.add("pool", coll2, reads=["ag2in"], writes=["ag2"], is_coll=True, tag="coll2")

    s.enabled = True
    s.add("pool", lambda e: e.nop(), writes=["p1done"] + list(s.last_writer.keys()),
          tag="join1")
    p1_es.close()

    p3_es = ExitStack()
    wgT = SB(p3_es, "wgT", (128, 8 * I_FFN), BF16)
    wuT = SB(p3_es, "wuT", (128, 8 * I_FFN), BF16)
    wdT = SB(p3_es, "wdT", (128, 22 * 1024), BF16)
    xqf = [SB(p3_es, f"xqf{p}", (128, 8 * 128), BF16) for p in range(2)]
    hfull = SB(p3_es, "hfull", (128, I_FFN), F32)
    hq = [SB(p3_es, f"hq{p}", (128, I_FFN), BF16) for p in range(2)]
    hqf = [SB(p3_es, f"hqf{p}", (128, 22 * 128), BF16) for p in range(2)]
    hgate = SB(p3_es, "hgate", (128, 512), F32)
    ffn_b = [SB(p3_es, f"ffnb{p}", (128, E), BF16) for p in range(2)]
    f_bmax = [SB(p3_es, f"f_bmax{j}", (128, 64), F32) for j in range(2)]
    f_sc = [SB(p3_es, f"f_sc{j}", (128, 64), F32) for j in range(2)]
    f_si = [SB(p3_es, f"f_si{j}", (128, 64), F32) for j in range(2)]
    f_xa = [SB(p3_es, f"f_xa{j}", (128, 704), F32) for j in range(2)]
    f_t1 = [SB(p3_es, f"f_t1{j}", (128, 704), F32) for j in range(2)]


    s.enabled = phase_limit >= 3
    def ld_wgT(e, gsrc, dst):
        outs = []
        for kt in range(8):
            src = gsrc.rearrange("(r k) i -> k r i", k=E)[128 * kt: 128 * (kt + 1)]
            outs.append(e.dma_start(out=dst[:, I_FFN * kt: I_FFN * (kt + 1)], in_=src))
        return outs
    s.add("sp", lambda e: ld_wgT(e, g_wg, wgT), reads=["p1done", "gwg"],
          writes=["wgT"], n_dma=8, tag="ldwgT")
    s.add("sp", lambda e: ld_wgT(e, g_wu, wuT), reads=["p1done", "gwu"],
          writes=["wuT"], n_dma=8, tag="ldwuT")
    def ld_wdT(e):
        outs = []
        for it in range(22):
            src = g_wd.rearrange("(r i) h -> i r h", i=I_FFN)[128 * it: 128 * (it + 1)]
            outs.append(e.dma_start(out=wdT[:, 1024 * it: 1024 * (it + 1)], in_=src))
        return outs
    s.add("sp", ld_wdT, reads=["p1done", "gwd"], writes=["wdT"],
          n_dma=22, tag="ldwdT")
    IB = [(0, 512), (512, 512), (1024, 512), (1536, 512), (2048, 512), (2560, 256)]

    def p3_tile(i):
        p = i % 2
        def ldxq(e):
            return [e.dma_start(out=xqf[p][:, 128 * kt: 128 * (kt + 1)],
                                in_=xq_s[C * i: C * (i + 1), 128 * kt: 128 * (kt + 1)],
                                transpose=True)
                    for kt in range(8)]
        s.add("sp", ldxq, reads=[f"xqs{i}", "p1done"], writes=[f"xqf{p}"], n_dma=8,
              tag=f"lxq{i}")
        for b, (i0, w) in enumerate(IB):
            gb = (b % 2) * 2
            ub = (b % 2) * 2 + 1
            def gu(e, i0=i0, w=w, gb=gb, ub=ub):
                last = None
                for wsb, bank in ((wgT, gb), (wuT, ub)):
                    for kt in range(8):
                        last = nc.tensor.matmul(
                            BANK(bank, w, 0), xqf[p][:, 128 * kt: 128 * (kt + 1)],
                            wsb[:, I_FFN * kt + i0: I_FFN * kt + i0 + w],
                            start=(kt == 0), stop=(kt == 7))
                return last
            s.add("pe", gu, reads=[f"xqf{p}", "wgT", "wuT", f"psb{gb}", f"psb{ub}"],
                  writes=[f"psb{gb}", f"psb{ub}"], tag=f"gu{i}_{b}", cost=4400)
            def sl(e, i0=i0, w=w, gb=gb):
                return e.activation(hgate[:, 0:w], BANK(gb, w, 0), AF.Silu)
            s.add("act", sl, reads=[f"psb{gb}"], writes=["hgate", f"psb{gb}"],
                  tag=f"sl{i}_{b}", tset="S", cost=700)
            def hm(e, i0=i0, w=w, ub=ub):
                return e.tensor_tensor(hfull[:, i0:i0 + w], hgate[:, 0:w],
                                       BANK(ub, w, 0), ALU.mult)
            s.add("dve", hm, reads=["hgate", f"psb{ub}"],
                  writes=[f"hfb{b}", f"psb{ub}"], tag=f"hm{i}_{b}", cost=700,
                  force=True)
        QCH = [(0, 704, ("hfb0", "hfb1")), (704, 704, ("hfb1", "hfb2")),
               (1408, 704, ("hfb2", "hfb3", "hfb4")), (2112, 704, ("hfb4", "hfb5"))]
        for qc, (i0, w, hdeps) in enumerate(QCH):
            q_ = qc % 2
            emit_quant2(hfull[:, i0:i0 + w], hq[p][:, i0:i0 + w], 44, w,
                        f"qf{q_}", f_xa[q_][:, 0:w], f_t1[q_][:, 0:w],
                        f_bmax[q_][:, 0:44], f_sc[q_][:, 0:44],
                        f_si[q_][:, 0:44], list(hdeps), f"hqv{qc}_{p}")
        trh_deps = (("hqv0", "hqv1"), ("hqv1", "hqv2"), ("hqv2", "hqv3"))
        for bi, (bank, it0, itn) in enumerate(((6, 0, 8), (7, 8, 16), (6, 16, 22))):
            def trh_pe(e, bank=bank, it0=it0, itn=itn):
                last = None
                for j, it in enumerate(range(it0, itn)):
                    last = nc.tensor.transpose(
                        P[:, 512 * bank + 64 * j: 512 * bank + 64 * (j + 1)].bitcast(BF16),
                        hq[p][:, 128 * it: 128 * (it + 1)], identb[:])
                return last
            s.add("pe", trh_pe,
                  reads=[f"{d}_{p}" for d in trh_deps[bi]] + ["identb", f"psb{bank}"],
                  writes=[f"psb{bank}"], tag=f"trh{i}_{bi}", cost=2300)
            def evh(e, bank=bank, it0=it0, itn=itn):
                return e.copy(hqf[p][:, 128 * it0: 128 * itn],
                              P[:, 512 * bank: 512 * bank + 64 * (itn - it0)].bitcast(BF16))
            s.add("act", evh, reads=[f"psb{bank}"],
                  writes=[f"hqf{p}", f"psb{bank}"], tag=f"evh{i}_{bi}", cost=900)
        def down(e):
            last = None
            for half, bank in ((0, 4), (1, 5)):
                for it in range(22):
                    last = nc.tensor.matmul(
                        BANK(bank), hqf[p][:, 128 * it: 128 * (it + 1)],
                        wdT[:, 1024 * it + 512 * half: 1024 * it + 512 * (half + 1)],
                        start=(it == 0), stop=(it == 21))
            return last
        s.add("pe", down, reads=[f"hqf{p}", "wdT", "psb4", "psb5"],
              writes=["psb4", "psb5"], tag=f"dn{i}", cost=12000)
        s.add("dve", lambda e: (e.tensor_copy(ffn_b[p][:, 0:512], BANK(4)),
                                e.tensor_copy(ffn_b[p][:, 512:1024], BANK(5)))[-1],
              reads=["psb4", "psb5"], writes=[f"ffnb{p}", "psb4", "psb5"],
              tag=f"evf{i}")
        s.add("pool", lambda e: dma(e, ffn_s[C * i: C * (i + 1), :], ffn_b[p][:]),
              reads=[f"ffnb{p}"], n_dma=1, tag=f"stf{i}")

    for i in range(NT):
        p3_tile(i)

    s.enabled = True
    s.add("pool", lambda e: e.nop(), writes=["p3done"] + list(s.last_writer.keys()),
          tag="join3")
    p3_es.close()

    s.enabled = phase_limit >= 4
    p2_es = ExitStack()
    wout_sb = SB(p2_es, "wout_sb", (128, 8 * 1024), BF16)
    summ_all = SB(p2_es, "summ_all", (128, NCORES * 516), F32)
    Sin = SB(p2_es, "Sin", (128, 512), F32)
    Sin_b = SB(p2_es, "Sinb", (128, 512), BF16)
    Dt = SB(p2_es, "Dt", (128, 4), F32)
    x2 = [SB(p2_es, f"x2_{p}", (128, E), F32) for p in range(2)]
    opre2 = [SB(p2_es, f"opre2_{p}", (128, E), BF16) for p in range(2)]
    qg2 = [SB(p2_es, f"qg2_{p}", (128, 512), BF16) for p in range(2)]
    sg2 = [SB(p2_es, f"sg2_{p}", (128, E), BF16) for p in range(2)]
    ffn2 = [SB(p2_es, f"ffn2_{p}", (128, E), BF16) for p in range(2)]
    ofull = SB(p2_es, "ofull", (128, E), F32)
    onorm = SB(p2_es, "onorm", (128, E), F32)
    sgo = SB(p2_es, "sgo", (128, E), BF16)
    sgof = [SB(p2_es, f"sgof{p}", (128, 8 * 128), BF16) for p in range(2)]
    out_sb = [SB(p2_es, f"out{p}", (128, E), F32) for p in range(2)]
    bn8 = SB(p2_es, "bn8", (128, 48), F32)
    oag = SB(p2_es, "oag", (128, 16), F32)
    orst = SB(p2_es, "orst", (128, 8), F32)
    otmp = SB(p2_es, "otmp", (128, 8), F32)

    s.add("pool", lambda e: cast_w(e, wout_sb[:], wout_d, 1024),
          reads=["p3done"], writes=["woutb"], n_dma=1, tag="ldwout")
    s.add("act", lambda e: dma(e, summ_all.rearrange("p (r f) -> p r f", f=516),
                               ag2_out.rearrange("(r p) f -> p r f", p=128)),
          reads=["ag2", "p3done"], writes=["sall"], n_dma=1, tag="ldsumm")

    s.add("dve", lambda e: e.memset(Sin[:], 0.0), reads=["p3done"],
          writes=["Sin"], tag="sin0")
    for r in range(NCORES - 1):
        base = 516 * r
        def dt_f(e, base=base, r=r):
            e.tensor_scalar_sub(Dt[:], summ_all[:, base + 512: base + 516], 1.0)
            return e.tensor_scalar(Dt[:], Dt[:], msk[:, r: r + 1], 1.0,
                                   ALU.mult, ALU.add)
        s.add("dve", dt_f, reads=["sall"], sreads=["msk"], writes=["Dt"],
              tag=f"dt{r}", force=True)
        def comb(e, base=base, r=r):
            last = None
            for kt in range(4):
                sl = slice(128 * kt, 128 * (kt + 1))
                e.tensor_scalar_mul(onorm[:, sl], summ_all[:, base + 128 * kt:
                                                           base + 128 * (kt + 1)],
                                    msk[:, r: r + 1])
                last = e.scalar_tensor_tensor(Sin[:, sl], Sin[:, sl],
                                              Dt[:, kt: kt + 1], onorm[:, sl],
                                              ALU.mult, ALU.add)
            return last
        s.add("dve", comb, reads=["sall", "Sin"], sreads=["Dt", "msk"],
              writes=["Sin"], tag=f"cb{r}", force=True)
    s.add("dve", lambda e: e.tensor_copy(Sin_b[:], Sin[:]), reads=["Sin"],
          writes=["Sinb"], tag="sinb")

    def p2_tile(i):
        p = i % 2
        def gate(n):
            s.enabled = (phase_limit >= 4) and (p2_parts >= n)
        gate(1)
        s.add("sp", lambda e: dma(e, x2[p][:], x_d[C * i: C * (i + 1), :]),
              reads=["p3done"], writes=[f"x2_{p}"], n_dma=1, tag=f"lx2{i}")
        def lds(e):
            return [dma(e, opre2[p][:], op_s[C * i: C * (i + 1), :]),
                    dma(e, qg2[p][:], qg_s[C * i: C * (i + 1), :]),
                    dma(e, sg2[p][:], sg_s[C * i: C * (i + 1), :]),
                    dma(e, ffn2[p][:], ffn_s[C * i: C * (i + 1), :])]
        s.add("act", lds, reads=["p3done"],
              writes=[f"opre2_{p}", f"qg2_{p}", f"sg2_{p}", f"ffn2_{p}"],
              n_dma=4, tag=f"lds{i}")
        gate(2)
        corr_pos = lambda h: (h % 2, 128 * (h // 2))
        def corr(e):
            last = None
            for h in (0, 2, 4, 6, 1, 3, 5, 7):
                hb = 64 * (h % 2)
                ht = 128 * (h // 2)
                ob, oc = corr_pos(h)
                last = nc.tensor.matmul(BANK(ob, 128, oc),
                                        qg2[p][hb:hb + 64, ht:ht + 128],
                                        Sin_b[hb:hb + 64, ht:ht + 128],
                                        start=True, stop=True)
            return last
        s.add("pe", corr, reads=[f"qg2_{p}", "Sinb", "psb0", "psb1"],
              writes=["psb0", "psb1"], tag=f"cr{i}", cost=1400)
        gate(2.5)
        def of_f(e):
            last = None
            for h in range(H):
                ob, oc = corr_pos(h)
                last = e.tensor_add(ofull[:, 128 * h: 128 * (h + 1)],
                                    opre2[p][:, 128 * h: 128 * (h + 1)],
                                    BANK(ob, 128, oc))
            return last
        s.add("dve", of_f,
              reads=["psb0", "psb1", f"opre2_{p}"],
              writes=["ofull", "psb0", "psb1"], tag=f"of{i}", force=True, cost=1400)
        gate(3)
        def lno(e):
            last = None
            for h in range(H):
                last = e.bn_stats(bn8[:, 6 * h: 6 * (h + 1)],
                                  ofull[:, 128 * h: 128 * (h + 1)])
            return last
        s.add("dve", lno, reads=["ofull"], writes=["bn8"], tag=f"lno{i}", force=True)
        def lna(e):
            last = None
            for h in range(H):
                last = e.bn_aggr(oag[:, 2 * h: 2 * (h + 1)], bn8[:, 6 * h: 6 * (h + 1)])
            return last
        s.add("dve", lna, reads=["bn8"], writes=["oag"], tag=f"lna{i}", force=True)
        gate(4)
        def ors(e):
            e.activation(otmp[:], oag.rearrange("p (h two) -> p h two", two=2)[:, :, 1],
                         AF.Ln, bias=K(0))
            return e.activation(orst[:], otmp[:], AF.Exp, scale=K(2))
        s.add("act", ors, reads=["oag", "konst"], writes=["orst"], tag=f"ors{i}",
              force=True, tset="E", cost=400)
        gate(5)
        def nrm(e):
            last = None
            for h in range(H):
                last = e.tensor_scalar(
                    onorm[:, 128 * h: 128 * (h + 1)],
                    ofull[:, 128 * h: 128 * (h + 1)],
                    oag[:, 2 * h: 2 * h + 1], orst[:, h: h + 1],
                    ALU.subtract, ALU.mult)
            return last
        s.add("dve", nrm, reads=["ofull"], sreads=["oag", "orst"],
              writes=["onorm2"], tag=f"nr{i}")
        s.add("dve", lambda e: e.tensor_mul(sgo[:], onorm[:], sg2[p][:]),
              reads=["onorm2", f"sg2_{p}"], writes=["sgo"], tag=f"sgo{i}")
        gate(6)
        def trs(e):
            last = None
            for kt in range(8):
                last = nc.tensor.transpose(
                    P[:, 64 * kt: 64 * (kt + 1)].bitcast(BF16),
                    sgo[:, 128 * kt: 128 * (kt + 1)], identb[:])
            return last
        s.add("pe", trs, reads=["sgo", "identb", "psb0"], writes=["psb0"],
              tag=f"trs{i}", cost=2300)
        def evs(e):
            return e.copy(sgof[p][:], P[:, 0:512].bitcast(BF16))
        s.add("act", evs, reads=["psb0"], writes=[f"sgof{p}", "psb0"], tag=f"evs{i}")
        gate(7)
        def wo(e):
            last = None
            for half, bank in ((0, 2), (1, 3)):
                for kt in range(8):
                    last = nc.tensor.matmul(
                        BANK(bank), sgof[p][:, 128 * kt: 128 * (kt + 1)],
                        wout_sb[:, kt * 1024 + 512 * half: kt * 1024 + 512 * (half + 1)],
                        start=(kt == 0), stop=(kt == 7))
            return last
        s.add("pe", wo, reads=[f"sgof{p}", "woutb", "psb2", "psb3"],
              writes=["psb2", "psb3"], tag=f"wo{i}", cost=4400)
        gate(8)
        def fin(e):
            e.tensor_add(out_sb[p][:, 0:512], BANK(2), x2[p][:, 0:512])
            e.tensor_add(out_sb[p][:, 512:1024], BANK(3), x2[p][:, 512:1024])
            e.tensor_add(out_sb[p][:, 0:512], out_sb[p][:, 0:512], ffn2[p][:, 0:512])
            return e.tensor_add(out_sb[p][:, 512:1024], out_sb[p][:, 512:1024],
                                ffn2[p][:, 512:1024])
        s.add("dve", fin, reads=["psb2", "psb3", f"x2_{p}", f"ffn2_{p}"],
              writes=[f"out{p}", "psb2", "psb3"], tag=f"fin{i}")
        gate(9)
        s.add("pool", lambda e: dma(e, y_d[C * i: C * (i + 1), :], out_sb[p][:]),
              reads=[f"out{p}"], writes=[f"ydma{i}"], n_dma=1, tag=f"sty{i}")
        s.enabled = phase_limit >= 4

    for i in range(NT):
        p2_tile(i)

    if taps:
        s.enabled = True
        tap_list = [("t_xq", xq_s, [SPAN, E], BF16, "none"),
                    ("t_sg", sg_s, [SPAN, E], BF16, "none"),
                    ("t_op", op_s, [SPAN, E], BF16, "none"),
                    ("t_qg", qg_s, [SPAN, 512], BF16, "none"),
                    ("t_ffn", ffn_s, [SPAN, E], BF16, "none"),
                    ("t_ag2i", ag2_in, [128, 516], F32, "none"),
                    ("t_gwg", g_wg, [NCORES * E, 352], BF16, "none"),
                    ("t_gwd", g_wd, [NCORES * I_FFN, 128], BF16, "none"),
                    ("t_ag2o", ag2_out, [NCORES * 128, 516], F32, "none")]
        for nm, srcd, shp, dt, _ in tap_list:
            td = nc.dram_tensor(nm, shp, dt, kind="ExternalOutput").ap()
            def cp(e, td=td, srcd=srcd):
                return dma(e, td, srcd)
            s.add("pool", cp, reads=["p3done", "ag2"], writes=[nm], n_dma=1, tag=nm)
        t_sin = nc.dram_tensor("t_sin", [128, 512], F32, kind="ExternalOutput").ap()
        s.add("pool", lambda e: dma(e, t_sin, Sin[:]), reads=["Sin"],
              writes=["t_sin"], n_dma=1, tag="t_sin")

    s.enabled = True
    s.add("pool", lambda e: e.nop(),
          reads=([f"ydma{i}" for i in range(NT)] if (phase_limit >= 4 and p2_parts >= 9) else []),
          writes=list(s.last_writer.keys()), tag="fence_out")

    s.finalize()
    with nc.Block() as block:
        s.emit(nc, block)
    p2_es.close()
    main_es.close()
    return nc


_NC_CACHE = None


def kernel(**inputs):
    import ml_dtypes
    BF = ml_dtypes.bfloat16
    F = np.float32
    g = lambda k: np.ascontiguousarray(np.asarray(inputs[k], dtype=F))
    x = g("L0_attn_x").reshape(16384, E)
    conv_w = g("L0_attn_conv_w")
    wq = g("L0_attn_wq"); wkg = g("L0_attn_wkg"); wv = g("L0_attn_wv")
    wg = g("L0_attn_wg"); bg = g("L0_attn_bg"); wout = g("L0_attn_wout")
    aug = g("L0_attn_aug_balance")
    w_gate = g("L0_ffn_w_gate"); w_up = g("L0_ffn_w_up"); w_down = g("L0_ffn_w_down")

    tri = np.triu(np.ones((128, 128), dtype=F))
    hones = np.zeros((128, 2), dtype=BF)
    hones[0:64, 0] = 1; hones[64:128, 1] = 1
    identb = np.eye(128, dtype=BF)
    cw_r = np.ascontiguousarray(conv_w.reshape(8, 128, 4).transpose(1, 0, 2).reshape(128, 32))
    aug_r = np.ascontiguousarray(aug.reshape(4, 128).T)
    bg1 = bg.reshape(1, E)

    global _NC_CACHE
    if _NC_CACHE is None:
        _NC_CACHE = build_program()
    nc = _NC_CACHE

    in_maps = []
    for j in range(NCORES):
        t0 = j * SPAN
        xh = np.zeros((16, E), dtype=F)
        if j > 0:
            xh[13:16] = x[t0 - 3: t0]
        mskj = np.zeros((128, NCORES), dtype=F)
        mskj[:, :j] = 1.0
        in_maps.append(dict(
            x=np.ascontiguousarray(x[t0: t0 + SPAN]),
            xh=xh, msk=mskj, tri=tri, hones=hones, identb=identb,
            cw=cw_r, augr=aug_r, bg=bg1,
            wq=wq, wkg=wkg, wv=wv, wg=wg, wout=wout,
            wgsh=np.ascontiguousarray(w_gate[352 * j: 352 * (j + 1)]),
            wush=np.ascontiguousarray(w_up[352 * j: 352 * (j + 1)]),
            wdsh=np.ascontiguousarray(w_down[128 * j: 128 * (j + 1)]),
        ))
    res = run_bass_kernel_spmd(nc, in_maps, core_ids=list(range(NCORES)))
    out = np.concatenate([np.asarray(r["y"]) for r in res.results], axis=0)
    return out.reshape(1, 16384, E).astype(np.float32)
